# version 05
# baseline (speedup 1.0000x reference)
import math
from contextlib import ExitStack

import numpy as np
import ml_dtypes

import concourse.bass as bass
import concourse.mybir as mybir
from concourse.bass_utils import run_bass_kernel_spmd

F32 = mybir.dt.float32
BF16 = mybir.dt.bfloat16
AF = mybir.ActivationFunctionType
ALU = mybir.AluOpType

S, D, NH, HD, C = 4096, 1024, 8, 128, 128
NCH = S // C
EPS = 1e-5


def host_constants():
    gam = 1.0 - np.exp(np.linspace(math.log(1.0 / 32.0), math.log(1.0 / 512.0), NH))
    ii = np.arange(C)
    diff = ii[None, :] - ii[:, None]
    dloc = np.where(diff[:, :, None] >= 0,
                    gam[None, None, :] ** np.maximum(diff, 0)[:, :, None], 0.0)
    DLOC = np.ascontiguousarray(dloc.transpose(0, 2, 1)).astype(np.float32)
    GQ = np.ascontiguousarray(np.broadcast_to((gam[:, None] ** (ii[None, :] + 1))[None],
                                              (128, NH, C))).astype(np.float32)
    GK = np.ascontiguousarray(np.broadcast_to((gam[:, None] ** (C - 1 - ii[None, :]))[None],
                                              (128, NH, C))).astype(np.float32)
    GC = [float(np.float32(gam[h] ** C)) for h in range(NH)]

    half = HD // 2
    scale = (np.arange(0, HD, 2, dtype=np.float64) + 0.4 * HD) / (1.4 * HD)
    pos = np.arange(S, dtype=np.float64)
    sc = scale[None, :] ** (pos[:, None] / 512.0)
    inv_freq = 1.0 / 10000.0 ** (np.arange(half, dtype=np.float64) / half)
    sinus = pos[:, None] * inv_freq[None, :]
    sin, cos = np.sin(sinus), np.cos(sinus)

    def rep2(a):
        return np.repeat(a, 2, axis=-1)

    cosQ, sinQ = rep2(cos * sc), rep2(sin * sc)
    cosK, sinK = rep2(cos / sc), rep2(sin / sc)
    TAB = np.empty((NCH, 128, 2, 2, C), np.float32)
    for ci in range(NCH):
        sl = slice(ci * C, (ci + 1) * C)
        TAB[ci, :, 0, 0, :] = cosQ[sl].T
        TAB[ci, :, 1, 0, :] = sinQ[sl].T
        TAB[ci, :, 0, 1, :] = cosK[sl].T
        TAB[ci, :, 1, 1, :] = sinK[sl].T

    P = np.zeros((128, 128), np.float64)
    for m in range(64):
        P[2 * m, 2 * m + 1] = -1.0
        P[2 * m + 1, 2 * m] = 1.0
    PT = np.ascontiguousarray(P.T).astype(ml_dtypes.bfloat16)
    IDN = np.eye(128, dtype=ml_dtypes.bfloat16)
    IDNF = np.eye(128, dtype=np.float32)
    ONES = np.ones((1, 128), ml_dtypes.bfloat16)
    return dict(TAB=TAB, DLOC=DLOC, GQ=GQ, GK=GK, PT=PT, IDN=IDN, IDNF=IDNF, ONES=ONES), GC


class Sched:

    def __init__(self, nc, sems):
        self.nc = nc
        self.sems = sems
        self.cnt = {k: 0 for k in sems}
        self.ops = []
        self.sb_w = {}
        self.sb_r = {}
        self.ps_w_bytes = []
        self.ps_w_bank = {}
        self.ps_r_bank = {}
        self.waited = {}

    @staticmethod
    def banks(lo, hi):
        return range(lo // 512, (hi + 511) // 512)

    def op(self, eng, fn, reads=(), writes=(), pr=(), pw=(), dma=False, tag=""):
        sem = ("d_" if dma else "m_") + eng
        deps = {}

        def need(tok, force=False):
            if tok is None:
                return
            s, v = tok
            if v > deps.get(s, 0):
                deps[s] = v

        for b in reads:
            need(self.sb_w.get(b))
        for b in writes:
            need(self.sb_w.get(b))
            for s2, v2 in self.sb_r.get(b, {}).items():
                need((s2, v2))
        for (lo, hi) in pr:
            for bk in self.banks(lo, hi):
                for s2, v2 in self.ps_w_bank.get(bk, {}).items():
                    need((s2, v2), force=True)
                for s2, v2 in self.ps_r_bank.get(bk, {}).items():
                    if s2 != sem:
                        need((s2, v2), force=True)
        for (lo, hi) in pw:
            assert eng == "tensor", f"only PE writes PSUM (op {tag})"
            for (l2, h2, s2, v2) in self.ps_w_bytes:
                if l2 < hi and lo < h2:
                    need((s2, v2), force=True)
            for bk in self.banks(lo, hi):
                for s2, v2 in self.ps_r_bank.get(bk, {}).items():
                    need((s2, v2), force=True)

        if dma and self.cnt[sem] > 0:
            deps[sem] = max(deps.get(sem, 0), self.cnt[sem])

        waits = []
        for s, v in sorted(deps.items()):
            if self.waited.get((eng, s), 0) < v:
                waits.append((s, v))
                self.waited[(eng, s)] = v

        self.cnt[sem] += 16 if dma else 1
        tok = (sem, self.cnt[sem])
        for b in reads:
            self.sb_r.setdefault(b, {})[sem] = tok[1]
        for b in writes:
            self.sb_w[b] = tok
            self.sb_r[b] = {}
        for (lo, hi) in pr:
            for bk in self.banks(lo, hi):
                self.ps_r_bank.setdefault(bk, {})[sem] = tok[1]
        for (lo, hi) in pw:
            self.ps_w_bytes = [e for e in self.ps_w_bytes if not (lo <= e[0] and e[1] <= hi)]
            self.ps_w_bytes.append((lo, hi, sem, tok[1]))
            for bk in self.banks(lo, hi):
                self.ps_w_bank.setdefault(bk, {})[sem] = tok[1]
        self.ops.append((eng, waits, fn, sem, 16 if dma else 1, tag))
        return tok

    def run(self, block):
        for eng in ("sync", "scalar", "tensor", "vector", "gpsimd"):
            ops = [o for o in self.ops if o[0] == eng]

            def make(ops=ops, eng=eng):
                def go(e):
                    for (_, waits, fn, semname, inc, tag) in ops:
                        for (s, v) in waits:
                            e.wait_ge(self.sems[s], v)
                        inst = fn()
                        inst.then_inc(self.sems[semname], inc)
                    if eng == "scalar":
                        e.wait_ge(self.sems["d_scalar"], self.cnt["d_scalar"])
                return go

            getattr(block, eng)(make())


def build_nc(gc_list, dbg=None):
    dbg = dbg or set()
    nc = bass.Bass("TRN2", target_bir_lowering=False)

    def din(name, shape, dtype=F32):
        return nc.dram_tensor(name, shape, dtype, kind="ExternalInput").ap()

    x_h = din("x", [S, D])
    wq_h = din("wq", [NH, HD, HD]); wk_h = din("wk", [NH, HD, HD]); wv_h = din("wv", [NH, HD, HD])
    wg_h = din("wg", [D, D]); wo_h = din("wo", [D, D])
    gnw_h = din("gnw", [D]); gnb_h = din("gnb", [D])
    u1_h = din("u1", [D, 256]); v1_h = din("v1", [256, 4096]); b1_h = din("b1", [4096])
    u2_h = din("u2", [4096, 256]); v2_h = din("v2", [256, D]); b2_h = din("b2", [D])
    tab_h = din("TAB", [NCH, 128, 2, 2, C])
    dloc_h = din("DLOC", [128, NH, C]); gq_h = din("GQ", [128, NH, C]); gk_h = din("GK", [128, NH, C])
    pt_h = din("PT", [128, 128], BF16); idn_h = din("IDN", [128, 128], BF16)
    ones_h = din("ONES", [1, 128], BF16)
    y_h = nc.dram_tensor("y", [S, D], F32, kind="ExternalOutput").ap()

    dbg_names = []

    with ExitStack() as st:
        E = st.enter_context
        wq = E(nc.sbuf_tensor("wq_s", [128, NH, HD], BF16))
        wk = E(nc.sbuf_tensor("wk_s", [128, NH, HD], BF16))
        wv = E(nc.sbuf_tensor("wv_s", [128, NH, HD], BF16))
        wqr = E(nc.sbuf_tensor("wqr_s", [128, NH, HD], BF16))
        wkr = E(nc.sbuf_tensor("wkr_s", [128, NH, HD], BF16))
        wg = E(nc.sbuf_tensor("wg_s", [128, 8, D], BF16))
        wo = E(nc.sbuf_tensor("wo_s", [128, 8, D], BF16))
        u1 = E(nc.sbuf_tensor("u1_s", [128, 8, 256], BF16))
        v1 = E(nc.sbuf_tensor("v1_s", [128, 2, 4096], BF16))
        u2 = E(nc.sbuf_tensor("u2_s", [128, 32, 256], BF16))
        v2 = E(nc.sbuf_tensor("v2_s", [128, 2, D], BF16))
        b2r = E(nc.sbuf_tensor("b2r_s", [1, D], BF16))
        onesb = E(nc.sbuf_tensor("ones_s", [1, 128], BF16))
        gnwbc = E(nc.sbuf_tensor("gnwbc", [128, D], F32))
        gnbbc = E(nc.sbuf_tensor("gnbbc", [128, D], F32))
        b1r = E(nc.sbuf_tensor("b1r_s", [1, 4096], BF16))
        dloc = E(nc.sbuf_tensor("dloc", [128, NH, C], F32))
        gq = E(nc.sbuf_tensor("gq", [128, NH, C], F32))
        gk = E(nc.sbuf_tensor("gk", [128, NH, C], F32))
        idn = E(nc.sbuf_tensor("idn", [128, 128], BF16))
        pt = E(nc.sbuf_tensor("pt", [128, 128], BF16))
        wtt = E(nc.sbuf_tensor("wtt", [128, 128], BF16))

        x0 = [E(nc.sbuf_tensor(f"x0_{p}", [128, D], F32)) for p in range(2)]
        tabs = [E(nc.sbuf_tensor(f"tab_{p}", [128, 2, 2, C], F32)) for p in range(2)]
        xntm = E(nc.sbuf_tensor("xntm", [128, D], BF16))
        xnfm = [E(nc.sbuf_tensor(f"xnfm_{pp}", [128, NH, C], BF16)) for pp in range(2)]
        qkbf = E(nc.sbuf_tensor("qkbf", [128, NH, 2, C], BF16))
        qtb = [E(nc.sbuf_tensor(f"qtb_{pp}", [128, NH, C], BF16)) for pp in range(2)]
        ktb = E(nc.sbuf_tensor("ktb", [128, NH, C], BF16))
        kttm = [E(nc.sbuf_tensor(f"kttm_{pp}", [128, NH, C], BF16)) for pp in range(2)]
        vbf = [E(nc.sbuf_tensor(f"vbf_{pp}", [128, NH, C], BF16)) for pp in range(2)]
        stb = [E(nc.sbuf_tensor(f"stb_{pp}", [128, NH, C], BF16)) for pp in range(2)]
        Rst = E(nc.sbuf_tensor("Rst", [128, NH, HD], F32))
        Rbf = E(nc.sbuf_tensor("Rbf", [128, NH, HD], BF16))
        scr = E(nc.sbuf_tensor("scr", [128, 2, 3072], F32))
        hbf = E(nc.sbuf_tensor("hbf", [128, NH, C], BF16))
        hfm = E(nc.sbuf_tensor("hfm", [128, NH, C], BF16))
        x1 = [E(nc.sbuf_tensor(f"x1_{p}", [128, D], F32)) for p in range(2)]
        xn2tm = xntm
        xn2fm = E(nc.sbuf_tensor("xn2fm", [128, NH, C], BF16))
        pbf = E(nc.sbuf_tensor("pbf", [128, 2, 128], BF16))
        Hbf = E(nc.sbuf_tensor("Hbf", [128, 16, 128], BF16))
        qftmp = E(nc.sbuf_tensor("qftmp", [128, 256], F32))
        qfbf = E(nc.sbuf_tensor("qfbf", [128, 2, 128], BF16))
        outb = E(nc.sbuf_tensor("out_s", [128, D], F32))
        st6 = [E(nc.sbuf_tensor(f"st6_{p}", [128, 2, 6], F32)) for p in range(2)]
        st2 = [E(nc.sbuf_tensor(f"st2_{p}", [128, 2], F32)) for p in range(2)]
        st6g = E(nc.sbuf_tensor("st6g", [128, NH, 6], F32))
        gtmp = E(nc.sbuf_tensor("gtmp", [128, 5, NH], F32))
        rstd = E(nc.sbuf_tensor("rstd", [128, 8], F32))
        negmr = E(nc.sbuf_tensor("negmr", [128, 4], F32))
        rstdg = E(nc.sbuf_tensor("rstdg", [128, NH], F32))
        cst = E(nc.sbuf_tensor("cst", [128, 4], F32))
        lnsum = E(nc.sbuf_tensor("lnsum", [128, 4], F32))
        lnsq = E(nc.sbuf_tensor("lnsq", [128, 4], F32))
        lntmp = E(nc.sbuf_tensor("lntmp", [128, 4], F32))
        negmrg = E(nc.sbuf_tensor("negmrg", [128, NH], F32))

        PS = E(nc.psum_tensor("PS", [128, 4096], F32))
        PSB = PS[:].bitcast(BF16)

        sems = {}
        for name in ("m_tensor", "m_vector", "m_scalar", "m_gpsimd", "d_sync", "d_scalar", "d_gpsimd"):
            sems[name] = E(nc.semaphore(name))
        sch = Sched(nc, sems)

        def tap(stage, I, ap_fn, bufname):
            if (stage, I) in dbg:
                ap = ap_fn()
                nm = f"dbg_{stage}_{I}"
                t = nc.dram_tensor(nm, list(ap.shape), ap.dtype, kind="ExternalOutput").ap()
                dbg_names.append(nm)
                sch.op("sync", lambda ap=ap, t=t: nc.sync.dma_start(out=t, in_=ap),
                       reads=[bufname], dma=True, tag=f"tap_{stage}_{I}")

        for fn, wr, tg in (
            (lambda: nc.gpsimd.dma_start(out=wq[:], in_=wq_h.rearrange("h d e -> d h e")), ["wq"], "ld_wq"),
            (lambda: nc.gpsimd.dma_start(out=wk[:], in_=wk_h.rearrange("h d e -> d h e")), ["wk"], "ld_wk"),
            (lambda: nc.gpsimd.dma_start(out=wv[:], in_=wv_h.rearrange("h d e -> d h e")), ["wv"], "ld_wv"),
            (lambda: nc.gpsimd.dma_start(out=wg[:], in_=wg_h.rearrange("(b p) o -> p b o", p=128)), ["wg"], "ld_wg"),
            (lambda: nc.gpsimd.dma_start(out=wo[:], in_=wo_h.rearrange("(b p) o -> p b o", p=128)), ["wo"], "ld_wo"),
            (lambda: nc.gpsimd.dma_start(out=u1[:], in_=u1_h.rearrange("(b p) r -> p b r", p=128)), ["u1"], "ld_u1"),
            (lambda: nc.gpsimd.dma_start(out=v1[:], in_=v1_h.rearrange("(b p) f -> p b f", p=128)), ["v1"], "ld_v1"),
            (lambda: nc.gpsimd.dma_start(out=u2[:], in_=u2_h.rearrange("(b p) r -> p b r", p=128)), ["u2"], "ld_u2"),
            (lambda: nc.gpsimd.dma_start(out=v2[:], in_=v2_h.rearrange("(b p) o -> p b o", p=128)), ["v2"], "ld_v2"),
            (lambda: nc.gpsimd.dma_start(out=b1r[:], in_=b1_h.unsqueeze(0)), ["b1r"], "ld_b1"),
            (lambda: nc.gpsimd.dma_start(out=b2r[:], in_=b2_h.unsqueeze(0)), ["b2r"], "ld_b2"),
            (lambda: nc.gpsimd.dma_start(out=onesb[:], in_=ones_h), ["ones"], "ld_ones"),
        ):
            sch.op("gpsimd", fn, writes=wr, dma=True, tag=tg)

        for fn, wr, tg in (
            (lambda: nc.sync.dma_start(out=gnwbc[:], in_=gnw_h.unsqueeze(0).broadcast_to([128, D])), ["gnwbc"], "ld_gnw"),
            (lambda: nc.sync.dma_start(out=gnbbc[:], in_=gnb_h.unsqueeze(0).broadcast_to([128, D])), ["gnbbc"], "ld_gnb"),
            (lambda: nc.sync.dma_start(out=dloc[:], in_=dloc_h), ["dloc"], "ld_dloc"),
            (lambda: nc.sync.dma_start(out=gq[:], in_=gq_h), ["gq"], "ld_gq"),
            (lambda: nc.sync.dma_start(out=gk[:], in_=gk_h), ["gk"], "ld_gk"),
            (lambda: nc.sync.dma_start(out=idn[:], in_=idn_h), ["idn"], "ld_idn"),
            (lambda: nc.sync.dma_start(out=pt[:], in_=pt_h), ["pt"], "ld_pt"),
        ):
            sch.op("sync", fn, writes=wr, dma=True, tag=tg)

        for (src, dst, sname, dname) in ((wq, wqr, "wq", "wqr"), (wk, wkr, "wk", "wkr")):
            for h in range(NH):
                sch.op("tensor", lambda src=src, h=h: nc.tensor.transpose(PSB[:, 0:128], src[:, h, :], idn[:]),
                       reads=[sname, "idn"], pw=[(0, 64)], tag=f"T_{dname}{h}")
                sch.op("vector", lambda: nc.vector.tensor_copy(out=wtt[:], in_=PSB[:, 0:128]),
                       writes=["wtt"], pr=[(0, 64)], tag=f"ev_{dname}{h}")
                sch.op("tensor", lambda: nc.tensor.matmul(PS[:, 512:640], wtt[:], pt[:], start=True, stop=True),
                       reads=["wtt", "pt"], pw=[(512, 640)], tag=f"mm_{dname}{h}")
                sch.op("vector", lambda dst=dst, h=h: nc.vector.tensor_copy(out=dst[:, h, :], in_=PS[:, 512:640]),
                       writes=[dname], pr=[(512, 640)], tag=f"ev2_{dname}{h}")

        tap("wqr", 0, lambda: wqr[:], "wqr")
        tap("wkr", 0, lambda: wkr[:], "wkr")
        sch.op("vector", lambda: nc.vector.memset(Rst[:], 0.0), writes=["Rst"], tag="memset_R")
        sch.op("vector", lambda: (nc.vector.memset(cst[:, 0:1], EPS),
                                  nc.vector.memset(cst[:, 1:2], -0.5),
                                  nc.vector.memset(cst[:, 2:3], 0.25),
                                  nc.vector.memset(cst[:, 3:4], 1.0 / D))[-1],
               writes=["cst"], tag="memset_cst")

        def emit_loads(I):
            p = I % 2
            sch.op("sync", lambda I=I, p=p: nc.sync.dma_start(out=x0[p][:], in_=x_h[I * C:(I + 1) * C, :]),
                   writes=[f"x0{p}"], dma=True, tag=f"ld_x0_{I}")
            sch.op("sync", lambda I=I, p=p: nc.sync.dma_start(out=tabs[p][:], in_=tab_h[I]),
                   writes=[f"tab{p}"], dma=True, tag=f"ld_tab_{I}")

        emit_loads(0)

        def sacol(h):
            return 1024 + (h % 2) * 512 + (h // 2) * 128

        def ycol(h):
            return 2048 + (h % 2) * 512 + (h // 2) * 128

        def emit_ln_stats(I, p, srcfn, src_reads, src_pr, idx, tagp):
            sch.op("scalar", lambda: nc.scalar.activation(out=scr[:, p, 1024:2048], in_=srcfn(),
                                                          func=AF.Square, accum_out=lnsq[:, idx:idx + 1]),
                   reads=src_reads, writes=[f"sB{p}", f"lnsq{idx}"], pr=src_pr, tag=f"sq{tagp}_{I}")
            sch.op("vector", lambda: nc.vector.tensor_scalar(out=scr[:, p, 0:1024], in0=srcfn(),
                                                             scalar1=1.0, scalar2=None, op0=ALU.mult,
                                                             op1=ALU.add, accum_out=lnsum[:, idx:idx + 1]),
                   reads=src_reads, writes=[f"sA{p}", f"lnsum{idx}"], pr=src_pr, tag=f"sm{tagp}_{I}")
            sch.op("vector", lambda: nc.vector.tensor_tensor(out=lntmp[:, idx:idx + 1], in0=lnsum[:, idx:idx + 1],
                                                             in1=lnsum[:, idx:idx + 1], op=ALU.mult),
                   reads=[f"lnsum{idx}"], writes=[f"lntmp{idx}"], tag=f"varA{tagp}_{I}")
            sch.op("vector", lambda: nc.vector.scalar_tensor_tensor(out=lntmp[:, idx:idx + 1],
                                                                    in0=lntmp[:, idx:idx + 1],
                                                                    scalar=-1.0 / D, in1=lnsq[:, idx:idx + 1],
                                                                    op0=ALU.mult, op1=ALU.add),
                   reads=[f"lntmp{idx}", f"lnsq{idx}"], writes=[f"lntmp{idx}"], tag=f"varB{tagp}_{I}")
            sch.op("scalar", lambda: nc.scalar.activation(out=rstd[:, 4 + idx:5 + idx], in_=lntmp[:, idx:idx + 1],
                                                          func=AF.Ln, bias=cst[:, 0:1], scale=cst[:, 3:4]),
                   reads=[f"lntmp{idx}", "cst"], writes=[f"rstdL{idx}"], tag=f"rstdA{tagp}_{I}")
            sch.op("scalar", lambda: nc.scalar.activation(out=rstd[:, idx:idx + 1], in_=rstd[:, 4 + idx:5 + idx],
                                                          func=AF.Exp, scale=cst[:, 1:2]),
                   reads=[f"rstdL{idx}", "cst"], writes=[f"rstd{idx}"], tag=f"rstdB{tagp}_{I}")
            sch.op("vector", lambda: nc.vector.tensor_scalar(out=negmr[:, idx:idx + 1], in0=lnsum[:, idx:idx + 1],
                                                             scalar1=rstd[:, idx:idx + 1], scalar2=-1.0 / D,
                                                             op0=ALU.mult, op1=ALU.mult),
                   reads=[f"lnsum{idx}", f"rstd{idx}"], writes=[f"negmr{idx}"], tag=f"negmr{tagp}_{I}")

        def emit_front(I):
            p = I % 2

            emit_ln_stats(I, p, lambda p=p: x0[p][:], [f"x0{p}"], [], 0, "1")
            sch.op("vector", lambda p=p: nc.vector.tensor_scalar(out=xntm[:], in0=x0[p][:],
                                                                 scalar1=rstd[:, 0:1], scalar2=negmr[:, 0:1],
                                                                 op0=ALU.mult, op1=ALU.add),
                   reads=[f"x0{p}", "rstd0", "negmr0"], writes=["xntm"], tag=f"ln1ap_{I}")
            tap("xntm", I, lambda: xntm[:], "xntm")

            sch.op("tensor", lambda: [nc.tensor.transpose(PSB[:, 6144 + h * 128:6272 + h * 128],
                                                          xntm[:, h * 128:(h + 1) * 128], idn[:])
                                      for h in range(NH)][-1],
                   reads=["xntm", "idn"], pw=[(3072 + h * 64, 3136 + h * 64) for h in range(NH)], tag=f"xnT_{I}")
            sch.op("vector", lambda p=p: nc.vector.tensor_copy(out=xnfm[p][:], in_=PSB[:, 6144:7168]),
                   writes=[f"xnfm{p}"], pr=[(3072, 3584)], tag=f"evxn_{I}")
            tap("xnfm", I, lambda p=p: xnfm[p][:], f"xnfm{p}")

            def qkmm(base, wa, wb, h, pp):
                c0 = base + h * 256
                nc.tensor.matmul(PS[:, c0:c0 + 128], wa[:, h, :], xnfm[pp][:, h, :],
                                 start=(h % 2 == 0), stop=False)
                return nc.tensor.matmul(PS[:, c0 + 128:c0 + 256], wb[:, h, :], xnfm[pp][:, h, :],
                                        start=False, stop=(h % 2 == 1))
            sch.op("tensor", lambda p=p, qkmm=qkmm: [qkmm(0, wq, wk, h, p) for h in range(NH)][-1],
                   reads=[f"xnfm{p}", "wq", "wk"], pw=[(0, 2048)], tag=f"qk_{I}")
            sch.op("tensor", lambda p=p, qkmm=qkmm: [qkmm(2048, wqr, wkr, h, p) for h in range(NH)][-1],
                   reads=[f"xnfm{p}", "wqr", "wkr"], pw=[(2048, 4096)], tag=f"qkr_{I}")

            for g in range(2):
                qv = PS[:, g * 1024:(g + 1) * 1024].rearrange("p (h s t) -> p h s t", h=4, s=2)
                rv = PS[:, 2048 + g * 1024:3072 + g * 1024].rearrange("p (h s t) -> p h s t", h=4, s=2)
                cosb = tabs[p][:, 0, :, :].unsqueeze(1).broadcast_to([128, 4, 2, C])
                sinb = tabs[p][:, 1, :, :].unsqueeze(1).broadcast_to([128, 4, 2, C])
                t1v = scr[:, p, 0:1024].rearrange("p (h s t) -> p h s t", h=4, s=2)
                t2v = scr[:, p, 1024:2048].rearrange("p (h s t) -> p h s t", h=4, s=2)
                sch.op("vector", lambda qv=qv, cosb=cosb, t1v=t1v:
                       nc.vector.tensor_tensor(out=t1v, in0=qv, in1=cosb, op=ALU.mult),
                       reads=[f"tab{p}"], writes=[f"sA{p}"], pr=[(g * 1024, (g + 1) * 1024)], tag=f"t1_{I}_{g}")
                sch.op("vector", lambda rv=rv, sinb=sinb, t2v=t2v:
                       nc.vector.tensor_tensor(out=t2v, in0=rv, in1=sinb, op=ALU.mult),
                       reads=[f"tab{p}"], writes=[f"sB{p}"], pr=[(2048 + g * 1024, 3072 + g * 1024)], tag=f"t2_{I}_{g}")
                sch.op("gpsimd", lambda g=g, p=p: nc.gpsimd.tensor_tensor(
                           out=qkbf[:, 4 * g:4 * g + 4, :, :],
                           in0=scr[:, p, 0:1024], in1=scr[:, p, 1024:2048], op=ALU.add),
                       reads=[f"sA{p}", f"sB{p}"], writes=["qkbf"], tag=f"qkadd_{I}_{g}")
            tap("qkbf", I, lambda: qkbf[:], "qkbf")

            sch.op("gpsimd", lambda p=p: nc.gpsimd.tensor_tensor(out=qtb[p][:], in0=qkbf[:, :, 0, :],
                                                                 in1=gq[:], op=ALU.mult),
                   reads=["qkbf", "gq"], writes=[f"qtb{p}"], tag=f"qt_{I}")
            sch.op("gpsimd", lambda: nc.gpsimd.tensor_tensor(out=ktb[:], in0=qkbf[:, :, 1, :],
                                                             in1=gk[:], op=ALU.mult),
                   reads=["qkbf", "gk"], writes=["ktb"], tag=f"kt_{I}")

            sch.op("tensor", lambda p=p: [nc.tensor.matmul(PS[:, h * 128:(h + 1) * 128], xnfm[p][:, h, :],
                                                           wv[:, h, :], start=True, stop=True)
                                          for h in range(NH)][-1],
                   reads=[f"xnfm{p}", "wv"], pw=[(0, 1024)], tag=f"v_{I}")
            sch.op("vector", lambda p=p: nc.vector.tensor_copy(out=vbf[p][:], in_=PS[:, 0:1024]),
                   writes=[f"vbf{p}"], pr=[(0, 1024)], tag=f"evv_{I}")
            tap("vbf", I, lambda p=p: vbf[p][:], f"vbf{p}")

            sch.op("tensor", lambda: [nc.tensor.transpose(PSB[:, 6144 + h * 128:6272 + h * 128],
                                                          ktb[:, h, :], idn[:]) for h in range(NH)][-1],
                   reads=["ktb", "idn"], pw=[(3072 + h * 64, 3136 + h * 64) for h in range(NH)], tag=f"ktT_{I}")
            sch.op("vector", lambda p=p: nc.vector.tensor_copy(out=kttm[p][:], in_=PSB[:, 6144:7168]),
                   writes=[f"kttm{p}"], pr=[(3072, 3584)], tag=f"evkt_{I}")

            sch.op("tensor", lambda: [nc.tensor.matmul(PS[:, sacol(h):sacol(h) + 128],
                                                       qkbf[:, h, 1, :], qkbf[:, h, 0, :],
                                                       start=True, stop=True) for h in range(NH)][-1],
                   reads=["qkbf"], pw=[(sacol(h), sacol(h) + 128) for h in range(NH)], tag=f"s_{I}")
            for h in range(NH):
                sch.op("vector", lambda h=h, p=p: nc.vector.tensor_tensor(
                           out=stb[p][:, h, :], in0=PS[:, sacol(h):sacol(h) + 128], in1=dloc[:, h, :], op=ALU.mult),
                       reads=["dloc"], writes=[f"stb{p}h{h}"], pr=[(sacol(h), sacol(h) + 128)], tag=f"st_{I}_{h}")
            tap("stb", I, lambda p=p: stb[p][:], f"stb{p}h7")

        def emit_midtail(I):
            p = I % 2

            for q in range(4):
                h0, h1 = 2 * q, 2 * q + 1
                for h in (h0, h1):
                    sch.op("tensor", lambda h=h, p=p, I=I: nc.tensor.matmul(
                               PS[:, ycol(h):ycol(h) + 128], stb[p][:, h, :], vbf[p][:, h, :],
                               start=True, stop=(I == 0)),
                           reads=[f"stb{p}h{h}", f"vbf{p}"], pw=[(ycol(h), ycol(h) + 128)], tag=f"yi_{I}_{h}")
                for h in (h0, h1):
                    sch.op("tensor", lambda h=h, p=p: nc.tensor.matmul(
                               PS[:, sacol(h):sacol(h) + 128], kttm[p][:, h, :], vbf[p][:, h, :],
                               start=True, stop=True),
                           reads=[f"kttm{p}", f"vbf{p}"], pw=[(sacol(h), sacol(h) + 128)], tag=f"a_{I}_{h}")
                sch.op("tensor", lambda q=q, p=p: (
                           nc.tensor.matmul(PS[:, 3072:3584], xnfm[p][:, 2 * q, :],
                                            wg[:, 2 * q, 0:512], start=(q == 0), stop=False),
                           nc.tensor.matmul(PS[:, 3584:4096], xnfm[p][:, 2 * q, :],
                                            wg[:, 2 * q, 512:1024], start=(q == 0), stop=False))[-1],
                       reads=[f"xnfm{p}", "wg"], pw=[(3072, 4096)], tag=f"g_{I}_{2 * q}")
                if I > 0:
                    for h in (h0, h1):
                        sch.op("tensor", lambda h=h, p=p: nc.tensor.matmul(
                                   PS[:, ycol(h):ycol(h) + 128], qtb[p][:, h, :], Rbf[:, h, :],
                                   start=False, stop=True),
                               reads=[f"qtb{p}", "Rbf"], pw=[(ycol(h), ycol(h) + 128)], tag=f"yc_{I}_{h}")
                sch.op("tensor", lambda q=q, p=p: (
                           nc.tensor.matmul(PS[:, 3072:3584], xnfm[p][:, 2 * q + 1, :],
                                            wg[:, 2 * q + 1, 0:512], start=False, stop=(q == 3)),
                           nc.tensor.matmul(PS[:, 3584:4096], xnfm[p][:, 2 * q + 1, :],
                                            wg[:, 2 * q + 1, 512:1024], start=False, stop=(q == 3)))[-1],
                       reads=[f"xnfm{p}", "wg"], pw=[(3072, 4096)], tag=f"g_{I}_{2 * q + 1}")

            for h in range(NH):
                sch.op("vector", lambda h=h: nc.vector.scalar_tensor_tensor(
                           out=Rst[:, h, :], in0=Rst[:, h, :], scalar=gc_list[h],
                           in1=PS[:, sacol(h):sacol(h) + 128], op0=ALU.mult, op1=ALU.add),
                       reads=["Rst"], writes=["Rst"], pr=[(sacol(h), sacol(h) + 128)], tag=f"rup_{I}_{h}")
            sch.op("vector", lambda: nc.vector.tensor_copy(out=Rbf[:], in_=Rst[:]),
                   reads=["Rst"], writes=["Rbf"], tag=f"rbf_{I}")

            for h in range(NH):
                sch.op("vector", lambda h=h: nc.vector.bn_stats(out=st6g[:, h, :], in_=PS[:, ycol(h):ycol(h) + 128]),
                       writes=[f"st6g{h}", "st6g7"] if h == NH - 1 else [f"st6g{h}"],
                       pr=[(ycol(h), ycol(h) + 128)], tag=f"bng_{I}_{h}")
            allst = [f"st6g{h}" for h in range(NH)]
            sch.op("vector", lambda: nc.vector.tensor_tensor(out=gtmp[:, 0, :], in0=st6g[:, :, 1],
                                                             in1=st6g[:, :, 4], op=ALU.add),
                   reads=allst, writes=["gtA"], tag=f"gnc1_{I}")
            sch.op("vector", lambda: nc.vector.tensor_tensor(out=gtmp[:, 1, :], in0=st6g[:, :, 1],
                                                             in1=st6g[:, :, 4], op=ALU.subtract),
                   reads=allst, writes=["gtD"], tag=f"gnc2_{I}")
            sch.op("vector", lambda: nc.vector.tensor_tensor(out=gtmp[:, 2, :], in0=gtmp[:, 1, :],
                                                             in1=gtmp[:, 1, :], op=ALU.mult),
                   reads=["gtD"], writes=["gtDD"], tag=f"gnc3_{I}")
            sch.op("vector", lambda: nc.vector.tensor_tensor(out=gtmp[:, 3, :], in0=st6g[:, :, 2],
                                                             in1=st6g[:, :, 5], op=ALU.add),
                   reads=allst, writes=["gtS"], tag=f"gnc4_{I}")
            sch.op("vector", lambda: nc.vector.scalar_tensor_tensor(out=gtmp[:, 4, :], in0=gtmp[:, 3, :],
                                                                    scalar=4.0 / HD, in1=gtmp[:, 2, :],
                                                                    op0=ALU.mult, op1=ALU.add),
                   reads=["gtS", "gtDD"], writes=["gtV"], tag=f"gnc5_{I}")
            sch.op("scalar", lambda: nc.scalar.activation(out=rstdg[:], in_=gtmp[:, 4, :], func=AF.Ln,
                                                          bias=cst[:, 0:1], scale=cst[:, 2:3]),
                   reads=["gtV", "cst"], writes=["rstdg"], tag=f"rstdgA_{I}")
            sch.op("scalar", lambda: nc.scalar.activation(out=rstdg[:], in_=rstdg[:], func=AF.Exp, scale=cst[:, 1:2]),
                   reads=["cst", "rstdg"], writes=["rstdg"], tag=f"rstdgB_{I}")
            sch.op("vector", lambda: nc.vector.scalar_tensor_tensor(out=negmrg[:], in0=gtmp[:, 0, :], scalar=-0.5,
                                                                    in1=rstdg[:], op0=ALU.mult, op1=ALU.mult),
                   reads=["gtA", "rstdg"], writes=["negmrg"], tag=f"negmrg_{I}")

            for h in range(NH):
                sch.op("vector", lambda h=h, p=p: nc.vector.tensor_scalar(
                           out=scr[:, p, h * 128:(h + 1) * 128], in0=PS[:, ycol(h):ycol(h) + 128],
                           scalar1=rstdg[:, h:h + 1], scalar2=negmrg[:, h:h + 1], op0=ALU.mult, op1=ALU.add),
                       reads=["rstdg", "negmrg"], writes=[f"sA{p}"],
                       pr=[(ycol(h), ycol(h) + 128)], tag=f"gnap_{I}_{h}")
            sch.op("scalar", lambda p=p: nc.scalar.activation(out=scr[:, p, 1024:2048], in_=PS[:, 3072:4096],
                                                              func=AF.Silu),
                   writes=[f"sB{p}"], pr=[(3072, 4096)], tag=f"silu_{I}")
            sch.op("gpsimd", lambda p=p: nc.gpsimd.tensor_tensor(out=scr[:, p, 2048:3072], in0=scr[:, p, 0:1024],
                                                                 in1=gnwbc[:], op=ALU.mult),
                   reads=[f"sA{p}", "gnwbc"], writes=[f"sC{p}"], tag=f"gnw_{I}")
            sch.op("gpsimd", lambda p=p: nc.gpsimd.tensor_tensor(out=scr[:, p, 0:1024], in0=scr[:, p, 2048:3072],
                                                                 in1=gnbbc[:], op=ALU.add),
                   reads=[f"sC{p}", "gnbbc"], writes=[f"sA{p}"], tag=f"gnb_{I}")
            sch.op("vector", lambda p=p: nc.vector.tensor_tensor(out=hbf[:], in0=scr[:, p, 0:1024],
                                                                 in1=scr[:, p, 1024:2048], op=ALU.mult),
                   reads=[f"sA{p}", f"sB{p}"], writes=["hbf"], tag=f"gate_{I}")
            tap("hbf", I, lambda: hbf[:], "hbf")

            sch.op("tensor", lambda: [nc.tensor.transpose(PSB[:, 2048 + h * 128:2176 + h * 128],
                                                          hbf[:, h, :], idn[:]) for h in range(NH)][-1],
                   reads=["hbf", "idn"], pw=[(1024 + h * 64, 1088 + h * 64) for h in range(NH)], tag=f"hT_{I}")
            sch.op("vector", lambda: nc.vector.tensor_copy(out=hfm[:], in_=PSB[:, 2048:3072]),
                   writes=["hfm"], pr=[(1024, 1536)], tag=f"evh_{I}")
            for hb in range(NH):
                sch.op("tensor", lambda hb=hb: (nc.tensor.matmul(PS[:, 0:512], hfm[:, hb, :], wo[:, hb, 0:512],
                                                                 start=(hb == 0), stop=(hb == NH - 1)),
                                                nc.tensor.matmul(PS[:, 512:1024], hfm[:, hb, :], wo[:, hb, 512:1024],
                                                                 start=(hb == 0), stop=(hb == NH - 1)))[-1],
                       reads=["hfm", "wo"], pw=[(0, 1024)], tag=f"wo_{I}_{hb}")

            emit_ln_stats(I, p, lambda: PS[:, 0:1024], [], [(0, 1024)], 1, "2")
            sch.op("vector", lambda p=p: nc.vector.tensor_scalar(out=scr[:, p, 2048:3072], in0=PS[:, 0:1024],
                                                                 scalar1=rstd[:, 1:2], scalar2=negmr[:, 1:2],
                                                                 op0=ALU.mult, op1=ALU.add),
                   reads=["rstd1", "negmr1"], writes=[f"sC{p}"], pr=[(0, 1024)], tag=f"ln2ap_{I}")
            sch.op("gpsimd", lambda p=p: nc.gpsimd.tensor_tensor(out=x1[p][:], in0=scr[:, p, 2048:3072],
                                                                 in1=x0[p][:], op=ALU.add),
                   reads=[f"sC{p}", f"x0{p}"], writes=[f"x1{p}"], tag=f"resid1_{I}")
            tap("x1", I, lambda p=p: x1[p][:], f"x1{p}")

            emit_ln_stats(I, p, lambda p=p: x1[p][:], [f"x1{p}"], [], 2, "3")
            sch.op("vector", lambda p=p: nc.vector.tensor_scalar(out=xn2tm[:], in0=x1[p][:],
                                                                 scalar1=rstd[:, 2:3], scalar2=negmr[:, 2:3],
                                                                 op0=ALU.mult, op1=ALU.add),
                   reads=[f"x1{p}", "rstd2", "negmr2"], writes=["xntm"], tag=f"ln3ap_{I}")
            sch.op("tensor", lambda: [nc.tensor.transpose(PSB[:, 3072 + h * 128:3200 + h * 128],
                                                          xn2tm[:, h * 128:(h + 1) * 128], idn[:])
                                      for h in range(NH)][-1],
                   reads=["xntm", "idn"], pw=[(1536 + h * 64, 1600 + h * 64) for h in range(NH)], tag=f"xn2T_{I}")
            sch.op("vector", lambda: nc.vector.tensor_copy(out=xn2fm[:], in_=PSB[:, 3072:4096]),
                   writes=["xn2fm"], pr=[(1536, 2048)], tag=f"evxn2_{I}")
            tap("xn2fm", I, lambda: xn2fm[:], "xn2fm")

            for k in range(8):
                base = 1024 + (k % 2) * 512
                sch.op("tensor", lambda k=k, base=base: (
                           nc.tensor.matmul(PS[:, base:base + 128], u1[:, k, 0:128], xn2fm[:, k, :],
                                            start=(k < 2), stop=False),
                           nc.tensor.matmul(PS[:, base + 128:base + 256], u1[:, k, 128:256], xn2fm[:, k, :],
                                            start=False, stop=(k >= 6)))[-1],
                       reads=["xn2fm", "u1"], pw=[(base, base + 256)], tag=f"p_{I}_{k}")
            sch.op("vector", lambda: nc.vector.tensor_copy(out=qftmp[:], in_=PS[:, 1024:1280]),
                   writes=["qftmp"], pr=[(1024, 1280)], tag=f"pev1_{I}")
            sch.op("vector", lambda: nc.vector.tensor_tensor(out=pbf[:], in0=qftmp[:],
                                                             in1=PS[:, 1536:1792], op=ALU.add),
                   reads=["qftmp"], writes=["pbf"], pr=[(1536, 1792)], tag=f"pev2_{I}")
            tap("pbf", I, lambda: pbf[:], "pbf")

            qf_cols = (0, 512, 1024, 1536)
            for half in range(2):
                fbs = list(range(half * 16, half * 16 + 16))
                for rc in range(2):
                    sch.op("tensor", lambda rc=rc, fbs=fbs: [
                               nc.tensor.matmul(PS[:, 2048 + (fb % 16) * 128:2176 + (fb % 16) * 128],
                                                v1[:, rc, fb * 128:(fb + 1) * 128], pbf[:, rc, :],
                                                start=(rc == 0 and fb % 4 == 0),
                                                stop=False) for fb in fbs][-1],
                           reads=["pbf", "v1"], pw=[(2048, 4096)], tag=f"h_{I}_{half}_{rc}")
                sch.op("tensor", lambda fbs=fbs: [
                           nc.tensor.matmul(PS[:, 2048 + (fb % 16) * 128:2176 + (fb % 16) * 128],
                                            b1r[0:1, fb * 128:(fb + 1) * 128], onesb[:],
                                            start=False, stop=(fb % 4 == 3)) for fb in fbs][-1],
                       reads=["b1r", "ones"], pw=[(2048, 4096)], tag=f"hb1_{I}_{half}")
                sch.op("scalar", lambda: nc.scalar.activation(out=Hbf[:], in_=PS[:, 2048:4096],
                                                              func=AF.Gelu_apprx_tanh),
                       writes=["Hbf"], pr=[(2048, 4096)], tag=f"gelu_{I}_{half}")
                for fb in fbs:
                    c0 = qf_cols[fb % 4]
                    sch.op("tensor", lambda fb=fb, c0=c0: (
                               nc.tensor.matmul(PS[:, c0:c0 + 128], u2[:, fb, 0:128], Hbf[:, fb % 16, :],
                                                start=(fb < 4), stop=False),
                               nc.tensor.matmul(PS[:, c0 + 128:c0 + 256], u2[:, fb, 128:256], Hbf[:, fb % 16, :],
                                                start=False, stop=(fb >= 28)))[-1],
                           reads=["Hbf", "u2"], pw=[(c0, c0 + 256)], tag=f"u2_{I}_{fb}")

            sch.op("vector", lambda: nc.vector.tensor_copy(out=qftmp[:], in_=PS[:, 0:256]),
                   writes=["qftmp"], pr=[(0, 256)], tag=f"qfc1_{I}")
            sch.op("vector", lambda: nc.vector.tensor_tensor(out=qftmp[:], in0=qftmp[:], in1=PS[:, 512:768], op=ALU.add),
                   reads=["qftmp"], writes=["qftmp"], pr=[(512, 768)], tag=f"qfc2_{I}")
            sch.op("vector", lambda: nc.vector.tensor_tensor(out=qftmp[:], in0=qftmp[:], in1=PS[:, 1024:1280], op=ALU.add),
                   reads=["qftmp"], writes=["qftmp"], pr=[(1024, 1280)], tag=f"qfc3_{I}")
            sch.op("vector", lambda: nc.vector.tensor_tensor(out=qfbf[:], in0=qftmp[:], in1=PS[:, 1536:1792], op=ALU.add),
                   reads=["qftmp"], writes=["qfbf"], pr=[(1536, 1792)], tag=f"qfc4_{I}")
            tap("qfbf", I, lambda: qfbf[:], "qfbf")

            sch.op("tensor", lambda: (nc.tensor.matmul(PS[:, 3072:3584], qfbf[:, 0, :], v2[:, 0, 0:512],
                                                       start=True, stop=False),
                                      nc.tensor.matmul(PS[:, 3584:4096], qfbf[:, 0, :], v2[:, 0, 512:1024],
                                                       start=True, stop=False))[-1],
                   reads=["qfbf", "v2"], pw=[(3072, 4096)], tag=f"v2a_{I}")
            sch.op("tensor", lambda: (nc.tensor.matmul(PS[:, 3072:3584], qfbf[:, 1, :], v2[:, 1, 0:512],
                                                       start=False, stop=False),
                                      nc.tensor.matmul(PS[:, 3584:4096], qfbf[:, 1, :], v2[:, 1, 512:1024],
                                                       start=False, stop=False))[-1],
                   reads=["qfbf", "v2"], pw=[(3072, 4096)], tag=f"v2b_{I}")
            sch.op("tensor", lambda: (nc.tensor.matmul(PS[:, 3072:3584], onesb[:], b2r[0:1, 0:512],
                                                       start=False, stop=True),
                                      nc.tensor.matmul(PS[:, 3584:4096], onesb[:], b2r[0:1, 512:1024],
                                                       start=False, stop=True))[-1],
                   reads=["ones", "b2r"], pw=[(3072, 4096)], tag=f"b2_{I}")

            emit_ln_stats(I, p, lambda: PS[:, 3072:4096], [], [(3072, 4096)], 3, "4")
            sch.op("vector", lambda p=p: nc.vector.tensor_scalar(out=scr[:, p, 2048:3072], in0=PS[:, 3072:4096],
                                                                 scalar1=rstd[:, 3:4], scalar2=negmr[:, 3:4],
                                                                 op0=ALU.mult, op1=ALU.add),
                   reads=["rstd3", "negmr3"], writes=[f"sC{p}"], pr=[(3072, 4096)], tag=f"ln4ap_{I}")
            sch.op("gpsimd", lambda p=p: nc.gpsimd.tensor_tensor(out=outb[:], in0=scr[:, p, 2048:3072],
                                                                 in1=x1[p][:], op=ALU.add),
                   reads=[f"sC{p}", f"x1{p}"], writes=["out"], tag=f"resid2_{I}")
            sch.op("scalar", lambda I=I: nc.scalar.dma_start(out=y_h[I * C:(I + 1) * C, :], in_=outb[:]),
                   reads=["out"], dma=True, tag=f"store_{I}")

        for I in range(NCH):
            emit_front(I)
            if I > 0:
                emit_midtail(I - 1)
            if I + 1 < NCH:
                emit_loads(I + 1)
        emit_midtail(NCH - 1)

        block = E(nc.Block())
        sch.run(block)

    return nc, dbg_names


def kernel(L0_attn_X, L0_attn_W_Q, L0_attn_W_K, L0_attn_W_V, L0_attn_W_G, L0_attn_W_O,
           L0_attn_gn_weight, L0_attn_gn_bias, L0_ffn_U1, L0_ffn_V1, L0_ffn_b1,
           L0_ffn_U2, L0_ffn_V2, L0_ffn_b2):
    consts, gc_list = host_constants()
    nc, _ = build_nc(gc_list)
    common = dict(
        wq=np.ascontiguousarray(L0_attn_W_Q, np.float32),
        wk=np.ascontiguousarray(L0_attn_W_K, np.float32),
        wv=np.ascontiguousarray(L0_attn_W_V, np.float32),
        wg=np.ascontiguousarray(L0_attn_W_G, np.float32),
        wo=np.ascontiguousarray(L0_attn_W_O, np.float32),
        gnw=np.ascontiguousarray(L0_attn_gn_weight, np.float32),
        gnb=np.ascontiguousarray(L0_attn_gn_bias, np.float32),
        u1=np.ascontiguousarray(L0_ffn_U1, np.float32),
        v1=np.ascontiguousarray(L0_ffn_V1, np.float32),
        b1=np.ascontiguousarray(L0_ffn_b1, np.float32),
        u2=np.ascontiguousarray(L0_ffn_U2, np.float32),
        v2=np.ascontiguousarray(L0_ffn_V2, np.float32),
        b2=np.ascontiguousarray(L0_ffn_b2, np.float32),
        **consts,
    )
    X = np.ascontiguousarray(L0_attn_X, np.float32)
    in_maps = [dict(common, x=np.ascontiguousarray(X[c])) for c in range(8)]
    res = run_bass_kernel_spmd(nc, in_maps, core_ids=list(range(8)))
    return np.stack([np.asarray(r["y"], np.float32) for r in res.results], axis=0)


# version 06
# speedup vs baseline: 1.1341x; 1.1341x over previous
import math
from contextlib import ExitStack

import numpy as np
import ml_dtypes

import concourse.bass as bass
import concourse.mybir as mybir
from concourse.bass_utils import run_bass_kernel_spmd

F32 = mybir.dt.float32
BF16 = mybir.dt.bfloat16
AF = mybir.ActivationFunctionType
ALU = mybir.AluOpType

S, D, NH, HD, C = 4096, 1024, 8, 128, 128
NCH = S // C
EPS = 1e-5


def host_constants():
    gam = 1.0 - np.exp(np.linspace(math.log(1.0 / 32.0), math.log(1.0 / 512.0), NH))
    ii = np.arange(C)
    diff = ii[None, :] - ii[:, None]
    dloc = np.where(diff[:, :, None] >= 0,
                    gam[None, None, :] ** np.maximum(diff, 0)[:, :, None], 0.0)
    DLOC = np.ascontiguousarray(dloc.transpose(0, 2, 1)).astype(np.float32)
    GQ = np.ascontiguousarray(np.broadcast_to((gam[:, None] ** (ii[None, :] + 1))[None],
                                              (128, NH, C))).astype(np.float32)
    GK = np.ascontiguousarray(np.broadcast_to((gam[:, None] ** (C - 1 - ii[None, :]))[None],
                                              (128, NH, C))).astype(np.float32)
    GC = [float(np.float32(gam[h] ** C)) for h in range(NH)]

    half = HD // 2
    scale = (np.arange(0, HD, 2, dtype=np.float64) + 0.4 * HD) / (1.4 * HD)
    pos = np.arange(S, dtype=np.float64)
    sc = scale[None, :] ** (pos[:, None] / 512.0)
    inv_freq = 1.0 / 10000.0 ** (np.arange(half, dtype=np.float64) / half)
    sinus = pos[:, None] * inv_freq[None, :]
    sin, cos = np.sin(sinus), np.cos(sinus)

    def rep2(a):
        return np.repeat(a, 2, axis=-1)

    cosQ, sinQ = rep2(cos * sc), rep2(sin * sc)
    cosK, sinK = rep2(cos / sc), rep2(sin / sc)
    TAB = np.empty((NCH, 128, 2, 2, C), np.float32)
    for ci in range(NCH):
        sl = slice(ci * C, (ci + 1) * C)
        TAB[ci, :, 0, 0, :] = cosQ[sl].T
        TAB[ci, :, 1, 0, :] = sinQ[sl].T
        TAB[ci, :, 0, 1, :] = cosK[sl].T
        TAB[ci, :, 1, 1, :] = sinK[sl].T

    P = np.zeros((128, 128), np.float64)
    for m in range(64):
        P[2 * m, 2 * m + 1] = -1.0
        P[2 * m + 1, 2 * m] = 1.0
    PT = np.ascontiguousarray(P.T).astype(ml_dtypes.bfloat16)
    IDN = np.eye(128, dtype=ml_dtypes.bfloat16)
    IDNF = np.eye(128, dtype=np.float32)
    ONES = np.ones((1, 128), ml_dtypes.bfloat16)
    return dict(TAB=TAB, DLOC=DLOC, GQ=GQ, GK=GK, PT=PT, IDN=IDN, IDNF=IDNF, ONES=ONES), GC


class Sched:

    def __init__(self, nc, sems):
        self.nc = nc
        self.sems = sems
        self.cnt = {k: 0 for k in sems}
        self.ops = []
        self.sb_w = {}
        self.sb_r = {}
        self.ps_w_bytes = []
        self.ps_w_bank = {}
        self.ps_r_bank = {}
        self.waited = {}

    @staticmethod
    def banks(lo, hi):
        return range(lo // 512, (hi + 511) // 512)

    def op(self, eng, fn, reads=(), writes=(), pr=(), pw=(), dma=False, tag=""):
        sem = ("d_" if dma else "m_") + eng
        deps = {}

        def need(tok, force=False):
            if tok is None:
                return
            s, v = tok
            if v > deps.get(s, 0):
                deps[s] = v

        for b in reads:
            need(self.sb_w.get(b))
        for b in writes:
            need(self.sb_w.get(b))
            for s2, v2 in self.sb_r.get(b, {}).items():
                need((s2, v2))
        for (lo, hi) in pr:
            for bk in self.banks(lo, hi):
                for s2, v2 in self.ps_w_bank.get(bk, {}).items():
                    need((s2, v2), force=True)
                for s2, v2 in self.ps_r_bank.get(bk, {}).items():
                    if s2 != sem:
                        need((s2, v2), force=True)
        for (lo, hi) in pw:
            assert eng == "tensor", f"only PE writes PSUM (op {tag})"
            for (l2, h2, s2, v2) in self.ps_w_bytes:
                if l2 < hi and lo < h2:
                    need((s2, v2), force=True)
            for bk in self.banks(lo, hi):
                for s2, v2 in self.ps_r_bank.get(bk, {}).items():
                    need((s2, v2), force=True)

        if dma and self.cnt[sem] > 0:
            deps[sem] = max(deps.get(sem, 0), self.cnt[sem])

        waits = []
        for s, v in sorted(deps.items()):
            if self.waited.get((eng, s), 0) < v:
                waits.append((s, v))
                self.waited[(eng, s)] = v

        self.cnt[sem] += 16 if dma else 1
        tok = (sem, self.cnt[sem])
        for b in reads:
            self.sb_r.setdefault(b, {})[sem] = tok[1]
        for b in writes:
            self.sb_w[b] = tok
            self.sb_r[b] = {}
        for (lo, hi) in pr:
            for bk in self.banks(lo, hi):
                self.ps_r_bank.setdefault(bk, {})[sem] = tok[1]
        for (lo, hi) in pw:
            self.ps_w_bytes = [e for e in self.ps_w_bytes if not (lo <= e[0] and e[1] <= hi)]
            self.ps_w_bytes.append((lo, hi, sem, tok[1]))
            for bk in self.banks(lo, hi):
                self.ps_w_bank.setdefault(bk, {})[sem] = tok[1]
        self.ops.append((eng, waits, fn, sem, 16 if dma else 1, tag))
        return tok

    def run(self, block):
        for eng in ("sync", "scalar", "tensor", "vector", "gpsimd"):
            ops = [o for o in self.ops if o[0] == eng]

            def make(ops=ops, eng=eng):
                def go(e):
                    for (_, waits, fn, semname, inc, tag) in ops:
                        for (s, v) in waits:
                            e.wait_ge(self.sems[s], v)
                        inst = fn()
                        inst.then_inc(self.sems[semname], inc)
                    if eng == "scalar":
                        e.wait_ge(self.sems["d_scalar"], self.cnt["d_scalar"])
                return go

            getattr(block, eng)(make())


def build_nc(gc_list, dbg=None):
    dbg = dbg or set()
    nc = bass.Bass("TRN2", target_bir_lowering=False)

    def din(name, shape, dtype=F32):
        return nc.dram_tensor(name, shape, dtype, kind="ExternalInput").ap()

    x_h = din("x", [S, D])
    wq_h = din("wq", [NH, HD, HD]); wk_h = din("wk", [NH, HD, HD]); wv_h = din("wv", [NH, HD, HD])
    wg_h = din("wg", [D, D]); wo_h = din("wo", [D, D])
    gnw_h = din("gnw", [D]); gnb_h = din("gnb", [D])
    u1_h = din("u1", [D, 256]); v1_h = din("v1", [256, 4096]); b1_h = din("b1", [4096])
    u2_h = din("u2", [4096, 256]); v2_h = din("v2", [256, D]); b2_h = din("b2", [D])
    tab_h = din("TAB", [NCH, 128, 2, 2, C])
    dloc_h = din("DLOC", [128, NH, C]); gq_h = din("GQ", [128, NH, C]); gk_h = din("GK", [128, NH, C])
    pt_h = din("PT", [128, 128], BF16); idn_h = din("IDN", [128, 128], BF16)
    ones_h = din("ONES", [1, 128], BF16)
    y_h = nc.dram_tensor("y", [S, D], F32, kind="ExternalOutput").ap()

    dbg_names = []

    with ExitStack() as st:
        E = st.enter_context
        wq = E(nc.sbuf_tensor("wq_s", [128, NH, HD], BF16))
        wk = E(nc.sbuf_tensor("wk_s", [128, NH, HD], BF16))
        wv = E(nc.sbuf_tensor("wv_s", [128, NH, HD], BF16))
        wqr = E(nc.sbuf_tensor("wqr_s", [128, NH, HD], BF16))
        wkr = E(nc.sbuf_tensor("wkr_s", [128, NH, HD], BF16))
        wg = E(nc.sbuf_tensor("wg_s", [128, 8, D], BF16))
        wo = E(nc.sbuf_tensor("wo_s", [128, 8, D], BF16))
        u1 = E(nc.sbuf_tensor("u1_s", [128, 8, 256], BF16))
        v1 = E(nc.sbuf_tensor("v1_s", [128, 2, 4096], BF16))
        u2 = E(nc.sbuf_tensor("u2_s", [128, 32, 256], BF16))
        v2 = E(nc.sbuf_tensor("v2_s", [128, 2, D], BF16))
        b2r = E(nc.sbuf_tensor("b2r_s", [1, D], BF16))
        onesb = E(nc.sbuf_tensor("ones_s", [1, 128], BF16))
        gnwbc = E(nc.sbuf_tensor("gnwbc", [128, D], F32))
        gnbbc = E(nc.sbuf_tensor("gnbbc", [128, D], F32))
        b1r = E(nc.sbuf_tensor("b1r_s", [1, 4096], BF16))
        dloc = E(nc.sbuf_tensor("dloc", [128, NH, C], F32))
        gq = E(nc.sbuf_tensor("gq", [128, NH, C], F32))
        gk = E(nc.sbuf_tensor("gk", [128, NH, C], F32))
        idn = E(nc.sbuf_tensor("idn", [128, 128], BF16))
        pt = E(nc.sbuf_tensor("pt", [128, 128], BF16))
        wtt = E(nc.sbuf_tensor("wtt", [128, 128], BF16))

        x0 = [E(nc.sbuf_tensor(f"x0_{p}", [128, D], F32)) for p in range(2)]
        tabs = [E(nc.sbuf_tensor(f"tab_{p}", [128, 2, 2, C], F32)) for p in range(2)]
        xntm = E(nc.sbuf_tensor("xntm", [128, D], BF16))
        xnfm = [E(nc.sbuf_tensor(f"xnfm_{pp}", [128, NH, C], BF16)) for pp in range(2)]
        qkbf = E(nc.sbuf_tensor("qkbf", [128, NH, 2, C], BF16))
        qtb = [E(nc.sbuf_tensor(f"qtb_{pp}", [128, NH, C], BF16)) for pp in range(2)]
        ktb = E(nc.sbuf_tensor("ktb", [128, NH, C], BF16))
        kttm = [E(nc.sbuf_tensor(f"kttm_{pp}", [128, NH, C], BF16)) for pp in range(2)]
        vbf = [E(nc.sbuf_tensor(f"vbf_{pp}", [128, NH, C], BF16)) for pp in range(2)]
        stb = [E(nc.sbuf_tensor(f"stb_{pp}", [128, NH, C], BF16)) for pp in range(2)]
        Rst = E(nc.sbuf_tensor("Rst", [128, NH, HD], F32))
        Rbf = E(nc.sbuf_tensor("Rbf", [128, NH, HD], BF16))
        scr = E(nc.sbuf_tensor("scr", [128, 2, 3072], F32))
        hbf = E(nc.sbuf_tensor("hbf", [128, NH, C], BF16))
        hfm = E(nc.sbuf_tensor("hfm", [128, NH, C], BF16))
        x1 = [E(nc.sbuf_tensor(f"x1_{p}", [128, D], F32)) for p in range(2)]
        xn2tm = xntm
        xn2fm = E(nc.sbuf_tensor("xn2fm", [128, NH, C], BF16))
        pbf = E(nc.sbuf_tensor("pbf", [128, 2, 128], BF16))
        Hbf = E(nc.sbuf_tensor("Hbf", [128, 16, 128], BF16))
        qftmp = E(nc.sbuf_tensor("qftmp", [128, 256], F32))
        qfbf = E(nc.sbuf_tensor("qfbf", [128, 2, 128], BF16))
        outb = E(nc.sbuf_tensor("out_s", [128, D], F32))
        st6 = [E(nc.sbuf_tensor(f"st6_{p}", [128, 2, 6], F32)) for p in range(2)]
        st2 = [E(nc.sbuf_tensor(f"st2_{p}", [128, 2], F32)) for p in range(2)]
        st6g = E(nc.sbuf_tensor("st6g", [128, NH, 6], F32))
        gtmp = E(nc.sbuf_tensor("gtmp", [128, 5, NH], F32))
        rstd = E(nc.sbuf_tensor("rstd", [128, 8], F32))
        negmr = E(nc.sbuf_tensor("negmr", [128, 4], F32))
        rstdg = E(nc.sbuf_tensor("rstdg", [128, NH], F32))
        cst = E(nc.sbuf_tensor("cst", [128, 4], F32))
        lnsum = E(nc.sbuf_tensor("lnsum", [128, 4], F32))
        lnsq = E(nc.sbuf_tensor("lnsq", [128, 4], F32))
        lntmp = E(nc.sbuf_tensor("lntmp", [128, 4], F32))
        negmrg = E(nc.sbuf_tensor("negmrg", [128, NH], F32))

        PS = E(nc.psum_tensor("PS", [128, 4096], F32))
        PSB = PS[:].bitcast(BF16)

        sems = {}
        for name in ("m_tensor", "m_vector", "m_scalar", "m_gpsimd", "d_sync", "d_scalar", "d_gpsimd"):
            sems[name] = E(nc.semaphore(name))
        sch = Sched(nc, sems)

        def tap(stage, I, ap_fn, bufname):
            if (stage, I) in dbg:
                ap = ap_fn()
                nm = f"dbg_{stage}_{I}"
                t = nc.dram_tensor(nm, list(ap.shape), ap.dtype, kind="ExternalOutput").ap()
                dbg_names.append(nm)
                sch.op("sync", lambda ap=ap, t=t: nc.sync.dma_start(out=t, in_=ap),
                       reads=[bufname], dma=True, tag=f"tap_{stage}_{I}")

        for fn, wr, tg in (
            (lambda: nc.gpsimd.dma_start(out=wq[:], in_=wq_h.rearrange("h d e -> d h e")), ["wq"], "ld_wq"),
            (lambda: nc.gpsimd.dma_start(out=wk[:], in_=wk_h.rearrange("h d e -> d h e")), ["wk"], "ld_wk"),
            (lambda: nc.gpsimd.dma_start(out=wv[:], in_=wv_h.rearrange("h d e -> d h e")), ["wv"], "ld_wv"),
            (lambda: nc.gpsimd.dma_start(out=wg[:], in_=wg_h.rearrange("(b p) o -> p b o", p=128)), ["wg"], "ld_wg"),
            (lambda: nc.gpsimd.dma_start(out=wo[:], in_=wo_h.rearrange("(b p) o -> p b o", p=128)), ["wo"], "ld_wo"),
            (lambda: nc.gpsimd.dma_start(out=u1[:], in_=u1_h.rearrange("(b p) r -> p b r", p=128)), ["u1"], "ld_u1"),
            (lambda: nc.gpsimd.dma_start(out=v1[:], in_=v1_h.rearrange("(b p) f -> p b f", p=128)), ["v1"], "ld_v1"),
            (lambda: nc.gpsimd.dma_start(out=u2[:], in_=u2_h.rearrange("(b p) r -> p b r", p=128)), ["u2"], "ld_u2"),
            (lambda: nc.gpsimd.dma_start(out=v2[:], in_=v2_h.rearrange("(b p) o -> p b o", p=128)), ["v2"], "ld_v2"),
            (lambda: nc.gpsimd.dma_start(out=b1r[:], in_=b1_h.unsqueeze(0)), ["b1r"], "ld_b1"),
            (lambda: nc.gpsimd.dma_start(out=b2r[:], in_=b2_h.unsqueeze(0)), ["b2r"], "ld_b2"),
            (lambda: nc.gpsimd.dma_start(out=onesb[:], in_=ones_h), ["ones"], "ld_ones"),
        ):
            sch.op("gpsimd", fn, writes=wr, dma=True, tag=tg)

        for fn, wr, tg in (
            (lambda: nc.sync.dma_start(out=gnwbc[:], in_=gnw_h.unsqueeze(0).broadcast_to([128, D])), ["gnwbc"], "ld_gnw"),
            (lambda: nc.sync.dma_start(out=gnbbc[:], in_=gnb_h.unsqueeze(0).broadcast_to([128, D])), ["gnbbc"], "ld_gnb"),
            (lambda: nc.sync.dma_start(out=dloc[:], in_=dloc_h), ["dloc"], "ld_dloc"),
            (lambda: nc.sync.dma_start(out=gq[:], in_=gq_h), ["gq"], "ld_gq"),
            (lambda: nc.sync.dma_start(out=gk[:], in_=gk_h), ["gk"], "ld_gk"),
            (lambda: nc.sync.dma_start(out=idn[:], in_=idn_h), ["idn"], "ld_idn"),
            (lambda: nc.sync.dma_start(out=pt[:], in_=pt_h), ["pt"], "ld_pt"),
        ):
            sch.op("sync", fn, writes=wr, dma=True, tag=tg)

        for (src, dst, sname, dname) in ((wq, wqr, "wq", "wqr"), (wk, wkr, "wk", "wkr")):
            for h in range(NH):
                sch.op("tensor", lambda src=src, h=h: nc.tensor.transpose(PSB[:, 0:128], src[:, h, :], idn[:]),
                       reads=[sname, "idn"], pw=[(0, 64)], tag=f"T_{dname}{h}")
                sch.op("vector", lambda: nc.vector.tensor_copy(out=wtt[:], in_=PSB[:, 0:128]),
                       writes=["wtt"], pr=[(0, 64)], tag=f"ev_{dname}{h}")
                sch.op("tensor", lambda: nc.tensor.matmul(PS[:, 512:640], wtt[:], pt[:], start=True, stop=True),
                       reads=["wtt", "pt"], pw=[(512, 640)], tag=f"mm_{dname}{h}")
                sch.op("vector", lambda dst=dst, h=h: nc.vector.tensor_copy(out=dst[:, h, :], in_=PS[:, 512:640]),
                       writes=[dname], pr=[(512, 640)], tag=f"ev2_{dname}{h}")

        tap("wqr", 0, lambda: wqr[:], "wqr")
        tap("wkr", 0, lambda: wkr[:], "wkr")
        sch.op("vector", lambda: nc.vector.memset(Rst[:], 0.0),
               writes=[f"Rst{h}" for h in range(NH)], tag="memset_R")
        sch.op("vector", lambda: (nc.vector.memset(cst[:, 0:1], EPS),
                                  nc.vector.memset(cst[:, 1:2], -0.5),
                                  nc.vector.memset(cst[:, 2:3], 0.25),
                                  nc.vector.memset(cst[:, 3:4], 1.0 / D))[-1],
               writes=["cst"], tag="memset_cst")

        def emit_loads(I):
            p = I % 2
            sch.op("sync", lambda I=I, p=p: nc.sync.dma_start(out=x0[p][:], in_=x_h[I * C:(I + 1) * C, :]),
                   writes=[f"x0{p}"], dma=True, tag=f"ld_x0_{I}")
            sch.op("sync", lambda I=I, p=p: nc.sync.dma_start(out=tabs[p][:], in_=tab_h[I]),
                   writes=[f"tab{p}"], dma=True, tag=f"ld_tab_{I}")

        emit_loads(0)

        def sacol(h):
            return 1024 + (h % 2) * 512 + (h // 2) * 128

        def ycol(h):
            return 2048 + (h % 2) * 512 + (h // 2) * 128

        def emit_ln_stats(I, p, srcfn, src_reads, src_pr, idx, tagp):
            sch.op("scalar", lambda: nc.scalar.activation(out=scr[:, p, 1024:2048], in_=srcfn(),
                                                          func=AF.Square, accum_out=lnsq[:, idx:idx + 1]),
                   reads=src_reads, writes=[f"sB{p}", f"lnsq{idx}"], pr=src_pr, tag=f"sq{tagp}_{I}")
            sch.op("vector", lambda: nc.vector.tensor_scalar(out=scr[:, p, 0:1024], in0=srcfn(),
                                                             scalar1=1.0, scalar2=None, op0=ALU.mult,
                                                             op1=ALU.add, accum_out=lnsum[:, idx:idx + 1]),
                   reads=src_reads, writes=[f"sA{p}", f"lnsum{idx}"], pr=src_pr, tag=f"sm{tagp}_{I}")
            sch.op("vector", lambda: nc.vector.tensor_tensor(out=lntmp[:, idx:idx + 1], in0=lnsum[:, idx:idx + 1],
                                                             in1=lnsum[:, idx:idx + 1], op=ALU.mult),
                   reads=[f"lnsum{idx}"], writes=[f"lntmp{idx}"], tag=f"varA{tagp}_{I}")
            sch.op("vector", lambda: nc.vector.scalar_tensor_tensor(out=lntmp[:, idx:idx + 1],
                                                                    in0=lntmp[:, idx:idx + 1],
                                                                    scalar=-1.0 / D, in1=lnsq[:, idx:idx + 1],
                                                                    op0=ALU.mult, op1=ALU.add),
                   reads=[f"lntmp{idx}", f"lnsq{idx}"], writes=[f"lntmp{idx}"], tag=f"varB{tagp}_{I}")
            sch.op("scalar", lambda: nc.scalar.activation(out=rstd[:, 4 + idx:5 + idx], in_=lntmp[:, idx:idx + 1],
                                                          func=AF.Ln, bias=cst[:, 0:1], scale=cst[:, 3:4]),
                   reads=[f"lntmp{idx}", "cst"], writes=[f"rstdL{idx}"], tag=f"rstdA{tagp}_{I}")
            sch.op("scalar", lambda: nc.scalar.activation(out=rstd[:, idx:idx + 1], in_=rstd[:, 4 + idx:5 + idx],
                                                          func=AF.Exp, scale=cst[:, 1:2]),
                   reads=[f"rstdL{idx}", "cst"], writes=[f"rstd{idx}"], tag=f"rstdB{tagp}_{I}")
            sch.op("vector", lambda: nc.vector.tensor_scalar(out=negmr[:, idx:idx + 1], in0=lnsum[:, idx:idx + 1],
                                                             scalar1=rstd[:, idx:idx + 1], scalar2=-1.0 / D,
                                                             op0=ALU.mult, op1=ALU.mult),
                   reads=[f"lnsum{idx}", f"rstd{idx}"], writes=[f"negmr{idx}"], tag=f"negmr{tagp}_{I}")

        def emit_front(I):
            p = I % 2

            emit_ln_stats(I, p, lambda p=p: x0[p][:], [f"x0{p}"], [], 0, "1")
            sch.op("vector", lambda p=p: nc.vector.tensor_scalar(out=xntm[:], in0=x0[p][:],
                                                                 scalar1=rstd[:, 0:1], scalar2=negmr[:, 0:1],
                                                                 op0=ALU.mult, op1=ALU.add),
                   reads=[f"x0{p}", "rstd0", "negmr0"], writes=["xntm"], tag=f"ln1ap_{I}")
            tap("xntm", I, lambda: xntm[:], "xntm")

            sch.op("tensor", lambda: [nc.tensor.transpose(PSB[:, 6144 + h * 128:6272 + h * 128],
                                                          xntm[:, h * 128:(h + 1) * 128], idn[:])
                                      for h in range(NH)][-1],
                   reads=["xntm", "idn"], pw=[(3072 + h * 64, 3136 + h * 64) for h in range(NH)], tag=f"xnT_{I}")
            sch.op("vector", lambda p=p: nc.vector.tensor_copy(out=xnfm[p][:], in_=PSB[:, 6144:7168]),
                   writes=[f"xnfm{p}"], pr=[(3072, 3584)], tag=f"evxn_{I}")
            tap("xnfm", I, lambda p=p: xnfm[p][:], f"xnfm{p}")

            def qkmm(base, wa, wb, h, pp):
                c0 = base + h * 256
                nc.tensor.matmul(PS[:, c0:c0 + 128], wa[:, h, :], xnfm[pp][:, h, :],
                                 start=(h % 2 == 0), stop=False)
                return nc.tensor.matmul(PS[:, c0 + 128:c0 + 256], wb[:, h, :], xnfm[pp][:, h, :],
                                        start=False, stop=(h % 2 == 1))
            sch.op("tensor", lambda p=p, qkmm=qkmm: [qkmm(0, wq, wk, h, p) for h in range(NH)][-1],
                   reads=[f"xnfm{p}", "wq", "wk"], pw=[(0, 2048)], tag=f"qk_{I}")
            sch.op("tensor", lambda p=p, qkmm=qkmm: [qkmm(2048, wqr, wkr, h, p) for h in range(NH)][-1],
                   reads=[f"xnfm{p}", "wqr", "wkr"], pw=[(2048, 4096)], tag=f"qkr_{I}")

            for g in range(2):
                qv = PS[:, g * 1024:(g + 1) * 1024].rearrange("p (h s t) -> p h s t", h=4, s=2)
                rv = PS[:, 2048 + g * 1024:3072 + g * 1024].rearrange("p (h s t) -> p h s t", h=4, s=2)
                cosb = tabs[p][:, 0, :, :].unsqueeze(1).broadcast_to([128, 4, 2, C])
                sinb = tabs[p][:, 1, :, :].unsqueeze(1).broadcast_to([128, 4, 2, C])
                t1v = scr[:, p, 0:1024].rearrange("p (h s t) -> p h s t", h=4, s=2)
                t2v = scr[:, p, 1024:2048].rearrange("p (h s t) -> p h s t", h=4, s=2)
                sch.op("vector", lambda qv=qv, cosb=cosb, t1v=t1v:
                       nc.vector.tensor_tensor(out=t1v, in0=qv, in1=cosb, op=ALU.mult),
                       reads=[f"tab{p}"], writes=[f"sA{p}"], pr=[(g * 1024, (g + 1) * 1024)], tag=f"t1_{I}_{g}")
                sch.op("vector", lambda rv=rv, sinb=sinb, t2v=t2v:
                       nc.vector.tensor_tensor(out=t2v, in0=rv, in1=sinb, op=ALU.mult),
                       reads=[f"tab{p}"], writes=[f"sB{p}"], pr=[(2048 + g * 1024, 3072 + g * 1024)], tag=f"t2_{I}_{g}")
                sch.op("vector", lambda g=g, p=p: nc.vector.tensor_tensor(
                           out=qkbf[:, 4 * g:4 * g + 4, :, :],
                           in0=scr[:, p, 0:1024], in1=scr[:, p, 1024:2048], op=ALU.add),
                       reads=[f"sA{p}", f"sB{p}"], writes=["qkbf"], tag=f"qkadd_{I}_{g}")
            tap("qkbf", I, lambda: qkbf[:], "qkbf")

            sch.op("gpsimd", lambda p=p: nc.gpsimd.tensor_tensor(out=qtb[p][:], in0=qkbf[:, :, 0, :],
                                                                 in1=gq[:], op=ALU.mult),
                   reads=["qkbf", "gq"], writes=[f"qtb{p}"], tag=f"qt_{I}")
            sch.op("gpsimd", lambda: nc.gpsimd.tensor_tensor(out=ktb[:], in0=qkbf[:, :, 1, :],
                                                             in1=gk[:], op=ALU.mult),
                   reads=["qkbf", "gk"], writes=["ktb"], tag=f"kt_{I}")

            sch.op("tensor", lambda p=p: [nc.tensor.matmul(PS[:, h * 128:(h + 1) * 128], xnfm[p][:, h, :],
                                                           wv[:, h, :], start=True, stop=True)
                                          for h in range(NH)][-1],
                   reads=[f"xnfm{p}", "wv"], pw=[(0, 1024)], tag=f"v_{I}")
            sch.op("vector", lambda p=p: nc.vector.tensor_copy(out=vbf[p][:], in_=PS[:, 0:1024]),
                   writes=[f"vbf{p}"], pr=[(0, 1024)], tag=f"evv_{I}")
            tap("vbf", I, lambda p=p: vbf[p][:], f"vbf{p}")

            sch.op("tensor", lambda: [nc.tensor.transpose(PSB[:, 6144 + h * 128:6272 + h * 128],
                                                          ktb[:, h, :], idn[:]) for h in range(NH)][-1],
                   reads=["ktb", "idn"], pw=[(3072 + h * 64, 3136 + h * 64) for h in range(NH)], tag=f"ktT_{I}")
            sch.op("vector", lambda p=p: nc.vector.tensor_copy(out=kttm[p][:], in_=PSB[:, 6144:7168]),
                   writes=[f"kttm{p}"], pr=[(3072, 3584)], tag=f"evkt_{I}")

            sch.op("tensor", lambda: [nc.tensor.matmul(PS[:, sacol(h):sacol(h) + 128],
                                                       qkbf[:, h, 1, :], qkbf[:, h, 0, :],
                                                       start=True, stop=True) for h in range(NH)][-1],
                   reads=["qkbf"], pw=[(sacol(h), sacol(h) + 128) for h in range(NH)], tag=f"s_{I}")
            for h in range(NH):
                sch.op("vector", lambda h=h, p=p: nc.vector.tensor_tensor(
                           out=stb[p][:, h, :], in0=PS[:, sacol(h):sacol(h) + 128], in1=dloc[:, h, :], op=ALU.mult),
                       reads=["dloc"], writes=[f"stb{p}h{h}"], pr=[(sacol(h), sacol(h) + 128)], tag=f"st_{I}_{h}")
            tap("stb", I, lambda p=p: stb[p][:], f"stb{p}h7")

        def emit_midtail(I):
            p = I % 2

            for q in range(4):
                h0, h1 = 2 * q, 2 * q + 1
                for h in (h0, h1):
                    sch.op("tensor", lambda h=h, p=p, I=I: nc.tensor.matmul(
                               PS[:, ycol(h):ycol(h) + 128], stb[p][:, h, :], vbf[p][:, h, :],
                               start=True, stop=(I == 0)),
                           reads=[f"stb{p}h{h}", f"vbf{p}"], pw=[(ycol(h), ycol(h) + 128)], tag=f"yi_{I}_{h}")
                for h in (h0, h1):
                    sch.op("tensor", lambda h=h, p=p: nc.tensor.matmul(
                               PS[:, sacol(h):sacol(h) + 128], kttm[p][:, h, :], vbf[p][:, h, :],
                               start=True, stop=True),
                           reads=[f"kttm{p}", f"vbf{p}"], pw=[(sacol(h), sacol(h) + 128)], tag=f"a_{I}_{h}")
                sch.op("tensor", lambda q=q, p=p: (
                           nc.tensor.matmul(PS[:, 3072:3584], xnfm[p][:, 2 * q, :],
                                            wg[:, 2 * q, 0:512], start=(q == 0), stop=False),
                           nc.tensor.matmul(PS[:, 3584:4096], xnfm[p][:, 2 * q, :],
                                            wg[:, 2 * q, 512:1024], start=(q == 0), stop=False))[-1],
                       reads=[f"xnfm{p}", "wg"], pw=[(3072, 4096)], tag=f"g_{I}_{2 * q}")
                if I > 0:
                    for h in (h0, h1):
                        sch.op("tensor", lambda h=h, p=p: nc.tensor.matmul(
                                   PS[:, ycol(h):ycol(h) + 128], qtb[p][:, h, :], Rbf[:, h, :],
                                   start=False, stop=True),
                               reads=[f"qtb{p}", "Rbf"], pw=[(ycol(h), ycol(h) + 128)], tag=f"yc_{I}_{h}")
                sch.op("tensor", lambda q=q, p=p: (
                           nc.tensor.matmul(PS[:, 3072:3584], xnfm[p][:, 2 * q + 1, :],
                                            wg[:, 2 * q + 1, 0:512], start=False, stop=(q == 3)),
                           nc.tensor.matmul(PS[:, 3584:4096], xnfm[p][:, 2 * q + 1, :],
                                            wg[:, 2 * q + 1, 512:1024], start=False, stop=(q == 3)))[-1],
                       reads=[f"xnfm{p}", "wg"], pw=[(3072, 4096)], tag=f"g_{I}_{2 * q + 1}")

            for h in range(NH):
                sch.op("vector", lambda h=h: nc.vector.scalar_tensor_tensor(
                           out=Rst[:, h, :], in0=Rst[:, h, :], scalar=gc_list[h],
                           in1=PS[:, sacol(h):sacol(h) + 128], op0=ALU.mult, op1=ALU.add),
                       reads=[f"Rst{h}"], writes=[f"Rst{h}"], pr=[(sacol(h), sacol(h) + 128)], tag=f"rup_{I}_{h}")
            sch.op("vector", lambda: nc.vector.tensor_copy(out=Rbf[:], in_=Rst[:]),
                   reads=[f"Rst{h}" for h in range(NH)], writes=["Rbf"], tag=f"rbf_{I}")

            for h in range(NH):
                sch.op("vector", lambda h=h: nc.vector.bn_stats(out=st6g[:, h, :], in_=PS[:, ycol(h):ycol(h) + 128]),
                       writes=[f"st6g{h}", "st6g7"] if h == NH - 1 else [f"st6g{h}"],
                       pr=[(ycol(h), ycol(h) + 128)], tag=f"bng_{I}_{h}")
            allst = [f"st6g{h}" for h in range(NH)]
            sch.op("vector", lambda: nc.vector.tensor_tensor(out=gtmp[:, 0, :], in0=st6g[:, :, 1],
                                                             in1=st6g[:, :, 4], op=ALU.add),
                   reads=allst, writes=["gtA"], tag=f"gnc1_{I}")
            sch.op("vector", lambda: nc.vector.tensor_tensor(out=gtmp[:, 1, :], in0=st6g[:, :, 1],
                                                             in1=st6g[:, :, 4], op=ALU.subtract),
                   reads=allst, writes=["gtD"], tag=f"gnc2_{I}")
            sch.op("vector", lambda: nc.vector.tensor_tensor(out=gtmp[:, 2, :], in0=gtmp[:, 1, :],
                                                             in1=gtmp[:, 1, :], op=ALU.mult),
                   reads=["gtD"], writes=["gtDD"], tag=f"gnc3_{I}")
            sch.op("vector", lambda: nc.vector.tensor_tensor(out=gtmp[:, 3, :], in0=st6g[:, :, 2],
                                                             in1=st6g[:, :, 5], op=ALU.add),
                   reads=allst, writes=["gtS"], tag=f"gnc4_{I}")
            sch.op("vector", lambda: nc.vector.scalar_tensor_tensor(out=gtmp[:, 4, :], in0=gtmp[:, 3, :],
                                                                    scalar=4.0 / HD, in1=gtmp[:, 2, :],
                                                                    op0=ALU.mult, op1=ALU.add),
                   reads=["gtS", "gtDD"], writes=["gtV"], tag=f"gnc5_{I}")
            sch.op("scalar", lambda: nc.scalar.activation(out=rstdg[:], in_=gtmp[:, 4, :], func=AF.Ln,
                                                          bias=cst[:, 0:1], scale=cst[:, 2:3]),
                   reads=["gtV", "cst"], writes=["rstdg"], tag=f"rstdgA_{I}")
            sch.op("scalar", lambda: nc.scalar.activation(out=rstdg[:], in_=rstdg[:], func=AF.Exp, scale=cst[:, 1:2]),
                   reads=["cst", "rstdg"], writes=["rstdg"], tag=f"rstdgB_{I}")
            sch.op("vector", lambda: nc.vector.scalar_tensor_tensor(out=negmrg[:], in0=gtmp[:, 0, :], scalar=-0.5,
                                                                    in1=rstdg[:], op0=ALU.mult, op1=ALU.mult),
                   reads=["gtA", "rstdg"], writes=["negmrg"], tag=f"negmrg_{I}")

            for h in range(NH):
                sch.op("vector", lambda h=h, p=p: nc.vector.tensor_scalar(
                           out=scr[:, p, h * 128:(h + 1) * 128], in0=PS[:, ycol(h):ycol(h) + 128],
                           scalar1=rstdg[:, h:h + 1], scalar2=negmrg[:, h:h + 1], op0=ALU.mult, op1=ALU.add),
                       reads=["rstdg", "negmrg"] + ([f"sA{p}"] if h == 0 else []),
                       writes=[f"sA{p}" if h == 0 else f"sA{p}q{h}"],
                       pr=[(ycol(h), ycol(h) + 128)], tag=f"gnap_{I}_{h}")
            sch.op("scalar", lambda p=p: nc.scalar.activation(out=scr[:, p, 1024:2048], in_=PS[:, 3072:4096],
                                                              func=AF.Silu),
                   writes=[f"sB{p}"], pr=[(3072, 4096)], tag=f"silu_{I}")
            sch.op("gpsimd", lambda p=p: nc.gpsimd.tensor_tensor(out=scr[:, p, 2048:3072], in0=scr[:, p, 0:1024],
                                                                 in1=gnwbc[:], op=ALU.mult),
                   reads=[f"sA{p}", "gnwbc"] + [f"sA{p}q{h}" for h in range(1, NH)],
                   writes=[f"sC{p}"], tag=f"gnw_{I}")
            sch.op("gpsimd", lambda p=p: nc.gpsimd.tensor_tensor(out=scr[:, p, 0:1024], in0=scr[:, p, 2048:3072],
                                                                 in1=gnbbc[:], op=ALU.add),
                   reads=[f"sC{p}", "gnbbc"],
                   writes=[f"sA{p}"] + [f"sA{p}q{h}" for h in range(1, NH)], tag=f"gnb_{I}")
            sch.op("vector", lambda p=p: nc.vector.tensor_tensor(out=hbf[:], in0=scr[:, p, 0:1024],
                                                                 in1=scr[:, p, 1024:2048], op=ALU.mult),
                   reads=[f"sA{p}", f"sB{p}"], writes=["hbf"], tag=f"gate_{I}")
            tap("hbf", I, lambda: hbf[:], "hbf")

            sch.op("tensor", lambda: [nc.tensor.transpose(PSB[:, 2048 + h * 128:2176 + h * 128],
                                                          hbf[:, h, :], idn[:]) for h in range(NH)][-1],
                   reads=["hbf", "idn"], pw=[(1024 + h * 64, 1088 + h * 64) for h in range(NH)], tag=f"hT_{I}")
            sch.op("vector", lambda: nc.vector.tensor_copy(out=hfm[:], in_=PSB[:, 2048:3072]),
                   writes=["hfm"], pr=[(1024, 1536)], tag=f"evh_{I}")
            for hb in range(NH):
                sch.op("tensor", lambda hb=hb: (nc.tensor.matmul(PS[:, 0:512], hfm[:, hb, :], wo[:, hb, 0:512],
                                                                 start=(hb == 0), stop=(hb == NH - 1)),
                                                nc.tensor.matmul(PS[:, 512:1024], hfm[:, hb, :], wo[:, hb, 512:1024],
                                                                 start=(hb == 0), stop=(hb == NH - 1)))[-1],
                       reads=["hfm", "wo"], pw=[(0, 1024)], tag=f"wo_{I}_{hb}")

            emit_ln_stats(I, p, lambda: PS[:, 0:1024], [], [(0, 1024)], 1, "2")
            sch.op("vector", lambda p=p: nc.vector.tensor_scalar(out=scr[:, p, 2048:3072], in0=PS[:, 0:1024],
                                                                 scalar1=rstd[:, 1:2], scalar2=negmr[:, 1:2],
                                                                 op0=ALU.mult, op1=ALU.add),
                   reads=["rstd1", "negmr1"], writes=[f"sC{p}"], pr=[(0, 1024)], tag=f"ln2ap_{I}")
            sch.op("gpsimd", lambda p=p: nc.gpsimd.tensor_tensor(out=x1[p][:], in0=scr[:, p, 2048:3072],
                                                                 in1=x0[p][:], op=ALU.add),
                   reads=[f"sC{p}", f"x0{p}"], writes=[f"x1{p}"], tag=f"resid1_{I}")
            tap("x1", I, lambda p=p: x1[p][:], f"x1{p}")

            emit_ln_stats(I, p, lambda p=p: x1[p][:], [f"x1{p}"], [], 2, "3")
            sch.op("vector", lambda p=p: nc.vector.tensor_scalar(out=xn2tm[:], in0=x1[p][:],
                                                                 scalar1=rstd[:, 2:3], scalar2=negmr[:, 2:3],
                                                                 op0=ALU.mult, op1=ALU.add),
                   reads=[f"x1{p}", "rstd2", "negmr2"], writes=["xntm"], tag=f"ln3ap_{I}")
            sch.op("tensor", lambda: [nc.tensor.transpose(PSB[:, 3072 + h * 128:3200 + h * 128],
                                                          xn2tm[:, h * 128:(h + 1) * 128], idn[:])
                                      for h in range(NH)][-1],
                   reads=["xntm", "idn"], pw=[(1536 + h * 64, 1600 + h * 64) for h in range(NH)], tag=f"xn2T_{I}")
            sch.op("vector", lambda: nc.vector.tensor_copy(out=xn2fm[:], in_=PSB[:, 3072:4096]),
                   writes=["xn2fm"], pr=[(1536, 2048)], tag=f"evxn2_{I}")
            tap("xn2fm", I, lambda: xn2fm[:], "xn2fm")

            for k in range(8):
                base = 1024 + (k % 2) * 512
                sch.op("tensor", lambda k=k, base=base: (
                           nc.tensor.matmul(PS[:, base:base + 128], u1[:, k, 0:128], xn2fm[:, k, :],
                                            start=(k < 2), stop=False),
                           nc.tensor.matmul(PS[:, base + 128:base + 256], u1[:, k, 128:256], xn2fm[:, k, :],
                                            start=False, stop=(k >= 6)))[-1],
                       reads=["xn2fm", "u1"], pw=[(base, base + 256)], tag=f"p_{I}_{k}")
            sch.op("vector", lambda: nc.vector.tensor_copy(out=qftmp[:], in_=PS[:, 1024:1280]),
                   writes=["qftmp"], pr=[(1024, 1280)], tag=f"pev1_{I}")
            sch.op("vector", lambda: nc.vector.tensor_tensor(out=pbf[:], in0=qftmp[:],
                                                             in1=PS[:, 1536:1792], op=ALU.add),
                   reads=["qftmp"], writes=["pbf"], pr=[(1536, 1792)], tag=f"pev2_{I}")
            tap("pbf", I, lambda: pbf[:], "pbf")

            qf_cols = (0, 512, 1024, 1536)
            for half in range(2):
                fbs = list(range(half * 16, half * 16 + 16))
                for rc in range(2):
                    sch.op("tensor", lambda rc=rc, fbs=fbs: [
                               nc.tensor.matmul(PS[:, 2048 + (fb % 16) * 128:2176 + (fb % 16) * 128],
                                                v1[:, rc, fb * 128:(fb + 1) * 128], pbf[:, rc, :],
                                                start=(rc == 0 and fb % 4 == 0),
                                                stop=False) for fb in fbs][-1],
                           reads=["pbf", "v1"], pw=[(2048, 4096)], tag=f"h_{I}_{half}_{rc}")
                sch.op("tensor", lambda fbs=fbs: [
                           nc.tensor.matmul(PS[:, 2048 + (fb % 16) * 128:2176 + (fb % 16) * 128],
                                            b1r[0:1, fb * 128:(fb + 1) * 128], onesb[:],
                                            start=False, stop=(fb % 4 == 3)) for fb in fbs][-1],
                       reads=["b1r", "ones"], pw=[(2048, 4096)], tag=f"hb1_{I}_{half}")
                sch.op("scalar", lambda: nc.scalar.activation(out=Hbf[:], in_=PS[:, 2048:4096],
                                                              func=AF.Gelu_apprx_tanh),
                       writes=["Hbf"], pr=[(2048, 4096)], tag=f"gelu_{I}_{half}")
                for fb in fbs:
                    c0 = qf_cols[fb % 4]
                    sch.op("tensor", lambda fb=fb, c0=c0: (
                               nc.tensor.matmul(PS[:, c0:c0 + 128], u2[:, fb, 0:128], Hbf[:, fb % 16, :],
                                                start=(fb < 4), stop=False),
                               nc.tensor.matmul(PS[:, c0 + 128:c0 + 256], u2[:, fb, 128:256], Hbf[:, fb % 16, :],
                                                start=False, stop=(fb >= 28)))[-1],
                           reads=["Hbf", "u2"], pw=[(c0, c0 + 256)], tag=f"u2_{I}_{fb}")

            sch.op("vector", lambda: nc.vector.tensor_copy(out=qftmp[:], in_=PS[:, 0:256]),
                   writes=["qftmp"], pr=[(0, 256)], tag=f"qfc1_{I}")
            sch.op("vector", lambda: nc.vector.tensor_tensor(out=qftmp[:], in0=qftmp[:], in1=PS[:, 512:768], op=ALU.add),
                   reads=["qftmp"], writes=["qftmp"], pr=[(512, 768)], tag=f"qfc2_{I}")
            sch.op("vector", lambda: nc.vector.tensor_tensor(out=qftmp[:], in0=qftmp[:], in1=PS[:, 1024:1280], op=ALU.add),
                   reads=["qftmp"], writes=["qftmp"], pr=[(1024, 1280)], tag=f"qfc3_{I}")
            sch.op("vector", lambda: nc.vector.tensor_tensor(out=qfbf[:], in0=qftmp[:], in1=PS[:, 1536:1792], op=ALU.add),
                   reads=["qftmp"], writes=["qfbf"], pr=[(1536, 1792)], tag=f"qfc4_{I}")
            tap("qfbf", I, lambda: qfbf[:], "qfbf")

            sch.op("tensor", lambda: (nc.tensor.matmul(PS[:, 3072:3584], qfbf[:, 0, :], v2[:, 0, 0:512],
                                                       start=True, stop=False),
                                      nc.tensor.matmul(PS[:, 3584:4096], qfbf[:, 0, :], v2[:, 0, 512:1024],
                                                       start=True, stop=False))[-1],
                   reads=["qfbf", "v2"], pw=[(3072, 4096)], tag=f"v2a_{I}")
            sch.op("tensor", lambda: (nc.tensor.matmul(PS[:, 3072:3584], qfbf[:, 1, :], v2[:, 1, 0:512],
                                                       start=False, stop=False),
                                      nc.tensor.matmul(PS[:, 3584:4096], qfbf[:, 1, :], v2[:, 1, 512:1024],
                                                       start=False, stop=False))[-1],
                   reads=["qfbf", "v2"], pw=[(3072, 4096)], tag=f"v2b_{I}")
            sch.op("tensor", lambda: (nc.tensor.matmul(PS[:, 3072:3584], onesb[:], b2r[0:1, 0:512],
                                                       start=False, stop=True),
                                      nc.tensor.matmul(PS[:, 3584:4096], onesb[:], b2r[0:1, 512:1024],
                                                       start=False, stop=True))[-1],
                   reads=["ones", "b2r"], pw=[(3072, 4096)], tag=f"b2_{I}")

            emit_ln_stats(I, p, lambda: PS[:, 3072:4096], [], [(3072, 4096)], 3, "4")
            sch.op("vector", lambda p=p: nc.vector.tensor_scalar(out=scr[:, p, 2048:3072], in0=PS[:, 3072:4096],
                                                                 scalar1=rstd[:, 3:4], scalar2=negmr[:, 3:4],
                                                                 op0=ALU.mult, op1=ALU.add),
                   reads=["rstd3", "negmr3"], writes=[f"sC{p}"], pr=[(3072, 4096)], tag=f"ln4ap_{I}")
            sch.op("gpsimd", lambda p=p: nc.gpsimd.tensor_tensor(out=outb[:], in0=scr[:, p, 2048:3072],
                                                                 in1=x1[p][:], op=ALU.add),
                   reads=[f"sC{p}", f"x1{p}"], writes=["out"], tag=f"resid2_{I}")
            sch.op("scalar", lambda I=I: nc.scalar.dma_start(out=y_h[I * C:(I + 1) * C, :], in_=outb[:]),
                   reads=["out"], dma=True, tag=f"store_{I}")

        for I in range(NCH):
            emit_front(I)
            if I > 0:
                emit_midtail(I - 1)
            if I + 1 < NCH:
                emit_loads(I + 1)
        emit_midtail(NCH - 1)

        block = E(nc.Block())
        sch.run(block)

    return nc, dbg_names


def kernel(L0_attn_X, L0_attn_W_Q, L0_attn_W_K, L0_attn_W_V, L0_attn_W_G, L0_attn_W_O,
           L0_attn_gn_weight, L0_attn_gn_bias, L0_ffn_U1, L0_ffn_V1, L0_ffn_b1,
           L0_ffn_U2, L0_ffn_V2, L0_ffn_b2):
    consts, gc_list = host_constants()
    nc, _ = build_nc(gc_list)
    common = dict(
        wq=np.ascontiguousarray(L0_attn_W_Q, np.float32),
        wk=np.ascontiguousarray(L0_attn_W_K, np.float32),
        wv=np.ascontiguousarray(L0_attn_W_V, np.float32),
        wg=np.ascontiguousarray(L0_attn_W_G, np.float32),
        wo=np.ascontiguousarray(L0_attn_W_O, np.float32),
        gnw=np.ascontiguousarray(L0_attn_gn_weight, np.float32),
        gnb=np.ascontiguousarray(L0_attn_gn_bias, np.float32),
        u1=np.ascontiguousarray(L0_ffn_U1, np.float32),
        v1=np.ascontiguousarray(L0_ffn_V1, np.float32),
        b1=np.ascontiguousarray(L0_ffn_b1, np.float32),
        u2=np.ascontiguousarray(L0_ffn_U2, np.float32),
        v2=np.ascontiguousarray(L0_ffn_V2, np.float32),
        b2=np.ascontiguousarray(L0_ffn_b2, np.float32),
        **consts,
    )
    X = np.ascontiguousarray(L0_attn_X, np.float32)
    in_maps = [dict(common, x=np.ascontiguousarray(X[c])) for c in range(8)]
    res = run_bass_kernel_spmd(nc, in_maps, core_ids=list(range(8)))
    return np.stack([np.asarray(r["y"], np.float32) for r in res.results], axis=0)


# version 07
# speedup vs baseline: 1.3408x; 1.1823x over previous
import math
from contextlib import ExitStack

import numpy as np
import ml_dtypes

import concourse.bass as bass
import concourse.mybir as mybir
from concourse.bass_utils import run_bass_kernel_spmd

F32 = mybir.dt.float32
BF16 = mybir.dt.bfloat16
AF = mybir.ActivationFunctionType
ALU = mybir.AluOpType

S, D, NH, HD, C = 4096, 1024, 8, 128, 128
NCH = S // C
EPS = 1e-5


def host_constants():
    gam = 1.0 - np.exp(np.linspace(math.log(1.0 / 32.0), math.log(1.0 / 512.0), NH))
    ii = np.arange(C)
    diff = ii[None, :] - ii[:, None]
    dloc = np.where(diff[:, :, None] >= 0,
                    gam[None, None, :] ** np.maximum(diff, 0)[:, :, None], 0.0)
    DLOC = np.ascontiguousarray(dloc.transpose(0, 2, 1)).astype(np.float32)
    GQ = np.ascontiguousarray(np.broadcast_to((gam[:, None] ** (ii[None, :] + 1))[None],
                                              (128, NH, C))).astype(np.float32)
    GK = np.ascontiguousarray(np.broadcast_to((gam[:, None] ** (C - 1 - ii[None, :]))[None],
                                              (128, NH, C))).astype(np.float32)
    GC = [float(np.float32(gam[h] ** C)) for h in range(NH)]

    half = HD // 2
    scale = (np.arange(0, HD, 2, dtype=np.float64) + 0.4 * HD) / (1.4 * HD)
    pos = np.arange(S, dtype=np.float64)
    sc = scale[None, :] ** (pos[:, None] / 512.0)
    inv_freq = 1.0 / 10000.0 ** (np.arange(half, dtype=np.float64) / half)
    sinus = pos[:, None] * inv_freq[None, :]
    sin, cos = np.sin(sinus), np.cos(sinus)

    def rep2(a):
        return np.repeat(a, 2, axis=-1)

    cosQ, sinQ = rep2(cos * sc), rep2(sin * sc)
    cosK, sinK = rep2(cos / sc), rep2(sin / sc)
    TAB = np.empty((NCH, 128, 2, 2, C), np.float32)
    for ci in range(NCH):
        sl = slice(ci * C, (ci + 1) * C)
        TAB[ci, :, 0, 0, :] = cosQ[sl].T
        TAB[ci, :, 1, 0, :] = sinQ[sl].T
        TAB[ci, :, 0, 1, :] = cosK[sl].T
        TAB[ci, :, 1, 1, :] = sinK[sl].T

    P = np.zeros((128, 128), np.float64)
    for m in range(64):
        P[2 * m, 2 * m + 1] = -1.0
        P[2 * m + 1, 2 * m] = 1.0
    PT = np.ascontiguousarray(P.T).astype(ml_dtypes.bfloat16)
    IDN = np.eye(128, dtype=ml_dtypes.bfloat16)
    IDNF = np.eye(128, dtype=np.float32)
    ONES = np.ones((1, 128), ml_dtypes.bfloat16)
    return dict(TAB=TAB, DLOC=DLOC, GQ=GQ, GK=GK, PT=PT, IDN=IDN, IDNF=IDNF, ONES=ONES), GC


class Sched:

    ENG_COST = {"tensor": 800, "vector": 1100, "scalar": 900, "gpsimd": 2200, "sync": 3000}
    SEM_LAT = 250.0

    def __init__(self, nc, sems):
        self.nc = nc
        self.sems = sems
        self.ops = []
        self.sb_w = {}
        self.sb_r = {}
        self.ps_w_bytes = []
        self.bank_reads = {}
        self.g_open = {}
        self.g_close = {}
        self.g_members = {}
        self.g_state = {}

    @staticmethod
    def banks(lo, hi):
        return range(lo // 512, (hi + 511) // 512)

    def op(self, eng, fn, reads=(), writes=(), pr=(), pw=(), dma=False, tag="", cost=None, grp="oc"):
        idx = len(self.ops)
        deps = set()

        def need(j):
            if j is not None and j != idx:
                deps.add(j)

        for b in reads:
            need(self.sb_w.get(b))
        for b in writes:
            need(self.sb_w.get(b))
            for j in self.sb_r.get(b, ()):
                need(j)
        for (lo, hi) in pr:
            for bk in self.banks(lo, hi):
                need(self.g_close.get(bk))
                if self.g_state.get(bk) == "open":
                    for m in self.g_members.get(bk, ()):
                        need(m)
                for (j, e2) in self.bank_reads.get(bk, ()):
                    if e2 != eng:
                        need(j)
        opbanks = set()
        for (lo, hi) in pw:
            assert eng == "tensor", f"only PE writes PSUM (op {tag})"
            for (l2, h2, j) in self.ps_w_bytes:
                if l2 < hi and lo < h2:
                    need(j)
            for bk in self.banks(lo, hi):
                opbanks.add(bk)
        for bk in opbanks:
            if grp in ("o", "oc"):
                need(self.g_close.get(bk))
                for (j, e2) in self.bank_reads.get(bk, ()):
                    need(j)
                self.bank_reads[bk] = []
                self.g_open[bk] = idx
                self.g_members[bk] = [idx]
                self.g_state[bk] = "open"
            else:
                need(self.g_open.get(bk))
                for (j, e2) in self.bank_reads.get(bk, ()):
                    need(j)
                self.bank_reads[bk] = []
                self.g_members.setdefault(bk, []).append(idx)
            if grp in ("c", "oc"):
                for m in self.g_members.get(bk, ()):
                    need(m)
                self.g_close[bk] = idx
                self.g_state[bk] = "closed"

        for b in reads:
            self.sb_r.setdefault(b, []).append(idx)
        for b in writes:
            self.sb_w[b] = idx
            self.sb_r[b] = []
        for (lo, hi) in pr:
            for bk in self.banks(lo, hi):
                self.bank_reads.setdefault(bk, []).append((idx, eng))
        for (lo, hi) in pw:
            self.ps_w_bytes = [e for e in self.ps_w_bytes if not (lo <= e[0] and e[1] <= hi)]
            self.ps_w_bytes.append((lo, hi, idx))

        self.ops.append(dict(idx=idx, eng=eng, fn=fn, dma=dma, tag=tag,
                             cost=float(cost if cost is not None else self.ENG_COST[eng]),
                             deps=deps))
        return idx

    def _order(self):
        import heapq
        n = len(self.ops)
        children = [[] for _ in range(n)]
        indeg = [0] * n
        for o in self.ops:
            for d in o["deps"]:
                children[d].append(o["idx"])
                indeg[o["idx"]] += 1
        uprank = [0.0] * n
        for i in range(n - 1, -1, -1):
            c = self.ops[i]["cost"]
            m = 0.0
            for ch in children[i]:
                if uprank[ch] > m:
                    m = uprank[ch]
            uprank[i] = c + m
        rdy = [0.0] * n
        eng_avail = {}
        heap = []
        for i in range(n):
            if indeg[i] == 0:
                heapq.heappush(heap, (0.0, -uprank[i], i))
        order = []
        while heap:
            cand = []
            best = None
            while heap:
                t, nr, i = heapq.heappop(heap)
                eng = self.ops[i]["eng"]
                est = max(t, eng_avail.get(eng, 0.0))
                cand.append((est, nr, i, t))
                if best is None or (est, nr) < best[:2]:
                    best = (est, nr, i, t)
                if len(cand) >= 24:
                    break
            for c in cand:
                if c[2] != best[2]:
                    heapq.heappush(heap, (c[3], c[1], c[2]))
            est, _, i, _ = best
            o = self.ops[i]
            f = est + o["cost"]
            eng_avail[o["eng"]] = f
            o["finish"] = f
            order.append(i)
            for ch in children[i]:
                lat = self.SEM_LAT if self.ops[ch]["eng"] != o["eng"] else 100.0
                r = f + lat
                if r > rdy[ch]:
                    rdy[ch] = r
                indeg[ch] -= 1
                if indeg[ch] == 0:
                    heapq.heappush(heap, (rdy[ch], -uprank[ch], ch))
        assert len(order) == n, f"cycle: scheduled {len(order)} of {n}"
        return order

    def run(self, block):
        order = self._order()
        cnt = {k: 0 for k in self.sems}
        tok = {}
        sem_of = {}
        last_dma = {}
        waits_per = {i: [] for i in range(len(self.ops))}
        waited = {}
        for i in order:
            o = self.ops[i]
            sem = ("d_" if o["dma"] else "m_") + o["eng"]
            deps = {}
            for d in o["deps"]:
                s, v = sem_of[d], tok[d]
                if v > deps.get(s, 0):
                    deps[s] = v
            if o["dma"] and cnt[sem] > 0:
                deps[sem] = max(deps.get(sem, 0), cnt[sem])
            for s, v in sorted(deps.items()):
                if waited.get((o["eng"], s), 0) < v:
                    waits_per[i].append((s, v))
                    waited[(o["eng"], s)] = v
            cnt[sem] += 16 if o["dma"] else 1
            tok[i] = cnt[sem]
            sem_of[i] = sem
        self.cnt = cnt
        per_eng = {e: [i for i in order if self.ops[i]["eng"] == e]
                   for e in ("sync", "scalar", "tensor", "vector", "gpsimd")}

        for eng in ("sync", "scalar", "tensor", "vector", "gpsimd"):
            idxs = per_eng[eng]

            def make(idxs=idxs, eng=eng):
                def go(e):
                    for i in idxs:
                        o = self.ops[i]
                        for (s, v) in waits_per[i]:
                            e.wait_ge(self.sems[s], v)
                        inst = o["fn"]()
                        inst.then_inc(self.sems[("d_" if o["dma"] else "m_") + eng], 16 if o["dma"] else 1)
                    if eng == "scalar":
                        e.wait_ge(self.sems["d_scalar"], self.cnt["d_scalar"])
                return go

            getattr(block, eng)(make())


def build_nc(gc_list, dbg=None):
    dbg = dbg or set()
    nc = bass.Bass("TRN2", target_bir_lowering=False)

    def din(name, shape, dtype=F32):
        return nc.dram_tensor(name, shape, dtype, kind="ExternalInput").ap()

    x_h = din("x", [S, D])
    wq_h = din("wq", [NH, HD, HD]); wk_h = din("wk", [NH, HD, HD]); wv_h = din("wv", [NH, HD, HD])
    wg_h = din("wg", [D, D]); wo_h = din("wo", [D, D])
    gnw_h = din("gnw", [D]); gnb_h = din("gnb", [D])
    u1_h = din("u1", [D, 256]); v1_h = din("v1", [256, 4096]); b1_h = din("b1", [4096])
    u2_h = din("u2", [4096, 256]); v2_h = din("v2", [256, D]); b2_h = din("b2", [D])
    tab_h = din("TAB", [NCH, 128, 2, 2, C])
    dloc_h = din("DLOC", [128, NH, C]); gq_h = din("GQ", [128, NH, C]); gk_h = din("GK", [128, NH, C])
    pt_h = din("PT", [128, 128], BF16); idn_h = din("IDN", [128, 128], BF16)
    ones_h = din("ONES", [1, 128], BF16)
    y_h = nc.dram_tensor("y", [S, D], F32, kind="ExternalOutput").ap()

    dbg_names = []

    with ExitStack() as st:
        E = st.enter_context
        wq = E(nc.sbuf_tensor("wq_s", [128, NH, HD], BF16))
        wk = E(nc.sbuf_tensor("wk_s", [128, NH, HD], BF16))
        wv = E(nc.sbuf_tensor("wv_s", [128, NH, HD], BF16))
        wqr = E(nc.sbuf_tensor("wqr_s", [128, NH, HD], BF16))
        wkr = E(nc.sbuf_tensor("wkr_s", [128, NH, HD], BF16))
        wg = E(nc.sbuf_tensor("wg_s", [128, 8, D], BF16))
        wo = E(nc.sbuf_tensor("wo_s", [128, 8, D], BF16))
        u1 = E(nc.sbuf_tensor("u1_s", [128, 8, 256], BF16))
        v1 = E(nc.sbuf_tensor("v1_s", [128, 2, 4096], BF16))
        u2 = E(nc.sbuf_tensor("u2_s", [128, 32, 256], BF16))
        v2 = E(nc.sbuf_tensor("v2_s", [128, 2, D], BF16))
        b2r = E(nc.sbuf_tensor("b2r_s", [1, D], BF16))
        onesb = E(nc.sbuf_tensor("ones_s", [1, 128], BF16))
        gnwbc = E(nc.sbuf_tensor("gnwbc", [128, D], F32))
        gnbbc = E(nc.sbuf_tensor("gnbbc", [128, D], F32))
        b1r = E(nc.sbuf_tensor("b1r_s", [1, 4096], BF16))
        dloc = E(nc.sbuf_tensor("dloc", [128, NH, C], F32))
        gq = E(nc.sbuf_tensor("gq", [128, NH, C], F32))
        gk = E(nc.sbuf_tensor("gk", [128, NH, C], F32))
        idn = E(nc.sbuf_tensor("idn", [128, 128], BF16))
        pt = E(nc.sbuf_tensor("pt", [128, 128], BF16))
        wtt = E(nc.sbuf_tensor("wtt", [128, 128], BF16))

        x0 = [E(nc.sbuf_tensor(f"x0_{p}", [128, D], F32)) for p in range(2)]
        tabs = [E(nc.sbuf_tensor(f"tab_{p}", [128, 2, 2, C], F32)) for p in range(2)]
        xntm = E(nc.sbuf_tensor("xntm", [128, D], BF16))
        xnfm = [E(nc.sbuf_tensor(f"xnfm_{pp}", [128, NH, C], BF16)) for pp in range(2)]
        qkbf = E(nc.sbuf_tensor("qkbf", [128, NH, 2, C], BF16))
        qtb = [E(nc.sbuf_tensor(f"qtb_{pp}", [128, NH, C], BF16)) for pp in range(2)]
        ktb = E(nc.sbuf_tensor("ktb", [128, NH, C], BF16))
        kttm = [E(nc.sbuf_tensor(f"kttm_{pp}", [128, NH, C], BF16)) for pp in range(2)]
        vbf = [E(nc.sbuf_tensor(f"vbf_{pp}", [128, NH, C], BF16)) for pp in range(2)]
        stb = [E(nc.sbuf_tensor(f"stb_{pp}", [128, NH, C], BF16)) for pp in range(2)]
        Rst = E(nc.sbuf_tensor("Rst", [128, NH, HD], F32))
        Rbf = E(nc.sbuf_tensor("Rbf", [128, NH, HD], BF16))
        scr = E(nc.sbuf_tensor("scr", [128, 2, 3072], F32))
        hbf = E(nc.sbuf_tensor("hbf", [128, NH, C], BF16))
        hfm = E(nc.sbuf_tensor("hfm", [128, NH, C], BF16))
        x1 = [E(nc.sbuf_tensor(f"x1_{p}", [128, D], F32)) for p in range(2)]
        xn2tm = xntm
        xn2fm = E(nc.sbuf_tensor("xn2fm", [128, NH, C], BF16))
        pbf = E(nc.sbuf_tensor("pbf", [128, 2, 128], BF16))
        Hbf = E(nc.sbuf_tensor("Hbf", [128, 16, 128], BF16))
        qftmp = E(nc.sbuf_tensor("qftmp", [128, 256], F32))
        qfbf = E(nc.sbuf_tensor("qfbf", [128, 2, 128], BF16))
        outb = E(nc.sbuf_tensor("out_s", [128, D], F32))
        st6 = [E(nc.sbuf_tensor(f"st6_{p}", [128, 2, 6], F32)) for p in range(2)]
        st2 = [E(nc.sbuf_tensor(f"st2_{p}", [128, 2], F32)) for p in range(2)]
        st6g = E(nc.sbuf_tensor("st6g", [128, NH, 6], F32))
        gtmp = E(nc.sbuf_tensor("gtmp", [128, 5, NH], F32))
        rstd = E(nc.sbuf_tensor("rstd", [128, 8], F32))
        negmr = E(nc.sbuf_tensor("negmr", [128, 4], F32))
        rstdg = E(nc.sbuf_tensor("rstdg", [128, NH], F32))
        cst = E(nc.sbuf_tensor("cst", [128, 4], F32))
        lnsum = E(nc.sbuf_tensor("lnsum", [128, 4], F32))
        lnsq = E(nc.sbuf_tensor("lnsq", [128, 4], F32))
        lntmp = E(nc.sbuf_tensor("lntmp", [128, 4], F32))
        negmrg = E(nc.sbuf_tensor("negmrg", [128, NH], F32))

        PS = E(nc.psum_tensor("PS", [128, 4096], F32))
        PSB = PS[:].bitcast(BF16)

        sems = {}
        for name in ("m_tensor", "m_vector", "m_scalar", "m_gpsimd", "d_sync", "d_scalar", "d_gpsimd"):
            sems[name] = E(nc.semaphore(name))
        sch = Sched(nc, sems)

        def tap(stage, I, ap_fn, bufname):
            if (stage, I) in dbg:
                ap = ap_fn()
                nm = f"dbg_{stage}_{I}"
                t = nc.dram_tensor(nm, list(ap.shape), ap.dtype, kind="ExternalOutput").ap()
                dbg_names.append(nm)
                names = [bufname] if isinstance(bufname, str) else list(bufname)
                sch.op("sync", lambda ap=ap, t=t: nc.sync.dma_start(out=t, in_=ap),
                       reads=names, dma=True, tag=f"tap_{stage}_{I}")

        for fn, wr, tg in (
            (lambda: nc.gpsimd.dma_start(out=wq[:], in_=wq_h.rearrange("h d e -> d h e")), ["wq"], "ld_wq"),
            (lambda: nc.gpsimd.dma_start(out=wk[:], in_=wk_h.rearrange("h d e -> d h e")), ["wk"], "ld_wk"),
            (lambda: nc.gpsimd.dma_start(out=wv[:], in_=wv_h.rearrange("h d e -> d h e")), ["wv"], "ld_wv"),
            (lambda: nc.gpsimd.dma_start(out=wg[:], in_=wg_h.rearrange("(b p) o -> p b o", p=128)), ["wg"], "ld_wg"),
            (lambda: nc.gpsimd.dma_start(out=wo[:], in_=wo_h.rearrange("(b p) o -> p b o", p=128)), ["wo"], "ld_wo"),
            (lambda: nc.gpsimd.dma_start(out=u1[:], in_=u1_h.rearrange("(b p) r -> p b r", p=128)), ["u1"], "ld_u1"),
            (lambda: nc.gpsimd.dma_start(out=v1[:], in_=v1_h.rearrange("(b p) f -> p b f", p=128)), ["v1"], "ld_v1"),
            (lambda: nc.gpsimd.dma_start(out=u2[:], in_=u2_h.rearrange("(b p) r -> p b r", p=128)), ["u2"], "ld_u2"),
            (lambda: nc.gpsimd.dma_start(out=v2[:], in_=v2_h.rearrange("(b p) o -> p b o", p=128)), ["v2"], "ld_v2"),
            (lambda: nc.gpsimd.dma_start(out=b1r[:], in_=b1_h.unsqueeze(0)), ["b1r"], "ld_b1"),
            (lambda: nc.gpsimd.dma_start(out=b2r[:], in_=b2_h.unsqueeze(0)), ["b2r"], "ld_b2"),
            (lambda: nc.gpsimd.dma_start(out=onesb[:], in_=ones_h), ["ones"], "ld_ones"),
        ):
            sch.op("gpsimd", fn, writes=wr, dma=True, tag=tg)

        for fn, wr, tg in (
            (lambda: nc.sync.dma_start(out=gnwbc[:], in_=gnw_h.unsqueeze(0).broadcast_to([128, D])), ["gnwbc"], "ld_gnw"),
            (lambda: nc.sync.dma_start(out=gnbbc[:], in_=gnb_h.unsqueeze(0).broadcast_to([128, D])), ["gnbbc"], "ld_gnb"),
            (lambda: nc.sync.dma_start(out=dloc[:], in_=dloc_h), ["dloc"], "ld_dloc"),
            (lambda: nc.sync.dma_start(out=gq[:], in_=gq_h), ["gq"], "ld_gq"),
            (lambda: nc.sync.dma_start(out=gk[:], in_=gk_h), ["gk"], "ld_gk"),
            (lambda: nc.sync.dma_start(out=idn[:], in_=idn_h), ["idn"], "ld_idn"),
            (lambda: nc.sync.dma_start(out=pt[:], in_=pt_h), ["pt"], "ld_pt"),
        ):
            sch.op("sync", fn, writes=wr, dma=True, tag=tg)

        for (src, dst, sname, dname) in ((wq, wqr, "wq", "wqr"), (wk, wkr, "wk", "wkr")):
            for h in range(NH):
                sch.op("tensor", lambda src=src, h=h: nc.tensor.transpose(PSB[:, 0:128], src[:, h, :], idn[:]),
                       reads=[sname, "idn"], pw=[(0, 64)], tag=f"T_{dname}{h}")
                sch.op("vector", lambda: nc.vector.tensor_copy(out=wtt[:], in_=PSB[:, 0:128]),
                       writes=["wtt"], pr=[(0, 64)], tag=f"ev_{dname}{h}")
                sch.op("tensor", lambda: nc.tensor.matmul(PS[:, 512:640], wtt[:], pt[:], start=True, stop=True),
                       reads=["wtt", "pt"], pw=[(512, 640)], tag=f"mm_{dname}{h}")
                sch.op("vector", lambda dst=dst, h=h: nc.vector.tensor_copy(out=dst[:, h, :], in_=PS[:, 512:640]),
                       writes=[dname], pr=[(512, 640)], tag=f"ev2_{dname}{h}")

        tap("wqr", 0, lambda: wqr[:], "wqr")
        tap("wkr", 0, lambda: wkr[:], "wkr")
        sch.op("vector", lambda: nc.vector.memset(Rst[:], 0.0),
               writes=[f"Rst{h}" for h in range(NH)], tag="memset_R")
        sch.op("vector", lambda: (nc.vector.memset(cst[:, 0:1], EPS),
                                  nc.vector.memset(cst[:, 1:2], -0.5),
                                  nc.vector.memset(cst[:, 2:3], 0.25),
                                  nc.vector.memset(cst[:, 3:4], 1.0 / D))[-1],
               writes=["cst"], tag="memset_cst")

        def emit_loads(I):
            p = I % 2
            sch.op("sync", lambda I=I, p=p: nc.sync.dma_start(out=x0[p][:], in_=x_h[I * C:(I + 1) * C, :]),
                   writes=[f"x0{p}"], dma=True, tag=f"ld_x0_{I}", cost=3500)
            sch.op("sync", lambda I=I, p=p: nc.sync.dma_start(out=tabs[p][:], in_=tab_h[I]),
                   writes=[f"tab{p}"], dma=True, tag=f"ld_tab_{I}", cost=2800)

        emit_loads(0)

        def sacol(h):
            return 1024 + (h % 2) * 512 + (h // 2) * 128

        def ycol(h):
            return 2048 + (h % 2) * 512 + (h // 2) * 128

        def emit_ln_stats(I, p, srcfn, src_reads, src_pr, idx, tagp):
            sch.op("scalar", lambda: nc.scalar.activation(out=scr[:, p, 1024:2048], in_=srcfn(),
                                                          func=AF.Square, accum_out=lnsq[:, idx:idx + 1]),
                   reads=src_reads, writes=[f"sB{p}", f"lnsq{idx}"], pr=src_pr, tag=f"sq{tagp}_{I}", cost=1100)
            sch.op("vector", lambda: nc.vector.tensor_scalar(out=scr[:, p, 0:1024], in0=srcfn(),
                                                             scalar1=1.0, scalar2=None, op0=ALU.mult,
                                                             op1=ALU.add, accum_out=lnsum[:, idx:idx + 1]),
                   reads=src_reads, writes=[f"sA{p}", f"lnsum{idx}"], pr=src_pr, tag=f"sm{tagp}_{I}", cost=1300)
            sch.op("vector", lambda: nc.vector.tensor_tensor(out=lntmp[:, idx:idx + 1], in0=lnsum[:, idx:idx + 1],
                                                             in1=lnsum[:, idx:idx + 1], op=ALU.mult),
                   reads=[f"lnsum{idx}"], writes=[f"lntmp{idx}"], tag=f"varA{tagp}_{I}", cost=200)
            sch.op("vector", lambda: nc.vector.scalar_tensor_tensor(out=lntmp[:, idx:idx + 1],
                                                                    in0=lntmp[:, idx:idx + 1],
                                                                    scalar=-1.0 / D, in1=lnsq[:, idx:idx + 1],
                                                                    op0=ALU.mult, op1=ALU.add),
                   reads=[f"lntmp{idx}", f"lnsq{idx}"], writes=[f"lntmp{idx}"], tag=f"varB{tagp}_{I}", cost=200)
            sch.op("scalar", lambda: nc.scalar.activation(out=rstd[:, 4 + idx:5 + idx], in_=lntmp[:, idx:idx + 1],
                                                          func=AF.Ln, bias=cst[:, 0:1], scale=cst[:, 3:4]),
                   reads=[f"lntmp{idx}", "cst"], writes=[f"rstdL{idx}"], tag=f"rstdA{tagp}_{I}", cost=200)
            sch.op("scalar", lambda: nc.scalar.activation(out=rstd[:, idx:idx + 1], in_=rstd[:, 4 + idx:5 + idx],
                                                          func=AF.Exp, scale=cst[:, 1:2]),
                   reads=[f"rstdL{idx}", "cst"], writes=[f"rstd{idx}"], tag=f"rstdB{tagp}_{I}", cost=200)
            sch.op("vector", lambda: nc.vector.tensor_scalar(out=negmr[:, idx:idx + 1], in0=lnsum[:, idx:idx + 1],
                                                             scalar1=rstd[:, idx:idx + 1], scalar2=-1.0 / D,
                                                             op0=ALU.mult, op1=ALU.mult),
                   reads=[f"lnsum{idx}", f"rstd{idx}"], writes=[f"negmr{idx}"], tag=f"negmr{tagp}_{I}", cost=200)

        def emit_front(I):
            p = I % 2

            emit_ln_stats(I, p, lambda p=p: x0[p][:], [f"x0{p}"], [], 0, "1")
            sch.op("vector", lambda p=p: nc.vector.tensor_scalar(out=xntm[:], in0=x0[p][:],
                                                                 scalar1=rstd[:, 0:1], scalar2=negmr[:, 0:1],
                                                                 op0=ALU.mult, op1=ALU.add),
                   reads=[f"x0{p}", "rstd0", "negmr0"], writes=["xntm"], tag=f"ln1ap_{I}", cost=700)
            tap("xntm", I, lambda: xntm[:], "xntm")

            sch.op("tensor", lambda: [nc.tensor.transpose(PSB[:, 6144 + h * 128:6272 + h * 128],
                                                          xntm[:, h * 128:(h + 1) * 128], idn[:])
                                      for h in range(NH)][-1],
                   reads=["xntm", "idn"], pw=[(3072 + h * 64, 3136 + h * 64) for h in range(NH)], tag=f"xnT_{I}", cost=2600)
            sch.op("vector", lambda p=p: nc.vector.tensor_copy(out=xnfm[p][:], in_=PSB[:, 6144:7168]),
                   writes=[f"xnfm{p}"], pr=[(3072, 3584)], tag=f"evxn_{I}", cost=700)
            tap("xnfm", I, lambda p=p: xnfm[p][:], f"xnfm{p}")

            def qkmm(base, wa, wb, h, pp):
                c0 = base + h * 256
                nc.tensor.matmul(PS[:, c0:c0 + 128], wa[:, h, :], xnfm[pp][:, h, :],
                                 start=(h % 2 == 0), stop=False)
                return nc.tensor.matmul(PS[:, c0 + 128:c0 + 256], wb[:, h, :], xnfm[pp][:, h, :],
                                        start=False, stop=(h % 2 == 1))
            sch.op("tensor", lambda p=p, qkmm=qkmm: [qkmm(0, wq, wk, h, p) for h in range(NH)][-1],
                   reads=[f"xnfm{p}", "wq", "wk"], pw=[(0, 2048)], tag=f"qk_{I}", cost=3400)
            sch.op("tensor", lambda p=p, qkmm=qkmm: [qkmm(2048, wqr, wkr, h, p) for h in range(NH)][-1],
                   reads=[f"xnfm{p}", "wqr", "wkr"], pw=[(2048, 4096)], tag=f"qkr_{I}", cost=3400)

            for g in range(2):
                qv = PS[:, g * 1024:(g + 1) * 1024].rearrange("p (h s t) -> p h s t", h=4, s=2)
                rv = PS[:, 2048 + g * 1024:3072 + g * 1024].rearrange("p (h s t) -> p h s t", h=4, s=2)
                cosb = tabs[p][:, 0, :, :].unsqueeze(1).broadcast_to([128, 4, 2, C])
                sinb = tabs[p][:, 1, :, :].unsqueeze(1).broadcast_to([128, 4, 2, C])
                t1v = scr[:, p, 0:1024].rearrange("p (h s t) -> p h s t", h=4, s=2)
                t2v = scr[:, p, 1024:2048].rearrange("p (h s t) -> p h s t", h=4, s=2)
                sch.op("vector", lambda qv=qv, cosb=cosb, t1v=t1v:
                       nc.vector.tensor_tensor(out=t1v, in0=qv, in1=cosb, op=ALU.mult),
                       reads=[f"tab{p}"], writes=[f"sA{p}"], pr=[(g * 1024, (g + 1) * 1024)], tag=f"t1_{I}_{g}", cost=1300)
                sch.op("vector", lambda rv=rv, sinb=sinb, t2v=t2v:
                       nc.vector.tensor_tensor(out=t2v, in0=rv, in1=sinb, op=ALU.mult),
                       reads=[f"tab{p}"], writes=[f"sB{p}"], pr=[(2048 + g * 1024, 3072 + g * 1024)], tag=f"t2_{I}_{g}", cost=1300)
                sch.op("vector", lambda g=g, p=p: nc.vector.tensor_tensor(
                           out=qkbf[:, 4 * g:4 * g + 4, :, :],
                           in0=scr[:, p, 0:1024], in1=scr[:, p, 1024:2048], op=ALU.add),
                       reads=[f"sA{p}", f"sB{p}"], writes=["qkbf"], tag=f"qkadd_{I}_{g}", cost=1300)
            tap("qkbf", I, lambda: qkbf[:], "qkbf")

            sch.op("gpsimd", lambda p=p: nc.gpsimd.tensor_tensor(out=qtb[p][:], in0=qkbf[:, :, 0, :],
                                                                 in1=gq[:], op=ALU.mult),
                   reads=["qkbf", "gq"], writes=[f"qtb{p}"], tag=f"qt_{I}", cost=2300)
            sch.op("gpsimd", lambda: nc.gpsimd.tensor_tensor(out=ktb[:], in0=qkbf[:, :, 1, :],
                                                             in1=gk[:], op=ALU.mult),
                   reads=["qkbf", "gk"], writes=["ktb"], tag=f"kt_{I}", cost=2300)

            sch.op("tensor", lambda p=p: [nc.tensor.matmul(PS[:, h * 128:(h + 1) * 128], xnfm[p][:, h, :],
                                                           wv[:, h, :], start=True, stop=True)
                                          for h in range(NH)][-1],
                   reads=[f"xnfm{p}", "wv"], pw=[(0, 1024)], tag=f"v_{I}", cost=1400)
            sch.op("vector", lambda p=p: nc.vector.tensor_copy(out=vbf[p][:], in_=PS[:, 0:1024]),
                   writes=[f"vbf{p}"], pr=[(0, 1024)], tag=f"evv_{I}", cost=1300)
            tap("vbf", I, lambda p=p: vbf[p][:], f"vbf{p}")

            sch.op("tensor", lambda: [nc.tensor.transpose(PSB[:, 6144 + h * 128:6272 + h * 128],
                                                          ktb[:, h, :], idn[:]) for h in range(NH)][-1],
                   reads=["ktb", "idn"], pw=[(3072 + h * 64, 3136 + h * 64) for h in range(NH)], tag=f"ktT_{I}", cost=2600)
            sch.op("vector", lambda p=p: nc.vector.tensor_copy(out=kttm[p][:], in_=PSB[:, 6144:7168]),
                   writes=[f"kttm{p}"], pr=[(3072, 3584)], tag=f"evkt_{I}", cost=700)

            sch.op("tensor", lambda: [nc.tensor.matmul(PS[:, sacol(h):sacol(h) + 128],
                                                       qkbf[:, h, 1, :], qkbf[:, h, 0, :],
                                                       start=True, stop=True) for h in range(NH)][-1],
                   reads=["qkbf"], pw=[(sacol(h), sacol(h) + 128) for h in range(NH)], tag=f"s_{I}", cost=1400)
            for h in range(NH):
                sch.op("vector", lambda h=h, p=p: nc.vector.tensor_tensor(
                           out=stb[p][:, h, :], in0=PS[:, sacol(h):sacol(h) + 128], in1=dloc[:, h, :], op=ALU.mult),
                       reads=["dloc"], writes=[f"stb{p}h{h}"], pr=[(sacol(h), sacol(h) + 128)], tag=f"st_{I}_{h}", cost=350)
            tap("stb", I, lambda p=p: stb[p][:], [f"stb{p}h{h}" for h in range(NH)])

        def emit_midtail(I):
            p = I % 2

            for q in range(4):
                h0, h1 = 2 * q, 2 * q + 1
                for h in (h0, h1):
                    sch.op("tensor", lambda h=h, p=p, I=I: nc.tensor.matmul(
                               PS[:, ycol(h):ycol(h) + 128], stb[p][:, h, :], vbf[p][:, h, :],
                               start=True, stop=(I == 0)),
                           reads=[f"stb{p}h{h}", f"vbf{p}"], pw=[(ycol(h), ycol(h) + 128)], tag=f"yi_{I}_{h}", cost=250, grp=("oc" if I == 0 else "o"))
                for h in (h0, h1):
                    sch.op("tensor", lambda h=h, p=p: nc.tensor.matmul(
                               PS[:, sacol(h):sacol(h) + 128], kttm[p][:, h, :], vbf[p][:, h, :],
                               start=True, stop=True),
                           reads=[f"kttm{p}", f"vbf{p}"], pw=[(sacol(h), sacol(h) + 128)], tag=f"a_{I}_{h}", cost=250)
                sch.op("tensor", lambda q=q, p=p: (
                           nc.tensor.matmul(PS[:, 3072:3584], xnfm[p][:, 2 * q, :],
                                            wg[:, 2 * q, 0:512], start=(q == 0), stop=False),
                           nc.tensor.matmul(PS[:, 3584:4096], xnfm[p][:, 2 * q, :],
                                            wg[:, 2 * q, 512:1024], start=(q == 0), stop=False))[-1],
                       reads=[f"xnfm{p}", "wg"], pw=[(3072, 4096)], tag=f"g_{I}_{2 * q}", cost=600, grp=("o" if q == 0 else "-"))
                if I > 0:
                    for h in (h0, h1):
                        sch.op("tensor", lambda h=h, p=p: nc.tensor.matmul(
                                   PS[:, ycol(h):ycol(h) + 128], qtb[p][:, h, :], Rbf[:, h, :],
                                   start=False, stop=True),
                               reads=[f"qtb{p}", "Rbf"], pw=[(ycol(h), ycol(h) + 128)], tag=f"yc_{I}_{h}", cost=250, grp="c")
                sch.op("tensor", lambda q=q, p=p: (
                           nc.tensor.matmul(PS[:, 3072:3584], xnfm[p][:, 2 * q + 1, :],
                                            wg[:, 2 * q + 1, 0:512], start=False, stop=(q == 3)),
                           nc.tensor.matmul(PS[:, 3584:4096], xnfm[p][:, 2 * q + 1, :],
                                            wg[:, 2 * q + 1, 512:1024], start=False, stop=(q == 3)))[-1],
                       reads=[f"xnfm{p}", "wg"], pw=[(3072, 4096)], tag=f"g_{I}_{2 * q + 1}", cost=600, grp=("c" if q == 3 else "-"))

            for h in range(NH):
                sch.op("vector", lambda h=h: nc.vector.scalar_tensor_tensor(
                           out=Rst[:, h, :], in0=Rst[:, h, :], scalar=gc_list[h],
                           in1=PS[:, sacol(h):sacol(h) + 128], op0=ALU.mult, op1=ALU.add),
                       reads=[f"Rst{h}"], writes=[f"Rst{h}"], pr=[(sacol(h), sacol(h) + 128)], tag=f"rup_{I}_{h}", cost=350)
            sch.op("vector", lambda: nc.vector.tensor_copy(out=Rbf[:], in_=Rst[:]),
                   reads=[f"Rst{h}" for h in range(NH)], writes=["Rbf"], tag=f"rbf_{I}", cost=700)

            for h in range(NH):
                sch.op("vector", lambda h=h: nc.vector.bn_stats(out=st6g[:, h, :], in_=PS[:, ycol(h):ycol(h) + 128]),
                       writes=[f"st6g{h}", "st6g7"] if h == NH - 1 else [f"st6g{h}"],
                       pr=[(ycol(h), ycol(h) + 128)], tag=f"bng_{I}_{h}", cost=350)
            allst = [f"st6g{h}" for h in range(NH)]
            sch.op("vector", lambda: nc.vector.tensor_tensor(out=gtmp[:, 0, :], in0=st6g[:, :, 1],
                                                             in1=st6g[:, :, 4], op=ALU.add),
                   reads=allst, writes=["gtA"], tag=f"gnc1_{I}")
            sch.op("vector", lambda: nc.vector.tensor_tensor(out=gtmp[:, 1, :], in0=st6g[:, :, 1],
                                                             in1=st6g[:, :, 4], op=ALU.subtract),
                   reads=allst, writes=["gtD"], tag=f"gnc2_{I}")
            sch.op("vector", lambda: nc.vector.tensor_tensor(out=gtmp[:, 2, :], in0=gtmp[:, 1, :],
                                                             in1=gtmp[:, 1, :], op=ALU.mult),
                   reads=["gtD"], writes=["gtDD"], tag=f"gnc3_{I}")
            sch.op("vector", lambda: nc.vector.tensor_tensor(out=gtmp[:, 3, :], in0=st6g[:, :, 2],
                                                             in1=st6g[:, :, 5], op=ALU.add),
                   reads=allst, writes=["gtS"], tag=f"gnc4_{I}")
            sch.op("vector", lambda: nc.vector.scalar_tensor_tensor(out=gtmp[:, 4, :], in0=gtmp[:, 3, :],
                                                                    scalar=4.0 / HD, in1=gtmp[:, 2, :],
                                                                    op0=ALU.mult, op1=ALU.add),
                   reads=["gtS", "gtDD"], writes=["gtV"], tag=f"gnc5_{I}")
            sch.op("scalar", lambda: nc.scalar.activation(out=rstdg[:], in_=gtmp[:, 4, :], func=AF.Ln,
                                                          bias=cst[:, 0:1], scale=cst[:, 2:3]),
                   reads=["gtV", "cst"], writes=["rstdg"], tag=f"rstdgA_{I}")
            sch.op("scalar", lambda: nc.scalar.activation(out=rstdg[:], in_=rstdg[:], func=AF.Exp, scale=cst[:, 1:2]),
                   reads=["cst", "rstdg"], writes=["rstdg"], tag=f"rstdgB_{I}")
            sch.op("vector", lambda: nc.vector.scalar_tensor_tensor(out=negmrg[:], in0=gtmp[:, 0, :], scalar=-0.5,
                                                                    in1=rstdg[:], op0=ALU.mult, op1=ALU.mult),
                   reads=["gtA", "rstdg"], writes=["negmrg"], tag=f"negmrg_{I}")

            for h in range(NH):
                sch.op("vector", lambda h=h, p=p: nc.vector.tensor_scalar(
                           out=scr[:, p, h * 128:(h + 1) * 128], in0=PS[:, ycol(h):ycol(h) + 128],
                           scalar1=rstdg[:, h:h + 1], scalar2=negmrg[:, h:h + 1], op0=ALU.mult, op1=ALU.add),
                       reads=["rstdg", "negmrg"] + ([f"sA{p}"] if h == 0 else []),
                       writes=[f"sA{p}" if h == 0 else f"sA{p}q{h}"],
                       pr=[(ycol(h), ycol(h) + 128)], tag=f"gnap_{I}_{h}", cost=350)
            sch.op("scalar", lambda p=p: nc.scalar.activation(out=scr[:, p, 1024:2048], in_=PS[:, 3072:4096],
                                                              func=AF.Silu),
                   writes=[f"sB{p}"], pr=[(3072, 4096)], tag=f"silu_{I}", cost=1100)
            sch.op("gpsimd", lambda p=p: nc.gpsimd.tensor_tensor(out=scr[:, p, 2048:3072], in0=scr[:, p, 0:1024],
                                                                 in1=gnwbc[:], op=ALU.mult),
                   reads=[f"sA{p}", "gnwbc"] + [f"sA{p}q{h}" for h in range(1, NH)],
                   writes=[f"sC{p}"], tag=f"gnw_{I}", cost=2300)
            sch.op("gpsimd", lambda p=p: nc.gpsimd.tensor_tensor(out=scr[:, p, 0:1024], in0=scr[:, p, 2048:3072],
                                                                 in1=gnbbc[:], op=ALU.add),
                   reads=[f"sC{p}", "gnbbc"],
                   writes=[f"sA{p}"] + [f"sA{p}q{h}" for h in range(1, NH)], tag=f"gnb_{I}", cost=2300)
            sch.op("vector", lambda p=p: nc.vector.tensor_tensor(out=hbf[:], in0=scr[:, p, 0:1024],
                                                                 in1=scr[:, p, 1024:2048], op=ALU.mult),
                   reads=[f"sA{p}", f"sB{p}"], writes=["hbf"], tag=f"gate_{I}", cost=1200)
            tap("hbf", I, lambda: hbf[:], "hbf")

            sch.op("tensor", lambda: [nc.tensor.transpose(PSB[:, 2048 + h * 128:2176 + h * 128],
                                                          hbf[:, h, :], idn[:]) for h in range(NH)][-1],
                   reads=["hbf", "idn"], pw=[(1024 + h * 64, 1088 + h * 64) for h in range(NH)], tag=f"hT_{I}", cost=2600)
            sch.op("vector", lambda: nc.vector.tensor_copy(out=hfm[:], in_=PSB[:, 2048:3072]),
                   writes=["hfm"], pr=[(1024, 1536)], tag=f"evh_{I}", cost=700)
            for hb in range(NH):
                sch.op("tensor", lambda hb=hb: (nc.tensor.matmul(PS[:, 0:512], hfm[:, hb, :], wo[:, hb, 0:512],
                                                                 start=(hb == 0), stop=(hb == NH - 1)),
                                                nc.tensor.matmul(PS[:, 512:1024], hfm[:, hb, :], wo[:, hb, 512:1024],
                                                                 start=(hb == 0), stop=(hb == NH - 1)))[-1],
                       reads=["hfm", "wo"], pw=[(0, 1024)], tag=f"wo_{I}_{hb}", cost=600, grp=("o" if hb == 0 else "c" if hb == NH - 1 else "-"))

            emit_ln_stats(I, p, lambda: PS[:, 0:1024], [], [(0, 1024)], 1, "2")
            sch.op("vector", lambda p=p: nc.vector.tensor_scalar(out=scr[:, p, 2048:3072], in0=PS[:, 0:1024],
                                                                 scalar1=rstd[:, 1:2], scalar2=negmr[:, 1:2],
                                                                 op0=ALU.mult, op1=ALU.add),
                   reads=["rstd1", "negmr1"], writes=[f"sC{p}"], pr=[(0, 1024)], tag=f"ln2ap_{I}", cost=1300)
            sch.op("gpsimd", lambda p=p: nc.gpsimd.tensor_tensor(out=x1[p][:], in0=scr[:, p, 2048:3072],
                                                                 in1=x0[p][:], op=ALU.add),
                   reads=[f"sC{p}", f"x0{p}"], writes=[f"x1{p}"], tag=f"resid1_{I}", cost=2300)
            tap("x1", I, lambda p=p: x1[p][:], f"x1{p}")

            emit_ln_stats(I, p, lambda p=p: x1[p][:], [f"x1{p}"], [], 2, "3")
            sch.op("vector", lambda p=p: nc.vector.tensor_scalar(out=xn2tm[:], in0=x1[p][:],
                                                                 scalar1=rstd[:, 2:3], scalar2=negmr[:, 2:3],
                                                                 op0=ALU.mult, op1=ALU.add),
                   reads=[f"x1{p}", "rstd2", "negmr2"], writes=["xntm"], tag=f"ln3ap_{I}", cost=700)
            sch.op("tensor", lambda: [nc.tensor.transpose(PSB[:, 3072 + h * 128:3200 + h * 128],
                                                          xn2tm[:, h * 128:(h + 1) * 128], idn[:])
                                      for h in range(NH)][-1],
                   reads=["xntm", "idn"], pw=[(1536 + h * 64, 1600 + h * 64) for h in range(NH)], tag=f"xn2T_{I}", cost=2600)
            sch.op("vector", lambda: nc.vector.tensor_copy(out=xn2fm[:], in_=PSB[:, 3072:4096]),
                   writes=["xn2fm"], pr=[(1536, 2048)], tag=f"evxn2_{I}", cost=700)
            tap("xn2fm", I, lambda: xn2fm[:], "xn2fm")

            for k in range(8):
                base = 1024 + (k % 2) * 512
                sch.op("tensor", lambda k=k, base=base: (
                           nc.tensor.matmul(PS[:, base:base + 128], u1[:, k, 0:128], xn2fm[:, k, :],
                                            start=(k < 2), stop=False),
                           nc.tensor.matmul(PS[:, base + 128:base + 256], u1[:, k, 128:256], xn2fm[:, k, :],
                                            start=False, stop=(k >= 6)))[-1],
                       reads=["xn2fm", "u1"], pw=[(base, base + 256)], tag=f"p_{I}_{k}", cost=350, grp=("o" if k < 2 else "c" if k >= 6 else "-"))
            sch.op("vector", lambda: nc.vector.tensor_copy(out=qftmp[:], in_=PS[:, 1024:1280]),
                   writes=["qftmp"], pr=[(1024, 1280)], tag=f"pev1_{I}", cost=450)
            sch.op("vector", lambda: nc.vector.tensor_tensor(out=pbf[:], in0=qftmp[:],
                                                             in1=PS[:, 1536:1792], op=ALU.add),
                   reads=["qftmp"], writes=["pbf"], pr=[(1536, 1792)], tag=f"pev2_{I}", cost=450)
            tap("pbf", I, lambda: pbf[:], "pbf")

            qf_cols = (0, 512, 1024, 1536)
            for half in range(2):
                fbs = list(range(half * 16, half * 16 + 16))
                for rc in range(2):
                    sch.op("tensor", lambda rc=rc, fbs=fbs: [
                               nc.tensor.matmul(PS[:, 2048 + (fb % 16) * 128:2176 + (fb % 16) * 128],
                                                v1[:, rc, fb * 128:(fb + 1) * 128], pbf[:, rc, :],
                                                start=(rc == 0 and fb % 4 == 0),
                                                stop=False) for fb in fbs][-1],
                           reads=["pbf", "v1"], pw=[(2048, 4096)], tag=f"h_{I}_{half}_{rc}", cost=2900, grp=("o" if rc == 0 else "-"))
                sch.op("tensor", lambda fbs=fbs: [
                           nc.tensor.matmul(PS[:, 2048 + (fb % 16) * 128:2176 + (fb % 16) * 128],
                                            b1r[0:1, fb * 128:(fb + 1) * 128], onesb[:],
                                            start=False, stop=(fb % 4 == 3)) for fb in fbs][-1],
                       reads=["b1r", "ones"], pw=[(2048, 4096)], tag=f"hb1_{I}_{half}", cost=900, grp="c")
                sch.op("scalar", lambda: nc.scalar.activation(out=Hbf[:], in_=PS[:, 2048:4096],
                                                              func=AF.Gelu_apprx_tanh),
                       writes=["Hbf"], pr=[(2048, 4096)], tag=f"gelu_{I}_{half}", cost=1900)
                for fb in fbs:
                    c0 = qf_cols[fb % 4]
                    sch.op("tensor", lambda fb=fb, c0=c0: (
                               nc.tensor.matmul(PS[:, c0:c0 + 128], u2[:, fb, 0:128], Hbf[:, fb % 16, :],
                                                start=(fb < 4), stop=False),
                               nc.tensor.matmul(PS[:, c0 + 128:c0 + 256], u2[:, fb, 128:256], Hbf[:, fb % 16, :],
                                                start=False, stop=(fb >= 28)))[-1],
                           reads=["Hbf", "u2"], pw=[(c0, c0 + 256)], tag=f"u2_{I}_{fb}", cost=400, grp=("o" if fb < 4 else "c" if fb >= 28 else "-"))

            sch.op("vector", lambda: nc.vector.tensor_copy(out=qftmp[:], in_=PS[:, 0:256]),
                   writes=["qftmp"], pr=[(0, 256)], tag=f"qfc1_{I}", cost=450)
            sch.op("vector", lambda: nc.vector.tensor_tensor(out=qftmp[:], in0=qftmp[:], in1=PS[:, 512:768], op=ALU.add),
                   reads=["qftmp"], writes=["qftmp"], pr=[(512, 768)], tag=f"qfc2_{I}", cost=450)
            sch.op("vector", lambda: nc.vector.tensor_tensor(out=qftmp[:], in0=qftmp[:], in1=PS[:, 1024:1280], op=ALU.add),
                   reads=["qftmp"], writes=["qftmp"], pr=[(1024, 1280)], tag=f"qfc3_{I}", cost=450)
            sch.op("vector", lambda: nc.vector.tensor_tensor(out=qfbf[:], in0=qftmp[:], in1=PS[:, 1536:1792], op=ALU.add),
                   reads=["qftmp"], writes=["qfbf"], pr=[(1536, 1792)], tag=f"qfc4_{I}", cost=450)
            tap("qfbf", I, lambda: qfbf[:], "qfbf")

            sch.op("tensor", lambda: (nc.tensor.matmul(PS[:, 3072:3584], qfbf[:, 0, :], v2[:, 0, 0:512],
                                                       start=True, stop=False),
                                      nc.tensor.matmul(PS[:, 3584:4096], qfbf[:, 0, :], v2[:, 0, 512:1024],
                                                       start=True, stop=False))[-1],
                   reads=["qfbf", "v2"], pw=[(3072, 4096)], tag=f"v2a_{I}", cost=500, grp="o")
            sch.op("tensor", lambda: (nc.tensor.matmul(PS[:, 3072:3584], qfbf[:, 1, :], v2[:, 1, 0:512],
                                                       start=False, stop=False),
                                      nc.tensor.matmul(PS[:, 3584:4096], qfbf[:, 1, :], v2[:, 1, 512:1024],
                                                       start=False, stop=False))[-1],
                   reads=["qfbf", "v2"], pw=[(3072, 4096)], tag=f"v2b_{I}", cost=500, grp="-")
            sch.op("tensor", lambda: (nc.tensor.matmul(PS[:, 3072:3584], onesb[:], b2r[0:1, 0:512],
                                                       start=False, stop=True),
                                      nc.tensor.matmul(PS[:, 3584:4096], onesb[:], b2r[0:1, 512:1024],
                                                       start=False, stop=True))[-1],
                   reads=["ones", "b2r"], pw=[(3072, 4096)], tag=f"b2_{I}", cost=500, grp="c")

            emit_ln_stats(I, p, lambda: PS[:, 3072:4096], [], [(3072, 4096)], 3, "4")
            sch.op("vector", lambda p=p: nc.vector.tensor_scalar(out=scr[:, p, 2048:3072], in0=PS[:, 3072:4096],
                                                                 scalar1=rstd[:, 3:4], scalar2=negmr[:, 3:4],
                                                                 op0=ALU.mult, op1=ALU.add),
                   reads=["rstd3", "negmr3"], writes=[f"sC{p}"], pr=[(3072, 4096)], tag=f"ln4ap_{I}", cost=1300)
            sch.op("gpsimd", lambda p=p: nc.gpsimd.tensor_tensor(out=outb[:], in0=scr[:, p, 2048:3072],
                                                                 in1=x1[p][:], op=ALU.add),
                   reads=[f"sC{p}", f"x1{p}"], writes=["out"], tag=f"resid2_{I}", cost=2300)
            sch.op("scalar", lambda I=I: nc.scalar.dma_start(out=y_h[I * C:(I + 1) * C, :], in_=outb[:]),
                   reads=["out"], dma=True, tag=f"store_{I}", cost=3500)

        for I in range(NCH):
            emit_front(I)
            if I > 0:
                emit_midtail(I - 1)
            if I + 1 < NCH:
                emit_loads(I + 1)
        emit_midtail(NCH - 1)

        block = E(nc.Block())
        sch.run(block)

    return nc, dbg_names


def kernel(L0_attn_X, L0_attn_W_Q, L0_attn_W_K, L0_attn_W_V, L0_attn_W_G, L0_attn_W_O,
           L0_attn_gn_weight, L0_attn_gn_bias, L0_ffn_U1, L0_ffn_V1, L0_ffn_b1,
           L0_ffn_U2, L0_ffn_V2, L0_ffn_b2):
    consts, gc_list = host_constants()
    nc, _ = build_nc(gc_list)
    common = dict(
        wq=np.ascontiguousarray(L0_attn_W_Q, np.float32),
        wk=np.ascontiguousarray(L0_attn_W_K, np.float32),
        wv=np.ascontiguousarray(L0_attn_W_V, np.float32),
        wg=np.ascontiguousarray(L0_attn_W_G, np.float32),
        wo=np.ascontiguousarray(L0_attn_W_O, np.float32),
        gnw=np.ascontiguousarray(L0_attn_gn_weight, np.float32),
        gnb=np.ascontiguousarray(L0_attn_gn_bias, np.float32),
        u1=np.ascontiguousarray(L0_ffn_U1, np.float32),
        v1=np.ascontiguousarray(L0_ffn_V1, np.float32),
        b1=np.ascontiguousarray(L0_ffn_b1, np.float32),
        u2=np.ascontiguousarray(L0_ffn_U2, np.float32),
        v2=np.ascontiguousarray(L0_ffn_V2, np.float32),
        b2=np.ascontiguousarray(L0_ffn_b2, np.float32),
        **consts,
    )
    X = np.ascontiguousarray(L0_attn_X, np.float32)
    in_maps = [dict(common, x=np.ascontiguousarray(X[c])) for c in range(8)]
    res = run_bass_kernel_spmd(nc, in_maps, core_ids=list(range(8)))
    return np.stack([np.asarray(r["y"], np.float32) for r in res.results], axis=0)


# version 08
# speedup vs baseline: 1.3444x; 1.0026x over previous
import math
from contextlib import ExitStack

import numpy as np
import ml_dtypes

import concourse.bass as bass
import concourse.mybir as mybir
from concourse.bass_utils import run_bass_kernel_spmd

F32 = mybir.dt.float32
BF16 = mybir.dt.bfloat16
AF = mybir.ActivationFunctionType
ALU = mybir.AluOpType

S, D, NH, HD, C = 4096, 1024, 8, 128, 128
NCH = S // C
EPS = 1e-5


def host_constants():
    gam = 1.0 - np.exp(np.linspace(math.log(1.0 / 32.0), math.log(1.0 / 512.0), NH))
    ii = np.arange(C)
    diff = ii[None, :] - ii[:, None]
    dloc = np.where(diff[:, :, None] >= 0,
                    gam[None, None, :] ** np.maximum(diff, 0)[:, :, None], 0.0)
    DLOC = np.ascontiguousarray(dloc.transpose(0, 2, 1)).astype(np.float32)
    GQ = np.ascontiguousarray(np.broadcast_to((gam[:, None] ** (ii[None, :] + 1))[None],
                                              (128, NH, C))).astype(np.float32)
    GK = np.ascontiguousarray(np.broadcast_to((gam[:, None] ** (C - 1 - ii[None, :]))[None],
                                              (128, NH, C))).astype(np.float32)
    GC = [float(np.float32(gam[h] ** C)) for h in range(NH)]

    half = HD // 2
    scale = (np.arange(0, HD, 2, dtype=np.float64) + 0.4 * HD) / (1.4 * HD)
    pos = np.arange(S, dtype=np.float64)
    sc = scale[None, :] ** (pos[:, None] / 512.0)
    inv_freq = 1.0 / 10000.0 ** (np.arange(half, dtype=np.float64) / half)
    sinus = pos[:, None] * inv_freq[None, :]
    sin, cos = np.sin(sinus), np.cos(sinus)

    def rep2(a):
        return np.repeat(a, 2, axis=-1)

    cosQ, sinQ = rep2(cos * sc), rep2(sin * sc)
    cosK, sinK = rep2(cos / sc), rep2(sin / sc)
    TAB = np.empty((NCH, 128, 2, 2, C), np.float32)
    for ci in range(NCH):
        sl = slice(ci * C, (ci + 1) * C)
        TAB[ci, :, 0, 0, :] = cosQ[sl].T
        TAB[ci, :, 1, 0, :] = sinQ[sl].T
        TAB[ci, :, 0, 1, :] = cosK[sl].T
        TAB[ci, :, 1, 1, :] = sinK[sl].T

    P = np.zeros((128, 128), np.float64)
    for m in range(64):
        P[2 * m, 2 * m + 1] = -1.0
        P[2 * m + 1, 2 * m] = 1.0
    PT = np.ascontiguousarray(P.T).astype(ml_dtypes.bfloat16)
    IDN = np.eye(128, dtype=ml_dtypes.bfloat16)
    IDNF = np.eye(128, dtype=np.float32)
    ONES = np.ones((1, 128), ml_dtypes.bfloat16)
    return dict(TAB=TAB, DLOC=DLOC, GQ=GQ, GK=GK, PT=PT, IDN=IDN, IDNF=IDNF, ONES=ONES), GC


class Sched:

    ENG_COST = {"tensor": 800, "vector": 1100, "scalar": 900, "gpsimd": 2200, "sync": 3000}
    SEM_LAT = 250.0

    def __init__(self, nc, sems):
        self.nc = nc
        self.sems = sems
        self.ops = []
        self.sb_w = {}
        self.sb_r = {}
        self.ps_w_bytes = []
        self.bank_reads = {}
        self.g_open = {}
        self.g_close = {}
        self.g_members = {}
        self.g_state = {}

    @staticmethod
    def banks(lo, hi):
        return range(lo // 512, (hi + 511) // 512)

    def op(self, eng, fn, reads=(), writes=(), pr=(), pw=(), dma=False, tag="", cost=None, grp="oc"):
        idx = len(self.ops)
        deps = set()

        def need(j):
            if j is not None and j != idx:
                deps.add(j)

        for b in reads:
            need(self.sb_w.get(b))
        for b in writes:
            need(self.sb_w.get(b))
            for j in self.sb_r.get(b, ()):
                need(j)
        for (lo, hi) in pr:
            for bk in self.banks(lo, hi):
                need(self.g_close.get(bk))
                if self.g_state.get(bk) == "open":
                    for m in self.g_members.get(bk, ()):
                        need(m)
                for (j, e2) in self.bank_reads.get(bk, ()):
                    if e2 != eng:
                        need(j)
        opbanks = set()
        for (lo, hi) in pw:
            assert eng == "tensor", f"only PE writes PSUM (op {tag})"
            for (l2, h2, j) in self.ps_w_bytes:
                if l2 < hi and lo < h2:
                    need(j)
            for bk in self.banks(lo, hi):
                opbanks.add(bk)
        for bk in opbanks:
            if grp in ("o", "oc"):
                need(self.g_close.get(bk))
                for (j, e2) in self.bank_reads.get(bk, ()):
                    need(j)
                self.bank_reads[bk] = []
                self.g_open[bk] = idx
                self.g_members[bk] = [idx]
                self.g_state[bk] = "open"
            else:
                need(self.g_open.get(bk))
                for (j, e2) in self.bank_reads.get(bk, ()):
                    need(j)
                self.bank_reads[bk] = []
                self.g_members.setdefault(bk, []).append(idx)
            if grp in ("c", "oc"):
                for m in self.g_members.get(bk, ()):
                    need(m)
                self.g_close[bk] = idx
                self.g_state[bk] = "closed"

        for b in reads:
            self.sb_r.setdefault(b, []).append(idx)
        for b in writes:
            self.sb_w[b] = idx
            self.sb_r[b] = []
        for (lo, hi) in pr:
            for bk in self.banks(lo, hi):
                self.bank_reads.setdefault(bk, []).append((idx, eng))
        for (lo, hi) in pw:
            self.ps_w_bytes = [e for e in self.ps_w_bytes if not (lo <= e[0] and e[1] <= hi)]
            self.ps_w_bytes.append((lo, hi, idx))

        self.ops.append(dict(idx=idx, eng=eng, fn=fn, dma=dma, tag=tag,
                             cost=float(cost if cost is not None else self.ENG_COST[eng]),
                             deps=deps))
        return idx

    def _order(self):
        import heapq
        n = len(self.ops)
        children = [[] for _ in range(n)]
        indeg = [0] * n
        for o in self.ops:
            for d in o["deps"]:
                children[d].append(o["idx"])
                indeg[o["idx"]] += 1
        uprank = [0.0] * n
        for i in range(n - 1, -1, -1):
            c = self.ops[i]["cost"]
            m = 0.0
            for ch in children[i]:
                if uprank[ch] > m:
                    m = uprank[ch]
            uprank[i] = c + m
        rdy = [0.0] * n
        eng_avail = {}
        heap = []
        for i in range(n):
            if indeg[i] == 0:
                heapq.heappush(heap, (0.0, -uprank[i], i))
        order = []
        while heap:
            cand = []
            best = None
            while heap:
                t, nr, i = heapq.heappop(heap)
                eng = self.ops[i]["eng"]
                est = max(t, eng_avail.get(eng, 0.0))
                cand.append((est, nr, i, t))
                if best is None or (est, nr) < best[:2]:
                    best = (est, nr, i, t)
                if len(cand) >= 24:
                    break
            for c in cand:
                if c[2] != best[2]:
                    heapq.heappush(heap, (c[3], c[1], c[2]))
            est, _, i, _ = best
            o = self.ops[i]
            f = est + o["cost"]
            eng_avail[o["eng"]] = f
            o["finish"] = f
            order.append(i)
            for ch in children[i]:
                lat = self.SEM_LAT if self.ops[ch]["eng"] != o["eng"] else 100.0
                r = f + lat
                if r > rdy[ch]:
                    rdy[ch] = r
                indeg[ch] -= 1
                if indeg[ch] == 0:
                    heapq.heappush(heap, (rdy[ch], -uprank[ch], ch))
        assert len(order) == n, f"cycle: scheduled {len(order)} of {n}"
        return order

    def run(self, block):
        order = self._order()
        cnt = {k: 0 for k in self.sems}
        tok = {}
        sem_of = {}
        last_dma = {}
        waits_per = {i: [] for i in range(len(self.ops))}
        waited = {}
        for i in order:
            o = self.ops[i]
            sem = ("d_" if o["dma"] else "m_") + o["eng"]
            deps = {}
            for d in o["deps"]:
                s, v = sem_of[d], tok[d]
                if v > deps.get(s, 0):
                    deps[s] = v
            if o["dma"] and cnt[sem] > 0:
                deps[sem] = max(deps.get(sem, 0), cnt[sem])
            for s, v in sorted(deps.items()):
                if waited.get((o["eng"], s), 0) < v:
                    waits_per[i].append((s, v))
                    waited[(o["eng"], s)] = v
            cnt[sem] += 16 if o["dma"] else 1
            tok[i] = cnt[sem]
            sem_of[i] = sem
        self.cnt = cnt
        per_eng = {e: [i for i in order if self.ops[i]["eng"] == e]
                   for e in ("sync", "scalar", "tensor", "vector", "gpsimd")}

        for eng in ("sync", "scalar", "tensor", "vector", "gpsimd"):
            idxs = per_eng[eng]

            def make(idxs=idxs, eng=eng):
                def go(e):
                    for i in idxs:
                        o = self.ops[i]
                        for (s, v) in waits_per[i]:
                            e.wait_ge(self.sems[s], v)
                        inst = o["fn"]()
                        inst.then_inc(self.sems[("d_" if o["dma"] else "m_") + eng], 16 if o["dma"] else 1)
                    if eng == "scalar":
                        e.wait_ge(self.sems["d_scalar"], self.cnt["d_scalar"])
                return go

            getattr(block, eng)(make())


def build_nc(gc_list, dbg=None):
    dbg = dbg or set()
    nc = bass.Bass("TRN2", target_bir_lowering=False)

    def din(name, shape, dtype=F32):
        return nc.dram_tensor(name, shape, dtype, kind="ExternalInput").ap()

    x_h = din("x", [S, D])
    wq_h = din("wq", [NH, HD, HD]); wk_h = din("wk", [NH, HD, HD]); wv_h = din("wv", [NH, HD, HD])
    wg_h = din("wg", [D, D]); wo_h = din("wo", [D, D])
    gnw_h = din("gnw", [D]); gnb_h = din("gnb", [D])
    u1_h = din("u1", [D, 256]); v1_h = din("v1", [256, 4096]); b1_h = din("b1", [4096])
    u2_h = din("u2", [4096, 256]); v2_h = din("v2", [256, D]); b2_h = din("b2", [D])
    tab_h = din("TAB", [NCH, 128, 2, 2, C])
    dloc_h = din("DLOC", [128, NH, C]); gq_h = din("GQ", [128, NH, C]); gk_h = din("GK", [128, NH, C])
    pt_h = din("PT", [128, 128], BF16); idn_h = din("IDN", [128, 128], BF16)
    ones_h = din("ONES", [1, 128], BF16)
    y_h = nc.dram_tensor("y", [S, D], F32, kind="ExternalOutput").ap()

    dbg_names = []

    with ExitStack() as st:
        E = st.enter_context
        wq = E(nc.sbuf_tensor("wq_s", [128, NH, HD], BF16))
        wk = E(nc.sbuf_tensor("wk_s", [128, NH, HD], BF16))
        wv = E(nc.sbuf_tensor("wv_s", [128, NH, HD], BF16))
        wqr = E(nc.sbuf_tensor("wqr_s", [128, NH, HD], BF16))
        wkr = E(nc.sbuf_tensor("wkr_s", [128, NH, HD], BF16))
        wg = E(nc.sbuf_tensor("wg_s", [128, 8, D], BF16))
        wo = E(nc.sbuf_tensor("wo_s", [128, 8, D], BF16))
        u1 = E(nc.sbuf_tensor("u1_s", [128, 8, 256], BF16))
        v1 = E(nc.sbuf_tensor("v1_s", [128, 2, 4096], BF16))
        u2 = E(nc.sbuf_tensor("u2_s", [128, 32, 256], BF16))
        v2 = E(nc.sbuf_tensor("v2_s", [128, 2, D], BF16))
        b2r = E(nc.sbuf_tensor("b2r_s", [1, D], BF16))
        onesb = E(nc.sbuf_tensor("ones_s", [1, 128], BF16))
        gnwbc = E(nc.sbuf_tensor("gnwbc", [128, D], F32))
        gnbbc = E(nc.sbuf_tensor("gnbbc", [128, D], F32))
        b1r = E(nc.sbuf_tensor("b1r_s", [1, 4096], BF16))
        dloc = E(nc.sbuf_tensor("dloc", [128, NH, C], F32))
        gq = E(nc.sbuf_tensor("gq", [128, NH, C], F32))
        gk = E(nc.sbuf_tensor("gk", [128, NH, C], F32))
        idn = E(nc.sbuf_tensor("idn", [128, 128], BF16))
        pt = E(nc.sbuf_tensor("pt", [128, 128], BF16))

        x0 = [E(nc.sbuf_tensor(f"x0_{p}", [128, D], F32)) for p in range(2)]
        tabs = [E(nc.sbuf_tensor(f"tab_{p}", [128, 2, 2, C], F32)) for p in range(2)]
        xntm = E(nc.sbuf_tensor("xntm", [128, D], BF16))
        xnfm = [E(nc.sbuf_tensor(f"xnfm_{pp}", [128, NH, C], BF16)) for pp in range(2)]
        qkbf = E(nc.sbuf_tensor("qkbf", [128, NH, 2, C], BF16))
        qtb = [E(nc.sbuf_tensor(f"qtb_{pp}", [128, NH, C], BF16)) for pp in range(2)]
        ktb = E(nc.sbuf_tensor("ktb", [128, NH, C], BF16))
        kttm = [E(nc.sbuf_tensor(f"kttm_{pp}", [128, NH, C], BF16)) for pp in range(2)]
        vbf = [E(nc.sbuf_tensor(f"vbf_{pp}", [128, NH, C], BF16)) for pp in range(2)]
        stb = [E(nc.sbuf_tensor(f"stb_{pp}", [128, NH, C], BF16)) for pp in range(2)]
        Rst = E(nc.sbuf_tensor("Rst", [128, NH, HD], F32))
        Rbf = E(nc.sbuf_tensor("Rbf", [128, NH, HD], BF16))
        scr = E(nc.sbuf_tensor("scr", [128, 2, 3072], F32))
        hbf = E(nc.sbuf_tensor("hbf", [128, NH, C], BF16))
        hfm = E(nc.sbuf_tensor("hfm", [128, NH, C], BF16))
        x1 = [E(nc.sbuf_tensor(f"x1_{p}", [128, D], F32)) for p in range(2)]
        xn2tm = xntm
        xn2fm = E(nc.sbuf_tensor("xn2fm", [128, NH, C], BF16))
        pbf = E(nc.sbuf_tensor("pbf", [128, 2, 128], BF16))
        Hbf = E(nc.sbuf_tensor("Hbf", [128, 16, 128], BF16))
        qftmp = E(nc.sbuf_tensor("qftmp", [128, 256], F32))
        qfbf = E(nc.sbuf_tensor("qfbf", [128, 2, 128], BF16))
        outb = E(nc.sbuf_tensor("out_s", [128, D], F32))
        st6 = [E(nc.sbuf_tensor(f"st6_{p}", [128, 2, 6], F32)) for p in range(2)]
        st2 = [E(nc.sbuf_tensor(f"st2_{p}", [128, 2], F32)) for p in range(2)]
        st6g = E(nc.sbuf_tensor("st6g", [128, NH, 6], F32))
        gtmp = E(nc.sbuf_tensor("gtmp", [128, 5, NH], F32))
        rstd = E(nc.sbuf_tensor("rstd", [128, 8], F32))
        negmr = E(nc.sbuf_tensor("negmr", [128, 4], F32))
        rstdg = E(nc.sbuf_tensor("rstdg", [128, NH], F32))
        cst = E(nc.sbuf_tensor("cst", [128, 4], F32))
        lnsum = E(nc.sbuf_tensor("lnsum", [128, 6], F32))
        lnsq = E(nc.sbuf_tensor("lnsq", [128, 4], F32))
        lntmp = E(nc.sbuf_tensor("lntmp", [128, 4], F32))
        negmrg = E(nc.sbuf_tensor("negmrg", [128, NH], F32))
        wors = E(nc.sbuf_tensor("wors", [128, NH], BF16))
        v2rs = E(nc.sbuf_tensor("v2rs", [128, 2], BF16))
        b2s = E(nc.sbuf_tensor("b2s", [1, 1], BF16))

        wtt = scr[:, 0, 0:64].bitcast(BF16)
        PS = E(nc.psum_tensor("PS", [128, 4096], F32))
        PSB = PS[:].bitcast(BF16)

        sems = {}
        for name in ("m_tensor", "m_vector", "m_scalar", "m_gpsimd", "d_sync", "d_scalar", "d_gpsimd"):
            sems[name] = E(nc.semaphore(name))
        sch = Sched(nc, sems)

        def tap(stage, I, ap_fn, bufname):
            if (stage, I) in dbg:
                ap = ap_fn()
                nm = f"dbg_{stage}_{I}"
                t = nc.dram_tensor(nm, list(ap.shape), ap.dtype, kind="ExternalOutput").ap()
                dbg_names.append(nm)
                names = [bufname] if isinstance(bufname, str) else list(bufname)
                sch.op("sync", lambda ap=ap, t=t: nc.sync.dma_start(out=t, in_=ap),
                       reads=names, dma=True, tag=f"tap_{stage}_{I}")

        for fn, wr, tg in (
            (lambda: nc.gpsimd.dma_start(out=wq[:], in_=wq_h.rearrange("h d e -> d h e")), ["wq"], "ld_wq"),
            (lambda: nc.gpsimd.dma_start(out=wk[:], in_=wk_h.rearrange("h d e -> d h e")), ["wk"], "ld_wk"),
            (lambda: nc.gpsimd.dma_start(out=wv[:], in_=wv_h.rearrange("h d e -> d h e")), ["wv"], "ld_wv"),
            (lambda: nc.gpsimd.dma_start(out=wg[:], in_=wg_h.rearrange("(b p) o -> p b o", p=128)), ["wg"], "ld_wg"),
            (lambda: nc.gpsimd.dma_start(out=wo[:], in_=wo_h.rearrange("(b p) o -> p b o", p=128)), ["wo"], "ld_wo"),
            (lambda: nc.gpsimd.dma_start(out=u1[:], in_=u1_h.rearrange("(b p) r -> p b r", p=128)), ["u1"], "ld_u1"),
            (lambda: nc.gpsimd.dma_start(out=v1[:], in_=v1_h.rearrange("(b p) f -> p b f", p=128)), ["v1"], "ld_v1"),
            (lambda: nc.gpsimd.dma_start(out=u2[:], in_=u2_h.rearrange("(b p) r -> p b r", p=128)), ["u2"], "ld_u2"),
            (lambda: nc.gpsimd.dma_start(out=v2[:], in_=v2_h.rearrange("(b p) o -> p b o", p=128)), ["v2"], "ld_v2"),
            (lambda: nc.gpsimd.dma_start(out=b1r[:], in_=b1_h.unsqueeze(0)), ["b1r"], "ld_b1"),
            (lambda: nc.gpsimd.dma_start(out=b2r[:], in_=b2_h.unsqueeze(0)), ["b2r"], "ld_b2"),
            (lambda: nc.gpsimd.dma_start(out=onesb[:], in_=ones_h), ["ones"], "ld_ones"),
        ):
            sch.op("gpsimd", fn, writes=wr, dma=True, tag=tg)

        for fn, wr, tg in (
            (lambda: nc.sync.dma_start(out=gnwbc[:], in_=gnw_h.unsqueeze(0).broadcast_to([128, D])), ["gnwbc"], "ld_gnw"),
            (lambda: nc.sync.dma_start(out=gnbbc[:], in_=gnb_h.unsqueeze(0).broadcast_to([128, D])), ["gnbbc"], "ld_gnb"),
            (lambda: nc.sync.dma_start(out=dloc[:], in_=dloc_h), ["dloc"], "ld_dloc"),
            (lambda: nc.sync.dma_start(out=gq[:], in_=gq_h), ["gq"], "ld_gq"),
            (lambda: nc.sync.dma_start(out=gk[:], in_=gk_h), ["gk"], "ld_gk"),
            (lambda: nc.sync.dma_start(out=idn[:], in_=idn_h), ["idn"], "ld_idn"),
            (lambda: nc.sync.dma_start(out=pt[:], in_=pt_h), ["pt"], "ld_pt"),
        ):
            sch.op("sync", fn, writes=wr, dma=True, tag=tg)

        for (src, dst, sname, dname) in ((wq, wqr, "wq", "wqr"), (wk, wkr, "wk", "wkr")):
            for h in range(NH):
                sch.op("tensor", lambda src=src, h=h: nc.tensor.transpose(PSB[:, 0:128], src[:, h, :], idn[:]),
                       reads=[sname, "idn"], pw=[(0, 64)], tag=f"T_{dname}{h}")
                sch.op("vector", lambda: nc.vector.tensor_copy(out=wtt, in_=PSB[:, 0:128]),
                       writes=["sA0"], pr=[(0, 64)], tag=f"ev_{dname}{h}")
                sch.op("tensor", lambda: nc.tensor.matmul(PS[:, 512:640], wtt, pt[:], start=True, stop=True),
                       reads=["sA0", "pt"], pw=[(512, 640)], tag=f"mm_{dname}{h}")
                sch.op("vector", lambda dst=dst, h=h: nc.vector.tensor_copy(out=dst[:, h, :], in_=PS[:, 512:640]),
                       writes=[dname], pr=[(512, 640)], tag=f"ev2_{dname}{h}")

        tap("wqr", 0, lambda: wqr[:], "wqr")
        tap("wkr", 0, lambda: wkr[:], "wkr")
        sch.op("vector", lambda: nc.vector.tensor_reduce(out=rstdg[:], in_=wo[:], axis=mybir.AxisListType.X,
                                                         op=ALU.add),
               reads=["wo"], writes=["rstdg"], tag="worsumA", cost=3500)
        sch.op("vector", lambda: nc.vector.tensor_copy(out=wors[:], in_=rstdg[:]),
               reads=["rstdg"], writes=["wors"], tag="worsumB", cost=200)
        sch.op("vector", lambda: nc.vector.tensor_reduce(out=negmrg[:, 0:2], in_=v2[:], axis=mybir.AxisListType.X,
                                                         op=ALU.add),
               reads=["v2"], writes=["negmrg"], tag="v2rsumA", cost=900)
        sch.op("vector", lambda: nc.vector.tensor_copy(out=v2rs[:], in_=negmrg[:, 0:2]),
               reads=["negmrg"], writes=["v2rs"], tag="v2rsumB", cost=200)
        sch.op("vector", lambda: nc.vector.tensor_reduce(out=lntmp[0:1, 3:4], in_=b2r[:], axis=mybir.AxisListType.X,
                                                         op=ALU.add),
               reads=["b2r"], writes=["lntmp3"], tag="b2sumA", cost=900)
        sch.op("vector", lambda: nc.vector.tensor_copy(out=b2s[:], in_=lntmp[0:1, 3:4]),
               reads=["lntmp3"], writes=["b2s"], tag="b2sumB", cost=200)
        sch.op("vector", lambda: nc.vector.memset(Rst[:], 0.0),
               writes=[f"Rst{h}" for h in range(NH)], tag="memset_R")
        sch.op("vector", lambda: (nc.vector.memset(cst[:, 0:1], EPS),
                                  nc.vector.memset(cst[:, 1:2], -0.5),
                                  nc.vector.memset(cst[:, 2:3], 0.25),
                                  nc.vector.memset(cst[:, 3:4], 1.0 / D))[-1],
               writes=["cst"], tag="memset_cst")

        def emit_loads(I):
            p = I % 2
            sch.op("sync", lambda I=I, p=p: nc.sync.dma_start(out=x0[p][:], in_=x_h[I * C:(I + 1) * C, :]),
                   writes=[f"x0{p}"], dma=True, tag=f"ld_x0_{I}", cost=3500)
            sch.op("sync", lambda I=I, p=p: nc.sync.dma_start(out=tabs[p][:], in_=tab_h[I]),
                   writes=[f"tab{p}"], dma=True, tag=f"ld_tab_{I}", cost=2800)

        emit_loads(0)

        def sacol(h):
            return 1024 + (h % 2) * 512 + (h // 2) * 128

        def ycol(h):
            return 2048 + (h % 2) * 512 + (h // 2) * 128

        def emit_ln_stats(I, p, srcfn, src_reads, src_pr, idx, tagp, sum_src=None, sumcol=None):
            sc_ = idx if sumcol is None else sumcol
            sname = f"lnsum{sc_}"
            sch.op("scalar", lambda: nc.scalar.activation(out=scr[:, p, 1024:2048], in_=srcfn(),
                                                          func=AF.Square, accum_out=lnsq[:, idx:idx + 1]),
                   reads=src_reads, writes=[f"sB{p}", f"lnsq{idx}"], pr=src_pr, tag=f"sq{tagp}_{I}", cost=1100)
            if sum_src is None:
                sch.op("vector", lambda sc_=sc_: nc.vector.tensor_scalar(out=scr[:, p, 0:1024], in0=srcfn(),
                                                                 scalar1=1.0, scalar2=None, op0=ALU.mult,
                                                                 op1=ALU.add, accum_out=lnsum[:, sc_:sc_ + 1]),
                       reads=src_reads, writes=[f"sA{p}", sname], pr=src_pr, tag=f"sm{tagp}_{I}", cost=1300)
            else:
                ap_fn, rds, prr = sum_src
                sch.op("vector", lambda sc_=sc_: nc.vector.tensor_copy(out=lnsum[:, sc_:sc_ + 1], in_=ap_fn()),
                       reads=rds, writes=[sname], pr=prr, tag=f"smc{tagp}_{I}", cost=200)
            sch.op("vector", lambda sc_=sc_: nc.vector.tensor_tensor(out=lntmp[:, idx:idx + 1], in0=lnsum[:, sc_:sc_ + 1],
                                                             in1=lnsum[:, sc_:sc_ + 1], op=ALU.mult),
                   reads=[sname], writes=[f"lntmp{idx}"], tag=f"varA{tagp}_{I}", cost=200)
            sch.op("vector", lambda: nc.vector.scalar_tensor_tensor(out=lntmp[:, idx:idx + 1],
                                                                    in0=lntmp[:, idx:idx + 1],
                                                                    scalar=-1.0 / D, in1=lnsq[:, idx:idx + 1],
                                                                    op0=ALU.mult, op1=ALU.add),
                   reads=[f"lntmp{idx}", f"lnsq{idx}"], writes=[f"lntmp{idx}"], tag=f"varB{tagp}_{I}", cost=200)
            sch.op("scalar", lambda: nc.scalar.activation(out=rstd[:, 4 + idx:5 + idx], in_=lntmp[:, idx:idx + 1],
                                                          func=AF.Ln, bias=cst[:, 0:1], scale=cst[:, 3:4]),
                   reads=[f"lntmp{idx}", "cst"], writes=[f"rstdL{idx}"], tag=f"rstdA{tagp}_{I}", cost=200)
            sch.op("scalar", lambda: nc.scalar.activation(out=rstd[:, idx:idx + 1], in_=rstd[:, 4 + idx:5 + idx],
                                                          func=AF.Exp, scale=cst[:, 1:2]),
                   reads=[f"rstdL{idx}", "cst"], writes=[f"rstd{idx}"], tag=f"rstdB{tagp}_{I}", cost=200)
            sch.op("vector", lambda sc_=sc_: nc.vector.tensor_scalar(out=negmr[:, idx:idx + 1], in0=lnsum[:, sc_:sc_ + 1],
                                                             scalar1=rstd[:, idx:idx + 1], scalar2=-1.0 / D,
                                                             op0=ALU.mult, op1=ALU.mult),
                   reads=[sname, f"rstd{idx}"], writes=[f"negmr{idx}"], tag=f"negmr{tagp}_{I}", cost=200)

        def emit_front(I):
            p = I % 2

            emit_ln_stats(I, p, lambda p=p: x0[p][:], [f"x0{p}"], [], 0, "1", sumcol=4 + p)
            sch.op("vector", lambda p=p: nc.vector.tensor_scalar(out=xntm[:], in0=x0[p][:],
                                                                 scalar1=rstd[:, 0:1], scalar2=negmr[:, 0:1],
                                                                 op0=ALU.mult, op1=ALU.add),
                   reads=[f"x0{p}", "rstd0", "negmr0"], writes=["xntm"], tag=f"ln1ap_{I}", cost=700)
            tap("xntm", I, lambda: xntm[:], "xntm")

            sch.op("tensor", lambda: [nc.tensor.transpose(PSB[:, 6144 + h * 128:6272 + h * 128],
                                                          xntm[:, h * 128:(h + 1) * 128], idn[:])
                                      for h in range(NH)][-1],
                   reads=["xntm", "idn"], pw=[(3072 + h * 64, 3136 + h * 64) for h in range(NH)], tag=f"xnT_{I}", cost=2600)
            sch.op("vector", lambda p=p: nc.vector.tensor_copy(out=xnfm[p][:], in_=PSB[:, 6144:7168]),
                   writes=[f"xnfm{p}"], pr=[(3072, 3584)], tag=f"evxn_{I}", cost=700)
            tap("xnfm", I, lambda p=p: xnfm[p][:], f"xnfm{p}")

            def qkmm(base, wa, wb, h, pp):
                c0 = base + h * 256
                nc.tensor.matmul(PS[:, c0:c0 + 128], wa[:, h, :], xnfm[pp][:, h, :],
                                 start=(h % 2 == 0), stop=False)
                return nc.tensor.matmul(PS[:, c0 + 128:c0 + 256], wb[:, h, :], xnfm[pp][:, h, :],
                                        start=False, stop=(h % 2 == 1))
            sch.op("tensor", lambda p=p, qkmm=qkmm: [qkmm(0, wq, wk, h, p) for h in range(NH)][-1],
                   reads=[f"xnfm{p}", "wq", "wk"], pw=[(0, 2048)], tag=f"qk_{I}", cost=3400)
            sch.op("tensor", lambda p=p, qkmm=qkmm: [qkmm(2048, wqr, wkr, h, p) for h in range(NH)][-1],
                   reads=[f"xnfm{p}", "wqr", "wkr"], pw=[(2048, 4096)], tag=f"qkr_{I}", cost=3400)

            for g in range(2):
                qv = PS[:, g * 1024:(g + 1) * 1024].rearrange("p (h s t) -> p h s t", h=4, s=2)
                rv = PS[:, 2048 + g * 1024:3072 + g * 1024].rearrange("p (h s t) -> p h s t", h=4, s=2)
                cosb = tabs[p][:, 0, :, :].unsqueeze(1).broadcast_to([128, 4, 2, C])
                sinb = tabs[p][:, 1, :, :].unsqueeze(1).broadcast_to([128, 4, 2, C])
                t1v = scr[:, p, 0:1024].rearrange("p (h s t) -> p h s t", h=4, s=2)
                t2v = scr[:, p, 1024:2048].rearrange("p (h s t) -> p h s t", h=4, s=2)
                sch.op("vector", lambda qv=qv, cosb=cosb, t1v=t1v:
                       nc.vector.tensor_tensor(out=t1v, in0=qv, in1=cosb, op=ALU.mult),
                       reads=[f"tab{p}"], writes=[f"sA{p}"], pr=[(g * 1024, (g + 1) * 1024)], tag=f"t1_{I}_{g}", cost=1300)
                sch.op("vector", lambda rv=rv, sinb=sinb, t2v=t2v:
                       nc.vector.tensor_tensor(out=t2v, in0=rv, in1=sinb, op=ALU.mult),
                       reads=[f"tab{p}"], writes=[f"sB{p}"], pr=[(2048 + g * 1024, 3072 + g * 1024)], tag=f"t2_{I}_{g}", cost=1300)
                sch.op("vector", lambda g=g, p=p: nc.vector.tensor_tensor(
                           out=qkbf[:, 4 * g:4 * g + 4, :, :],
                           in0=scr[:, p, 0:1024], in1=scr[:, p, 1024:2048], op=ALU.add),
                       reads=[f"sA{p}", f"sB{p}"], writes=["qkbf"], tag=f"qkadd_{I}_{g}", cost=1300)
            tap("qkbf", I, lambda: qkbf[:], "qkbf")

            sch.op("gpsimd", lambda p=p: nc.gpsimd.tensor_tensor(out=qtb[p][:], in0=qkbf[:, :, 0, :],
                                                                 in1=gq[:], op=ALU.mult),
                   reads=["qkbf", "gq"], writes=[f"qtb{p}"], tag=f"qt_{I}", cost=2300)
            sch.op("gpsimd", lambda: nc.gpsimd.tensor_tensor(out=ktb[:], in0=qkbf[:, :, 1, :],
                                                             in1=gk[:], op=ALU.mult),
                   reads=["qkbf", "gk"], writes=["ktb"], tag=f"kt_{I}", cost=2300)

            sch.op("tensor", lambda p=p: [nc.tensor.matmul(PS[:, h * 128:(h + 1) * 128], xnfm[p][:, h, :],
                                                           wv[:, h, :], start=True, stop=True)
                                          for h in range(NH)][-1],
                   reads=[f"xnfm{p}", "wv"], pw=[(0, 1024)], tag=f"v_{I}", cost=1400)
            sch.op("vector", lambda p=p: nc.vector.tensor_copy(out=vbf[p][:], in_=PS[:, 0:1024]),
                   writes=[f"vbf{p}"], pr=[(0, 1024)], tag=f"evv_{I}", cost=1300)
            tap("vbf", I, lambda p=p: vbf[p][:], f"vbf{p}")

            sch.op("tensor", lambda: [nc.tensor.transpose(PSB[:, 6144 + h * 128:6272 + h * 128],
                                                          ktb[:, h, :], idn[:]) for h in range(NH)][-1],
                   reads=["ktb", "idn"], pw=[(3072 + h * 64, 3136 + h * 64) for h in range(NH)], tag=f"ktT_{I}", cost=2600)
            sch.op("vector", lambda p=p: nc.vector.tensor_copy(out=kttm[p][:], in_=PSB[:, 6144:7168]),
                   writes=[f"kttm{p}"], pr=[(3072, 3584)], tag=f"evkt_{I}", cost=700)

            sch.op("tensor", lambda: [nc.tensor.matmul(PS[:, sacol(h):sacol(h) + 128],
                                                       qkbf[:, h, 1, :], qkbf[:, h, 0, :],
                                                       start=True, stop=True) for h in range(NH)][-1],
                   reads=["qkbf"], pw=[(sacol(h), sacol(h) + 128) for h in range(NH)], tag=f"s_{I}", cost=1400)
            for h in range(NH):
                sch.op("vector", lambda h=h, p=p: nc.vector.tensor_tensor(
                           out=stb[p][:, h, :], in0=PS[:, sacol(h):sacol(h) + 128], in1=dloc[:, h, :], op=ALU.mult),
                       reads=["dloc"], writes=[f"stb{p}h{h}"], pr=[(sacol(h), sacol(h) + 128)], tag=f"st_{I}_{h}", cost=350)
            tap("stb", I, lambda p=p: stb[p][:], [f"stb{p}h{h}" for h in range(NH)])

        def emit_midtail(I):
            p = I % 2

            for q in range(4):
                h0, h1 = 2 * q, 2 * q + 1
                for h in (h0, h1):
                    sch.op("tensor", lambda h=h, p=p, I=I: nc.tensor.matmul(
                               PS[:, ycol(h):ycol(h) + 128], stb[p][:, h, :], vbf[p][:, h, :],
                               start=True, stop=(I == 0)),
                           reads=[f"stb{p}h{h}", f"vbf{p}"], pw=[(ycol(h), ycol(h) + 128)], tag=f"yi_{I}_{h}", cost=250, grp=("oc" if I == 0 else "o"))
                for h in (h0, h1):
                    sch.op("tensor", lambda h=h, p=p: nc.tensor.matmul(
                               PS[:, sacol(h):sacol(h) + 128], kttm[p][:, h, :], vbf[p][:, h, :],
                               start=True, stop=True),
                           reads=[f"kttm{p}", f"vbf{p}"], pw=[(sacol(h), sacol(h) + 128)], tag=f"a_{I}_{h}", cost=250)
                sch.op("tensor", lambda q=q, p=p: (
                           nc.tensor.matmul(PS[:, 3072:3584], xnfm[p][:, 2 * q, :],
                                            wg[:, 2 * q, 0:512], start=(q == 0), stop=False),
                           nc.tensor.matmul(PS[:, 3584:4096], xnfm[p][:, 2 * q, :],
                                            wg[:, 2 * q, 512:1024], start=(q == 0), stop=False))[-1],
                       reads=[f"xnfm{p}", "wg"], pw=[(3072, 4096)], tag=f"g_{I}_{2 * q}", cost=600, grp=("o" if q == 0 else "-"))
                if I > 0:
                    for h in (h0, h1):
                        sch.op("tensor", lambda h=h, p=p: nc.tensor.matmul(
                                   PS[:, ycol(h):ycol(h) + 128], qtb[p][:, h, :], Rbf[:, h, :],
                                   start=False, stop=True),
                               reads=[f"qtb{p}", "Rbf"], pw=[(ycol(h), ycol(h) + 128)], tag=f"yc_{I}_{h}", cost=250, grp="c")
                sch.op("tensor", lambda q=q, p=p: (
                           nc.tensor.matmul(PS[:, 3072:3584], xnfm[p][:, 2 * q + 1, :],
                                            wg[:, 2 * q + 1, 0:512], start=False, stop=(q == 3)),
                           nc.tensor.matmul(PS[:, 3584:4096], xnfm[p][:, 2 * q + 1, :],
                                            wg[:, 2 * q + 1, 512:1024], start=False, stop=(q == 3)))[-1],
                       reads=[f"xnfm{p}", "wg"], pw=[(3072, 4096)], tag=f"g_{I}_{2 * q + 1}", cost=600, grp=("c" if q == 3 else "-"))

            for h in range(NH):
                sch.op("vector", lambda h=h: nc.vector.scalar_tensor_tensor(
                           out=Rst[:, h, :], in0=Rst[:, h, :], scalar=gc_list[h],
                           in1=PS[:, sacol(h):sacol(h) + 128], op0=ALU.mult, op1=ALU.add),
                       reads=[f"Rst{h}"], writes=[f"Rst{h}"], pr=[(sacol(h), sacol(h) + 128)], tag=f"rup_{I}_{h}", cost=350)
            sch.op("vector", lambda: nc.vector.tensor_copy(out=Rbf[:], in_=Rst[:]),
                   reads=[f"Rst{h}" for h in range(NH)], writes=["Rbf"], tag=f"rbf_{I}", cost=700)

            for h in range(NH):
                sch.op("vector", lambda h=h: nc.vector.bn_stats(out=st6g[:, h, :], in_=PS[:, ycol(h):ycol(h) + 128]),
                       writes=[f"st6g{h}", "st6g7"] if h == NH - 1 else [f"st6g{h}"],
                       pr=[(ycol(h), ycol(h) + 128)], tag=f"bng_{I}_{h}", cost=350)
            allst = [f"st6g{h}" for h in range(NH)]
            sch.op("vector", lambda: nc.vector.tensor_tensor(out=gtmp[:, 0, :], in0=st6g[:, :, 1],
                                                             in1=st6g[:, :, 4], op=ALU.add),
                   reads=allst, writes=["gtA"], tag=f"gnc1_{I}")
            sch.op("vector", lambda: nc.vector.tensor_tensor(out=gtmp[:, 1, :], in0=st6g[:, :, 1],
                                                             in1=st6g[:, :, 4], op=ALU.subtract),
                   reads=allst, writes=["gtD"], tag=f"gnc2_{I}")
            sch.op("vector", lambda: nc.vector.tensor_tensor(out=gtmp[:, 2, :], in0=gtmp[:, 1, :],
                                                             in1=gtmp[:, 1, :], op=ALU.mult),
                   reads=["gtD"], writes=["gtDD"], tag=f"gnc3_{I}")
            sch.op("vector", lambda: nc.vector.tensor_tensor(out=gtmp[:, 3, :], in0=st6g[:, :, 2],
                                                             in1=st6g[:, :, 5], op=ALU.add),
                   reads=allst, writes=["gtS"], tag=f"gnc4_{I}")
            sch.op("vector", lambda: nc.vector.scalar_tensor_tensor(out=gtmp[:, 4, :], in0=gtmp[:, 3, :],
                                                                    scalar=4.0 / HD, in1=gtmp[:, 2, :],
                                                                    op0=ALU.mult, op1=ALU.add),
                   reads=["gtS", "gtDD"], writes=["gtV"], tag=f"gnc5_{I}")
            sch.op("scalar", lambda: nc.scalar.activation(out=rstdg[:], in_=gtmp[:, 4, :], func=AF.Ln,
                                                          bias=cst[:, 0:1], scale=cst[:, 2:3]),
                   reads=["gtV", "cst"], writes=["rstdg"], tag=f"rstdgA_{I}")
            sch.op("scalar", lambda: nc.scalar.activation(out=rstdg[:], in_=rstdg[:], func=AF.Exp, scale=cst[:, 1:2]),
                   reads=["cst", "rstdg"], writes=["rstdg"], tag=f"rstdgB_{I}")
            sch.op("vector", lambda: nc.vector.scalar_tensor_tensor(out=negmrg[:], in0=gtmp[:, 0, :], scalar=-0.5,
                                                                    in1=rstdg[:], op0=ALU.mult, op1=ALU.mult),
                   reads=["gtA", "rstdg"], writes=["negmrg"], tag=f"negmrg_{I}")

            for h in range(NH):
                sch.op("vector", lambda h=h, p=p: nc.vector.tensor_scalar(
                           out=scr[:, p, h * 128:(h + 1) * 128], in0=PS[:, ycol(h):ycol(h) + 128],
                           scalar1=rstdg[:, h:h + 1], scalar2=negmrg[:, h:h + 1], op0=ALU.mult, op1=ALU.add),
                       reads=["rstdg", "negmrg"] + ([f"sA{p}"] if h == 0 else []),
                       writes=[f"sA{p}" if h == 0 else f"sA{p}q{h}"],
                       pr=[(ycol(h), ycol(h) + 128)], tag=f"gnap_{I}_{h}", cost=350)
            sch.op("scalar", lambda p=p: nc.scalar.activation(out=scr[:, p, 1024:2048], in_=PS[:, 3072:4096],
                                                              func=AF.Silu),
                   writes=[f"sB{p}"], pr=[(3072, 4096)], tag=f"silu_{I}", cost=1100)
            sch.op("gpsimd", lambda p=p: nc.gpsimd.tensor_tensor(out=scr[:, p, 2048:3072], in0=scr[:, p, 0:1024],
                                                                 in1=gnwbc[:], op=ALU.mult),
                   reads=[f"sA{p}", "gnwbc"] + [f"sA{p}q{h}" for h in range(1, NH)],
                   writes=[f"sC{p}"], tag=f"gnw_{I}", cost=2300)
            sch.op("gpsimd", lambda p=p: nc.gpsimd.tensor_tensor(out=scr[:, p, 0:1024], in0=scr[:, p, 2048:3072],
                                                                 in1=gnbbc[:], op=ALU.add),
                   reads=[f"sC{p}", "gnbbc"],
                   writes=[f"sA{p}"] + [f"sA{p}q{h}" for h in range(1, NH)], tag=f"gnb_{I}", cost=2300)
            sch.op("vector", lambda p=p: nc.vector.tensor_tensor(out=hbf[:], in0=scr[:, p, 0:1024],
                                                                 in1=scr[:, p, 1024:2048], op=ALU.mult),
                   reads=[f"sA{p}", f"sB{p}"], writes=["hbf"], tag=f"gate_{I}", cost=1200)
            tap("hbf", I, lambda: hbf[:], "hbf")

            sch.op("tensor", lambda: [nc.tensor.transpose(PSB[:, 2048 + h * 128:2176 + h * 128],
                                                          hbf[:, h, :], idn[:]) for h in range(NH)][-1],
                   reads=["hbf", "idn"], pw=[(1024 + h * 64, 1088 + h * 64) for h in range(NH)], tag=f"hT_{I}", cost=2600)
            sch.op("vector", lambda: nc.vector.tensor_copy(out=hfm[:], in_=PSB[:, 2048:3072]),
                   writes=["hfm"], pr=[(1024, 1536)], tag=f"evh_{I}", cost=700)
            for hb in range(NH):
                sch.op("tensor", lambda hb=hb: (nc.tensor.matmul(PS[:, 0:512], hfm[:, hb, :], wo[:, hb, 0:512],
                                                                 start=(hb == 0), stop=(hb == NH - 1)),
                                                nc.tensor.matmul(PS[:, 512:1024], hfm[:, hb, :], wo[:, hb, 512:1024],
                                                                 start=(hb == 0), stop=(hb == NH - 1)))[-1],
                       reads=["hfm", "wo"], pw=[(0, 1024)], tag=f"wo_{I}_{hb}", cost=600, grp=("o" if hb == 0 else "c" if hb == NH - 1 else "-"))
            sch.op("tensor", lambda: [nc.tensor.matmul(PS[:, 1024:1025], hfm[:, hb, :], wors[:, hb:hb + 1],
                                                       start=(hb == 0), stop=(hb == NH - 1))
                                      for hb in range(NH)][-1],
                   reads=["hfm", "wors"], pw=[(1024, 1025)], tag=f"wosum_{I}", cost=400)

            emit_ln_stats(I, p, lambda: PS[:, 0:1024], [], [(0, 1024)], 1, "2",
                          sum_src=(lambda: PS[:, 1024:1025], [], [(1024, 1025)]))
            sch.op("vector", lambda p=p: nc.vector.tensor_scalar(out=scr[:, p, 2048:3072], in0=PS[:, 0:1024],
                                                                 scalar1=rstd[:, 1:2], scalar2=negmr[:, 1:2],
                                                                 op0=ALU.mult, op1=ALU.add),
                   reads=["rstd1", "negmr1"], writes=[f"sC{p}"], pr=[(0, 1024)], tag=f"ln2ap_{I}", cost=1300)
            sch.op("gpsimd", lambda p=p: nc.gpsimd.tensor_tensor(out=x1[p][:], in0=scr[:, p, 2048:3072],
                                                                 in1=x0[p][:], op=ALU.add),
                   reads=[f"sC{p}", f"x0{p}"], writes=[f"x1{p}"], tag=f"resid1_{I}", cost=2300)
            tap("x1", I, lambda p=p: x1[p][:], f"x1{p}")

            emit_ln_stats(I, p, lambda p=p: x1[p][:], [f"x1{p}"], [], 2, "3",
                          sum_src=(lambda p=p: lnsum[:, 4 + p:5 + p], [f"lnsum{4 + p}"], []))
            sch.op("vector", lambda p=p: nc.vector.tensor_scalar(out=xn2tm[:], in0=x1[p][:],
                                                                 scalar1=rstd[:, 2:3], scalar2=negmr[:, 2:3],
                                                                 op0=ALU.mult, op1=ALU.add),
                   reads=[f"x1{p}", "rstd2", "negmr2"], writes=["xntm"], tag=f"ln3ap_{I}", cost=700)
            sch.op("tensor", lambda: [nc.tensor.transpose(PSB[:, 3072 + h * 128:3200 + h * 128],
                                                          xn2tm[:, h * 128:(h + 1) * 128], idn[:])
                                      for h in range(NH)][-1],
                   reads=["xntm", "idn"], pw=[(1536 + h * 64, 1600 + h * 64) for h in range(NH)], tag=f"xn2T_{I}", cost=2600)
            sch.op("vector", lambda: nc.vector.tensor_copy(out=xn2fm[:], in_=PSB[:, 3072:4096]),
                   writes=["xn2fm"], pr=[(1536, 2048)], tag=f"evxn2_{I}", cost=700)
            tap("xn2fm", I, lambda: xn2fm[:], "xn2fm")

            for k in range(8):
                base = 1024 + (k % 2) * 512
                sch.op("tensor", lambda k=k, base=base: (
                           nc.tensor.matmul(PS[:, base:base + 128], u1[:, k, 0:128], xn2fm[:, k, :],
                                            start=(k < 2), stop=False),
                           nc.tensor.matmul(PS[:, base + 128:base + 256], u1[:, k, 128:256], xn2fm[:, k, :],
                                            start=False, stop=(k >= 6)))[-1],
                       reads=["xn2fm", "u1"], pw=[(base, base + 256)], tag=f"p_{I}_{k}", cost=350, grp=("o" if k < 2 else "c" if k >= 6 else "-"))
            sch.op("vector", lambda: nc.vector.tensor_copy(out=qftmp[:], in_=PS[:, 1024:1280]),
                   writes=["qftmp"], pr=[(1024, 1280)], tag=f"pev1_{I}", cost=450)
            sch.op("vector", lambda: nc.vector.tensor_tensor(out=pbf[:], in0=qftmp[:],
                                                             in1=PS[:, 1536:1792], op=ALU.add),
                   reads=["qftmp"], writes=["pbf"], pr=[(1536, 1792)], tag=f"pev2_{I}", cost=450)
            tap("pbf", I, lambda: pbf[:], "pbf")

            qf_cols = (0, 512, 1024, 1536)
            for half in range(2):
                fbs = list(range(half * 16, half * 16 + 16))
                for rc in range(2):
                    sch.op("tensor", lambda rc=rc, fbs=fbs: [
                               nc.tensor.matmul(PS[:, 2048 + (fb % 16) * 128:2176 + (fb % 16) * 128],
                                                v1[:, rc, fb * 128:(fb + 1) * 128], pbf[:, rc, :],
                                                start=(rc == 0 and fb % 4 == 0),
                                                stop=False) for fb in fbs][-1],
                           reads=["pbf", "v1"], pw=[(2048, 4096)], tag=f"h_{I}_{half}_{rc}", cost=2900, grp=("o" if rc == 0 else "-"))
                sch.op("tensor", lambda fbs=fbs: [
                           nc.tensor.matmul(PS[:, 2048 + (fb % 16) * 128:2176 + (fb % 16) * 128],
                                            b1r[0:1, fb * 128:(fb + 1) * 128], onesb[:],
                                            start=False, stop=(fb % 4 == 3)) for fb in fbs][-1],
                       reads=["b1r", "ones"], pw=[(2048, 4096)], tag=f"hb1_{I}_{half}", cost=900, grp="c")
                sch.op("scalar", lambda: nc.scalar.activation(out=Hbf[:], in_=PS[:, 2048:4096],
                                                              func=AF.Gelu_apprx_tanh),
                       writes=["Hbf"], pr=[(2048, 4096)], tag=f"gelu_{I}_{half}", cost=1900)
                for fb in fbs:
                    c0 = qf_cols[fb % 4]
                    sch.op("tensor", lambda fb=fb, c0=c0: (
                               nc.tensor.matmul(PS[:, c0:c0 + 128], u2[:, fb, 0:128], Hbf[:, fb % 16, :],
                                                start=(fb < 4), stop=False),
                               nc.tensor.matmul(PS[:, c0 + 128:c0 + 256], u2[:, fb, 128:256], Hbf[:, fb % 16, :],
                                                start=False, stop=(fb >= 28)))[-1],
                           reads=["Hbf", "u2"], pw=[(c0, c0 + 256)], tag=f"u2_{I}_{fb}", cost=400, grp=("o" if fb < 4 else "c" if fb >= 28 else "-"))

            sch.op("vector", lambda: nc.vector.tensor_copy(out=qftmp[:], in_=PS[:, 0:256]),
                   writes=["qftmp"], pr=[(0, 256)], tag=f"qfc1_{I}", cost=450)
            sch.op("vector", lambda: nc.vector.tensor_tensor(out=qftmp[:], in0=qftmp[:], in1=PS[:, 512:768], op=ALU.add),
                   reads=["qftmp"], writes=["qftmp"], pr=[(512, 768)], tag=f"qfc2_{I}", cost=450)
            sch.op("vector", lambda: nc.vector.tensor_tensor(out=qftmp[:], in0=qftmp[:], in1=PS[:, 1024:1280], op=ALU.add),
                   reads=["qftmp"], writes=["qftmp"], pr=[(1024, 1280)], tag=f"qfc3_{I}", cost=450)
            sch.op("vector", lambda: nc.vector.tensor_tensor(out=qfbf[:], in0=qftmp[:], in1=PS[:, 1536:1792], op=ALU.add),
                   reads=["qftmp"], writes=["qfbf"], pr=[(1536, 1792)], tag=f"qfc4_{I}", cost=450)
            tap("qfbf", I, lambda: qfbf[:], "qfbf")

            sch.op("tensor", lambda: (nc.tensor.matmul(PS[:, 3072:3584], qfbf[:, 0, :], v2[:, 0, 0:512],
                                                       start=True, stop=False),
                                      nc.tensor.matmul(PS[:, 3584:4096], qfbf[:, 0, :], v2[:, 0, 512:1024],
                                                       start=True, stop=False))[-1],
                   reads=["qfbf", "v2"], pw=[(3072, 4096)], tag=f"v2a_{I}", cost=500, grp="o")
            sch.op("tensor", lambda: (nc.tensor.matmul(PS[:, 3072:3584], qfbf[:, 1, :], v2[:, 1, 0:512],
                                                       start=False, stop=False),
                                      nc.tensor.matmul(PS[:, 3584:4096], qfbf[:, 1, :], v2[:, 1, 512:1024],
                                                       start=False, stop=False))[-1],
                   reads=["qfbf", "v2"], pw=[(3072, 4096)], tag=f"v2b_{I}", cost=500, grp="-")
            sch.op("tensor", lambda: (nc.tensor.matmul(PS[:, 3072:3584], onesb[:], b2r[0:1, 0:512],
                                                       start=False, stop=True),
                                      nc.tensor.matmul(PS[:, 3584:4096], onesb[:], b2r[0:1, 512:1024],
                                                       start=False, stop=True))[-1],
                   reads=["ones", "b2r"], pw=[(3072, 4096)], tag=f"b2_{I}", cost=500, grp="c")
            sch.op("tensor", lambda: (nc.tensor.matmul(PS[:, 1152:1153], qfbf[:, 0, :], v2rs[:, 0:1],
                                                       start=True, stop=False),
                                      nc.tensor.matmul(PS[:, 1152:1153], qfbf[:, 1, :], v2rs[:, 1:2],
                                                       start=False, stop=False),
                                      nc.tensor.matmul(PS[:, 1152:1153], onesb[:], b2s[:],
                                                       start=False, stop=True))[-1],
                   reads=["qfbf", "v2rs", "b2s", "ones"], pw=[(1152, 1153)], tag=f"ffnsum_{I}", cost=300)

            emit_ln_stats(I, p, lambda: PS[:, 3072:4096], [], [(3072, 4096)], 3, "4",
                          sum_src=(lambda: PS[:, 1152:1153], [], [(1152, 1153)]))
            sch.op("vector", lambda p=p: nc.vector.tensor_scalar(out=scr[:, p, 2048:3072], in0=PS[:, 3072:4096],
                                                                 scalar1=rstd[:, 3:4], scalar2=negmr[:, 3:4],
                                                                 op0=ALU.mult, op1=ALU.add),
                   reads=["rstd3", "negmr3"], writes=[f"sC{p}"], pr=[(3072, 4096)], tag=f"ln4ap_{I}", cost=1300)
            sch.op("gpsimd", lambda p=p: nc.gpsimd.tensor_tensor(out=outb[:], in0=scr[:, p, 2048:3072],
                                                                 in1=x1[p][:], op=ALU.add),
                   reads=[f"sC{p}", f"x1{p}"], writes=["out"], tag=f"resid2_{I}", cost=2300)
            sch.op("scalar", lambda I=I: nc.scalar.dma_start(out=y_h[I * C:(I + 1) * C, :], in_=outb[:]),
                   reads=["out"], dma=True, tag=f"store_{I}", cost=3500)

        for I in range(NCH):
            emit_front(I)
            if I > 0:
                emit_midtail(I - 1)
            if I + 1 < NCH:
                emit_loads(I + 1)
        emit_midtail(NCH - 1)

        block = E(nc.Block())
        sch.run(block)

    return nc, dbg_names


def kernel(L0_attn_X, L0_attn_W_Q, L0_attn_W_K, L0_attn_W_V, L0_attn_W_G, L0_attn_W_O,
           L0_attn_gn_weight, L0_attn_gn_bias, L0_ffn_U1, L0_ffn_V1, L0_ffn_b1,
           L0_ffn_U2, L0_ffn_V2, L0_ffn_b2):
    consts, gc_list = host_constants()
    nc, _ = build_nc(gc_list)
    common = dict(
        wq=np.ascontiguousarray(L0_attn_W_Q, np.float32),
        wk=np.ascontiguousarray(L0_attn_W_K, np.float32),
        wv=np.ascontiguousarray(L0_attn_W_V, np.float32),
        wg=np.ascontiguousarray(L0_attn_W_G, np.float32),
        wo=np.ascontiguousarray(L0_attn_W_O, np.float32),
        gnw=np.ascontiguousarray(L0_attn_gn_weight, np.float32),
        gnb=np.ascontiguousarray(L0_attn_gn_bias, np.float32),
        u1=np.ascontiguousarray(L0_ffn_U1, np.float32),
        v1=np.ascontiguousarray(L0_ffn_V1, np.float32),
        b1=np.ascontiguousarray(L0_ffn_b1, np.float32),
        u2=np.ascontiguousarray(L0_ffn_U2, np.float32),
        v2=np.ascontiguousarray(L0_ffn_V2, np.float32),
        b2=np.ascontiguousarray(L0_ffn_b2, np.float32),
        **consts,
    )
    X = np.ascontiguousarray(L0_attn_X, np.float32)
    in_maps = [dict(common, x=np.ascontiguousarray(X[c])) for c in range(8)]
    res = run_bass_kernel_spmd(nc, in_maps, core_ids=list(range(8)))
    return np.stack([np.asarray(r["y"], np.float32) for r in res.results], axis=0)


# version 14 (final)
# speedup vs baseline: 1.3600x; 1.0117x over previous
import math
from contextlib import ExitStack

import numpy as np
import ml_dtypes

import concourse.bass as bass
import concourse.mybir as mybir
from concourse.bass_utils import run_bass_kernel_spmd

F32 = mybir.dt.float32
BF16 = mybir.dt.bfloat16
AF = mybir.ActivationFunctionType
ALU = mybir.AluOpType

S, D, NH, HD, C = 4096, 1024, 8, 128, 128
NCH = S // C
EPS = 1e-5


def host_constants():
    gam = 1.0 - np.exp(np.linspace(math.log(1.0 / 32.0), math.log(1.0 / 512.0), NH))
    ii = np.arange(C)
    diff = ii[None, :] - ii[:, None]
    dloc = np.where(diff[:, :, None] >= 0,
                    gam[None, None, :] ** np.maximum(diff, 0)[:, :, None], 0.0)
    DLOC = np.ascontiguousarray(dloc.transpose(0, 2, 1)).astype(np.float32)
    GQ = np.ascontiguousarray(np.broadcast_to((gam[:, None] ** (ii[None, :] + 1))[None],
                                              (128, NH, C))).astype(np.float32)
    GK = np.ascontiguousarray(np.broadcast_to((gam[:, None] ** (C - 1 - ii[None, :]))[None],
                                              (128, NH, C))).astype(np.float32)
    GC = [float(np.float32(gam[h] ** C)) for h in range(NH)]

    half = HD // 2
    scale = (np.arange(0, HD, 2, dtype=np.float64) + 0.4 * HD) / (1.4 * HD)
    pos = np.arange(S, dtype=np.float64)
    sc = scale[None, :] ** (pos[:, None] / 512.0)
    inv_freq = 1.0 / 10000.0 ** (np.arange(half, dtype=np.float64) / half)
    sinus = pos[:, None] * inv_freq[None, :]
    sin, cos = np.sin(sinus), np.cos(sinus)

    def rep2(a):
        return np.repeat(a, 2, axis=-1)

    cosQ, sinQ = rep2(cos * sc), rep2(sin * sc)
    cosK, sinK = rep2(cos / sc), rep2(sin / sc)
    TAB = np.empty((NCH, 128, 2, 2, C), np.float32)
    for ci in range(NCH):
        sl = slice(ci * C, (ci + 1) * C)
        TAB[ci, :, 0, 0, :] = cosQ[sl].T
        TAB[ci, :, 1, 0, :] = sinQ[sl].T
        TAB[ci, :, 0, 1, :] = cosK[sl].T
        TAB[ci, :, 1, 1, :] = sinK[sl].T

    P = np.zeros((128, 128), np.float64)
    for m in range(64):
        P[2 * m, 2 * m + 1] = -1.0
        P[2 * m + 1, 2 * m] = 1.0
    PT = np.ascontiguousarray(P.T).astype(ml_dtypes.bfloat16)
    IDN = np.eye(128, dtype=ml_dtypes.bfloat16)
    IDNF = np.eye(128, dtype=np.float32)
    ONES = np.ones((1, 128), ml_dtypes.bfloat16)
    return dict(TAB=TAB, DLOC=DLOC, GQ=GQ, GK=GK, PT=PT, IDN=IDN, IDNF=IDNF, ONES=ONES), GC


class Sched:

    ENG_COST = {"tensor": 800, "vector": 1100, "scalar": 900, "gpsimd": 2200, "sync": 3000}
    SEM_LAT = 250.0

    def __init__(self, nc, sems):
        self.nc = nc
        self.sems = sems
        self.ops = []
        self.sb_w = {}
        self.sb_r = {}
        self.ps_w_bytes = []
        self.bank_reads = {}
        self.g_open = {}
        self.g_close = {}
        self.g_members = {}
        self.g_state = {}

    @staticmethod
    def banks(lo, hi):
        return range(lo // 512, (hi + 511) // 512)

    def op(self, eng, fn, reads=(), writes=(), pr=(), pw=(), dma=False, tag="", cost=None, grp="oc"):
        idx = len(self.ops)
        deps = set()

        def need(j):
            if j is not None and j != idx:
                deps.add(j)

        for b in reads:
            need(self.sb_w.get(b))
        for b in writes:
            need(self.sb_w.get(b))
            for j in self.sb_r.get(b, ()):
                need(j)
        for (lo, hi) in pr:
            for bk in self.banks(lo, hi):
                need(self.g_close.get(bk))
                if self.g_state.get(bk) == "open":
                    for m in self.g_members.get(bk, ()):
                        need(m)
                for (j, e2) in self.bank_reads.get(bk, ()):
                    if e2 != eng:
                        need(j)
        opbanks = set()
        for (lo, hi) in pw:
            assert eng == "tensor", f"only PE writes PSUM (op {tag})"
            for (l2, h2, j) in self.ps_w_bytes:
                if l2 < hi and lo < h2:
                    need(j)
            for bk in self.banks(lo, hi):
                opbanks.add(bk)
        for bk in opbanks:
            if grp in ("o", "oc"):
                need(self.g_close.get(bk))
                for (j, e2) in self.bank_reads.get(bk, ()):
                    need(j)
                self.bank_reads[bk] = []
                self.g_open[bk] = idx
                self.g_members[bk] = [idx]
                self.g_state[bk] = "open"
            else:
                need(self.g_open.get(bk))
                for (j, e2) in self.bank_reads.get(bk, ()):
                    need(j)
                self.bank_reads[bk] = []
                self.g_members.setdefault(bk, []).append(idx)
            if grp in ("c", "oc"):
                for m in self.g_members.get(bk, ()):
                    need(m)
                self.g_close[bk] = idx
                self.g_state[bk] = "closed"

        for b in reads:
            self.sb_r.setdefault(b, []).append(idx)
        for b in writes:
            self.sb_w[b] = idx
            self.sb_r[b] = []
        for (lo, hi) in pr:
            for bk in self.banks(lo, hi):
                self.bank_reads.setdefault(bk, []).append((idx, eng))
        for (lo, hi) in pw:
            self.ps_w_bytes = [e for e in self.ps_w_bytes if not (lo <= e[0] and e[1] <= hi)]
            self.ps_w_bytes.append((lo, hi, idx))

        self.ops.append(dict(idx=idx, eng=eng, fn=fn, dma=dma, tag=tag,
                             cost=float(cost if cost is not None else self.ENG_COST[eng]),
                             deps=deps))
        return idx

    def _order(self):
        import heapq
        n = len(self.ops)
        children = [[] for _ in range(n)]
        indeg = [0] * n
        for o in self.ops:
            for d in o["deps"]:
                children[d].append(o["idx"])
                indeg[o["idx"]] += 1
        uprank = [0.0] * n
        for i in range(n - 1, -1, -1):
            c = self.ops[i]["cost"]
            m = 0.0
            for ch in children[i]:
                if uprank[ch] > m:
                    m = uprank[ch]
            uprank[i] = c + m
        rdy = [0.0] * n
        eng_avail = {}
        heap = []
        for i in range(n):
            if indeg[i] == 0:
                heapq.heappush(heap, (0.0, -uprank[i], i))
        order = []
        while heap:
            cand = []
            best = None
            while heap:
                t, nr, i = heapq.heappop(heap)
                eng = self.ops[i]["eng"]
                est = max(t, eng_avail.get(eng, 0.0))
                cand.append((est, nr, i, t))
                if best is None or (est, nr) < best[:2]:
                    best = (est, nr, i, t)
                if len(cand) >= 24:
                    break
            for c in cand:
                if c[2] != best[2]:
                    heapq.heappush(heap, (c[3], c[1], c[2]))
            est, _, i, _ = best
            o = self.ops[i]
            f = est + o["cost"]
            eng_avail[o["eng"]] = f
            o["finish"] = f
            order.append(i)
            for ch in children[i]:
                lat = self.SEM_LAT if self.ops[ch]["eng"] != o["eng"] else 100.0
                r = f + lat
                if r > rdy[ch]:
                    rdy[ch] = r
                indeg[ch] -= 1
                if indeg[ch] == 0:
                    heapq.heappush(heap, (rdy[ch], -uprank[ch], ch))
        assert len(order) == n, f"cycle: scheduled {len(order)} of {n}"
        return order

    def run(self, block):
        order = self._order()
        cnt = {k: 0 for k in self.sems}
        tok = {}
        sem_of = {}
        last_dma = {}
        waits_per = {i: [] for i in range(len(self.ops))}
        waited = {}
        for i in order:
            o = self.ops[i]
            sem = ("d_" if o["dma"] else "m_") + o["eng"]
            deps = {}
            for d in o["deps"]:
                s, v = sem_of[d], tok[d]
                if v > deps.get(s, 0):
                    deps[s] = v
            if o["dma"] and cnt[sem] > 0:
                deps[sem] = max(deps.get(sem, 0), cnt[sem])
            for s, v in sorted(deps.items()):
                if waited.get((o["eng"], s), 0) < v:
                    waits_per[i].append((s, v))
                    waited[(o["eng"], s)] = v
            cnt[sem] += 16 if o["dma"] else 1
            tok[i] = cnt[sem]
            sem_of[i] = sem
        self.cnt = cnt
        per_eng = {e: [i for i in order if self.ops[i]["eng"] == e]
                   for e in ("sync", "scalar", "tensor", "vector", "gpsimd")}

        for eng in ("sync", "scalar", "tensor", "vector", "gpsimd"):
            idxs = per_eng[eng]

            def make(idxs=idxs, eng=eng):
                def go(e):
                    for i in idxs:
                        o = self.ops[i]
                        for (s, v) in waits_per[i]:
                            e.wait_ge(self.sems[s], v)
                        inst = o["fn"]()
                        inst.then_inc(self.sems[("d_" if o["dma"] else "m_") + eng], 16 if o["dma"] else 1)
                    if eng == "scalar":
                        e.wait_ge(self.sems["d_scalar"], self.cnt["d_scalar"])
                return go

            getattr(block, eng)(make())


def build_nc(gc_list, dbg=None):
    dbg = dbg or set()
    nc = bass.Bass("TRN2", target_bir_lowering=False)

    def din(name, shape, dtype=F32):
        return nc.dram_tensor(name, shape, dtype, kind="ExternalInput").ap()

    x_h = din("x", [S, D])
    wq_h = din("wq", [NH, HD, HD]); wk_h = din("wk", [NH, HD, HD]); wv_h = din("wv", [NH, HD, HD])
    wg_h = din("wg", [D, D]); wo_h = din("wo", [D, D])
    gnw_h = din("gnw", [D]); gnb_h = din("gnb", [D])
    u1_h = din("u1", [D, 256]); v1_h = din("v1", [256, 4096]); b1_h = din("b1", [4096])
    u2_h = din("u2", [4096, 256]); v2_h = din("v2", [256, D]); b2_h = din("b2", [D])
    tab_h = din("TAB", [NCH, 128, 2, 2, C])
    dloc_h = din("DLOC", [128, NH, C]); gq_h = din("GQ", [128, NH, C]); gk_h = din("GK", [128, NH, C])
    pt_h = din("PT", [128, 128], BF16); idn_h = din("IDN", [128, 128], BF16)
    ones_h = din("ONES", [1, 128], BF16)
    y_h = nc.dram_tensor("y", [S, D], F32, kind="ExternalOutput").ap()

    dbg_names = []

    with ExitStack() as st:
        E = st.enter_context
        wq = E(nc.sbuf_tensor("wq_s", [128, NH, HD], BF16))
        wk = E(nc.sbuf_tensor("wk_s", [128, NH, HD], BF16))
        wv = E(nc.sbuf_tensor("wv_s", [128, NH, HD], BF16))
        wqr = E(nc.sbuf_tensor("wqr_s", [128, NH, HD], BF16))
        wkr = E(nc.sbuf_tensor("wkr_s", [128, NH, HD], BF16))
        wg = E(nc.sbuf_tensor("wg_s", [128, 8, D], BF16))
        wo = E(nc.sbuf_tensor("wo_s", [128, 8, D], BF16))
        u1 = E(nc.sbuf_tensor("u1_s", [128, 8, 256], BF16))
        v1 = E(nc.sbuf_tensor("v1_s", [128, 2, 4096], BF16))
        u2 = E(nc.sbuf_tensor("u2_s", [128, 32, 256], BF16))
        v2 = E(nc.sbuf_tensor("v2_s", [128, 2, D], BF16))
        b2r = E(nc.sbuf_tensor("b2r_s", [1, D], BF16))
        onesb = E(nc.sbuf_tensor("ones_s", [1, 128], BF16))
        gnwbc = E(nc.sbuf_tensor("gnwbc", [128, D], F32))
        gnbbc = E(nc.sbuf_tensor("gnbbc", [128, D], F32))
        b1r = E(nc.sbuf_tensor("b1r_s", [1, 4096], BF16))
        dloc = E(nc.sbuf_tensor("dloc", [128, NH, C], F32))
        gq = E(nc.sbuf_tensor("gq", [128, NH, C], F32))
        gk = E(nc.sbuf_tensor("gk", [128, NH, C], F32))
        idn = E(nc.sbuf_tensor("idn", [128, 128], BF16))
        pt = E(nc.sbuf_tensor("pt", [128, 128], BF16))

        x0 = [E(nc.sbuf_tensor(f"x0_{p}", [128, D], F32)) for p in range(2)]
        tabs = [E(nc.sbuf_tensor(f"tab_{p}", [128, 2, 2, C], F32)) for p in range(2)]
        xntm = E(nc.sbuf_tensor("xntm", [128, D], BF16))
        xn2tm = E(nc.sbuf_tensor("xn2tm", [128, D], BF16))
        lnst = E(nc.sbuf_tensor("lnst", [128, 8, 16], F32))
        xnfm = [E(nc.sbuf_tensor(f"xnfm_{pp}", [128, NH, C], BF16)) for pp in range(2)]
        qkbf = E(nc.sbuf_tensor("qkbf", [128, NH, 2, C], BF16))
        qtb = [E(nc.sbuf_tensor(f"qtb_{pp}", [128, NH, C], BF16)) for pp in range(2)]
        ktb = E(nc.sbuf_tensor("ktb", [128, NH, C], BF16))
        kttm = [E(nc.sbuf_tensor(f"kttm_{pp}", [128, NH, C], BF16)) for pp in range(2)]
        vbf = [E(nc.sbuf_tensor(f"vbf_{pp}", [128, NH, C], BF16)) for pp in range(2)]
        stb = [E(nc.sbuf_tensor(f"stb_{pp}", [128, NH, C], BF16)) for pp in range(2)]
        Rst = E(nc.sbuf_tensor("Rst", [128, NH, HD], F32))
        Rbf = E(nc.sbuf_tensor("Rbf", [128, NH, HD], BF16))
        scr = E(nc.sbuf_tensor("scr", [128, 2, 3072], F32))
        hbf = E(nc.sbuf_tensor("hbf", [128, NH, C], BF16))
        x1 = [E(nc.sbuf_tensor(f"x1_{p}", [128, D], F32)) for p in range(2)]
        xn2fm = E(nc.sbuf_tensor("xn2fm", [128, NH, C], BF16))
        pbf = E(nc.sbuf_tensor("pbf", [128, 2, 128], BF16))
        Hbf = E(nc.sbuf_tensor("Hbf", [128, 16, 128], BF16))
        outb = E(nc.sbuf_tensor("out_s", [128, D], F32))
        st6g = E(nc.sbuf_tensor("st6g", [128, NH, 6], F32))
        gtmp = E(nc.sbuf_tensor("gtmp", [128, 5, NH], F32))
        rstd = E(nc.sbuf_tensor("rstd", [128, 8], F32))
        negmr = E(nc.sbuf_tensor("negmr", [128, 8], F32))
        rstdg = E(nc.sbuf_tensor("rstdg", [128, NH], F32))
        cst = E(nc.sbuf_tensor("cst", [128, 4], F32))
        negmrg = E(nc.sbuf_tensor("negmrg", [128, NH], F32))

        wtt = scr[:, 0, 0:64].bitcast(BF16)
        PS = E(nc.psum_tensor("PS", [128, 4096], F32))
        PSB = PS[:].bitcast(BF16)

        sems = {}
        for name in ("m_tensor", "m_vector", "m_scalar", "m_gpsimd", "d_sync", "d_scalar", "d_gpsimd"):
            sems[name] = E(nc.semaphore(name))
        sch = Sched(nc, sems)

        def tap(stage, I, ap_fn, bufname):
            if (stage, I) in dbg:
                ap = ap_fn()
                nm = f"dbg_{stage}_{I}"
                t = nc.dram_tensor(nm, list(ap.shape), ap.dtype, kind="ExternalOutput").ap()
                dbg_names.append(nm)
                names = [bufname] if isinstance(bufname, str) else list(bufname)
                sch.op("sync", lambda ap=ap, t=t: nc.sync.dma_start(out=t, in_=ap),
                       reads=names, dma=True, tag=f"tap_{stage}_{I}")

        for fn, wr, tg in (
            (lambda: nc.gpsimd.dma_start(out=wq[:], in_=wq_h.rearrange("h d e -> d h e")), ["wq"], "ld_wq"),
            (lambda: nc.gpsimd.dma_start(out=wk[:], in_=wk_h.rearrange("h d e -> d h e")), ["wk"], "ld_wk"),
            (lambda: nc.gpsimd.dma_start(out=wv[:], in_=wv_h.rearrange("h d e -> d h e")), ["wv"], "ld_wv"),
            (lambda: nc.gpsimd.dma_start(out=wg[:], in_=wg_h.rearrange("(b p) o -> p b o", p=128)), ["wg"], "ld_wg"),
            (lambda: nc.gpsimd.dma_start(out=wo[:], in_=wo_h.rearrange("(b p) o -> p b o", p=128)), ["wo"], "ld_wo"),
            (lambda: nc.gpsimd.dma_start(out=u1[:], in_=u1_h.rearrange("(b p) r -> p b r", p=128)), ["u1"], "ld_u1"),
            (lambda: nc.gpsimd.dma_start(out=v1[:], in_=v1_h.rearrange("(b p) f -> p b f", p=128)), ["v1"], "ld_v1"),
            (lambda: nc.gpsimd.dma_start(out=u2[:], in_=u2_h.rearrange("(b p) r -> p b r", p=128)), ["u2"], "ld_u2"),
            (lambda: nc.gpsimd.dma_start(out=v2[:], in_=v2_h.rearrange("(b p) o -> p b o", p=128)), ["v2"], "ld_v2"),
            (lambda: nc.gpsimd.dma_start(out=b1r[:], in_=b1_h.unsqueeze(0)), ["b1r"], "ld_b1"),
            (lambda: nc.gpsimd.dma_start(out=b2r[:], in_=b2_h.unsqueeze(0)), ["b2r"], "ld_b2"),
            (lambda: nc.gpsimd.dma_start(out=onesb[:], in_=ones_h), ["ones"], "ld_ones"),
        ):
            sch.op("gpsimd", fn, writes=wr, dma=True, tag=tg)

        for fn, wr, tg in (
            (lambda: nc.sync.dma_start(out=gnwbc[:], in_=gnw_h.unsqueeze(0).broadcast_to([128, D])), ["gnwbc"], "ld_gnw"),
            (lambda: nc.sync.dma_start(out=gnbbc[:], in_=gnb_h.unsqueeze(0).broadcast_to([128, D])), ["gnbbc"], "ld_gnb"),
            (lambda: nc.sync.dma_start(out=dloc[:], in_=dloc_h), ["dloc"], "ld_dloc"),
            (lambda: nc.sync.dma_start(out=gq[:], in_=gq_h), ["gq"], "ld_gq"),
            (lambda: nc.sync.dma_start(out=gk[:], in_=gk_h), ["gk"], "ld_gk"),
            (lambda: nc.sync.dma_start(out=idn[:], in_=idn_h), ["idn"], "ld_idn"),
            (lambda: nc.sync.dma_start(out=pt[:], in_=pt_h), ["pt"], "ld_pt"),
        ):
            sch.op("sync", fn, writes=wr, dma=True, tag=tg)

        for (src, dst, sname, dname) in ((wq, wqr, "wq", "wqr"), (wk, wkr, "wk", "wkr")):
            for h in range(NH):
                sch.op("tensor", lambda src=src, h=h: nc.tensor.transpose(PSB[:, 0:128], src[:, h, :], idn[:]),
                       reads=[sname, "idn"], pw=[(0, 64)], tag=f"T_{dname}{h}")
                sch.op("vector", lambda: nc.vector.tensor_copy(out=wtt, in_=PSB[:, 0:128]),
                       writes=["sA0"], pr=[(0, 64)], tag=f"ev_{dname}{h}")
                sch.op("tensor", lambda: nc.tensor.matmul(PS[:, 512:640], wtt, pt[:], start=True, stop=True),
                       reads=["sA0", "pt"], pw=[(512, 640)], tag=f"mm_{dname}{h}")
                sch.op("vector", lambda dst=dst, h=h: nc.vector.tensor_copy(out=dst[:, h, :], in_=PS[:, 512:640]),
                       writes=[dname], pr=[(512, 640)], tag=f"ev2_{dname}{h}")

        tap("wqr", 0, lambda: wqr[:], "wqr")
        tap("wkr", 0, lambda: wkr[:], "wkr")
        sch.op("vector", lambda: nc.vector.memset(Rst[:], 0.0),
               writes=[f"Rst{h}" for h in range(NH)], tag="memset_R")
        sch.op("vector", lambda: (nc.vector.memset(cst[:, 0:1], EPS),
                                  nc.vector.memset(cst[:, 1:2], -0.5),
                                  nc.vector.memset(cst[:, 2:3], 0.25),
                                  nc.vector.memset(cst[:, 3:4], 1.0))[-1],
               writes=["cst"], tag="memset_cst")

        def emit_loads(I):
            p = I % 2
            sch.op("sync", lambda I=I, p=p: nc.sync.dma_start(out=x0[p][:], in_=x_h[I * C:(I + 1) * C, :]),
                   writes=[f"x0{p}"], dma=True, tag=f"ld_x0_{I}", cost=3500)
            sch.op("sync", lambda I=I, p=p: nc.sync.dma_start(out=tabs[p][:], in_=tab_h[I]),
                   writes=[f"tab{p}"], dma=True, tag=f"ld_tab_{I}", cost=2800)

        emit_loads(0)

        def sacol(h):
            return 1024 + (h % 2) * 512 + (h // 2) * 128

        def ycol(h):
            return 2048 + (h % 2) * 512 + (h // 2) * 128

        def emit_ln_stats(I, p, srcfn, src_reads, src_prs, idx, tagp, extra_rd=()):
            s = idx + 4 * p
            for k in range(2):
                sch.op("vector", lambda k=k: nc.vector.bn_stats(out=lnst[:, s, 6 * k:6 * k + 6], in_=srcfn(k)),
                       reads=src_reads, writes=[f"lnst{s}{k}"], pr=src_prs[k] if src_prs else [],
                       tag=f"bn{k}{tagp}_{I}", cost=420)
            sch.op("vector", lambda: nc.vector.bn_aggr(out=lnst[:, s, 12:14], in_=lnst[:, s, 0:12]),
                   reads=[f"lnst{s}0", f"lnst{s}1"], writes=[f"lnagg{s}"], tag=f"agg{tagp}_{I}", cost=180)
            sch.op("scalar", lambda: nc.scalar.activation(out=lnst[:, s, 14:15], in_=lnst[:, s, 13:14],
                                                          func=AF.Ln, bias=cst[:, 0:1], scale=cst[:, 3:4]),
                   reads=[f"lnagg{s}", "cst"] + list(extra_rd), writes=[f"rstdL{s}"], tag=f"rstdA{tagp}_{I}", cost=380)
            sch.op("scalar", lambda: nc.scalar.activation(out=rstd[:, s:s + 1], in_=lnst[:, s, 14:15],
                                                          func=AF.Exp, scale=cst[:, 1:2]),
                   reads=[f"rstdL{s}", "cst"], writes=[f"rstd{s}"], tag=f"rstdB{tagp}_{I}", cost=380)
            sch.op("vector", lambda: nc.vector.tensor_scalar(out=negmr[:, s:s + 1], in0=lnst[:, s, 12:13],
                                                             scalar1=rstd[:, s:s + 1], scalar2=-1.0,
                                                             op0=ALU.mult, op1=ALU.mult),
                   reads=[f"lnagg{s}", f"rstd{s}"], writes=[f"negmr{s}"], tag=f"negmr{tagp}_{I}", cost=150)

        def emit_front(I):
            p = I % 2

            s1 = 0 + 4 * p
            emit_ln_stats(I, p, lambda k, p=p: x0[p][:, 512 * k:512 * k + 512], [f"x0{p}"], None, 0, "1")
            sch.op("vector", lambda p=p, s1=s1: nc.vector.tensor_scalar(out=xntm[:], in0=x0[p][:],
                                                                 scalar1=rstd[:, s1:s1 + 1], scalar2=negmr[:, s1:s1 + 1],
                                                                 op0=ALU.mult, op1=ALU.add),
                   reads=[f"x0{p}", f"rstd{s1}", f"negmr{s1}"], writes=["xntm"], tag=f"ln1ap_{I}", cost=700)
            tap("xntm", I, lambda: xntm[:], "xntm")

            sch.op("tensor", lambda: [nc.tensor.transpose(PSB[:, 6144 + h * 128:6272 + h * 128],
                                                          xntm[:, h * 128:(h + 1) * 128], idn[:])
                                      for h in range(NH)][-1],
                   reads=["xntm", "idn"], pw=[(3072 + h * 64, 3136 + h * 64) for h in range(NH)], tag=f"xnT_{I}", cost=620)
            sch.op("scalar", lambda p=p: nc.scalar.activation(out=xnfm[p][:], in_=PSB[:, 6144:7168],
                                                              func=AF.Copy),
                   writes=[f"xnfm{p}"], pr=[(3072, 3584)], tag=f"evxn_{I}", cost=1100)
            tap("xnfm", I, lambda p=p: xnfm[p][:], f"xnfm{p}")

            def qkmm(base, wa, wb, h, pp):
                c0 = base + h * 256
                nc.tensor.matmul(PS[:, c0:c0 + 128], wa[:, h, :], xnfm[pp][:, h, :],
                                 start=(h % 2 == 0), stop=False)
                return nc.tensor.matmul(PS[:, c0 + 128:c0 + 256], wb[:, h, :], xnfm[pp][:, h, :],
                                        start=False, stop=(h % 2 == 1))
            sch.op("tensor", lambda p=p, qkmm=qkmm: [qkmm(0, wq, wk, h, p) for h in range(NH)][-1],
                   reads=[f"xnfm{p}", "wq", "wk"], pw=[(0, 2048)], tag=f"qk_{I}", cost=1950)
            sch.op("tensor", lambda p=p, qkmm=qkmm: [qkmm(2048, wqr, wkr, h, p) for h in range(NH)][-1],
                   reads=[f"xnfm{p}", "wqr", "wkr"], pw=[(2048, 4096)], tag=f"qkr_{I}", cost=1950)

            for g in range(2):
                qv = PS[:, g * 1024:(g + 1) * 1024].rearrange("p (h s t) -> p h s t", h=4, s=2)
                rv = PS[:, 2048 + g * 1024:3072 + g * 1024].rearrange("p (h s t) -> p h s t", h=4, s=2)
                cosb = tabs[p][:, 0, :, :].unsqueeze(1).broadcast_to([128, 4, 2, C])
                sinb = tabs[p][:, 1, :, :].unsqueeze(1).broadcast_to([128, 4, 2, C])
                t1v = scr[:, p, 0:1024].rearrange("p (h s t) -> p h s t", h=4, s=2)
                t2v = scr[:, p, 1024:2048].rearrange("p (h s t) -> p h s t", h=4, s=2)
                sch.op("vector", lambda qv=qv, cosb=cosb, t1v=t1v:
                       nc.vector.tensor_tensor(out=t1v, in0=qv, in1=cosb, op=ALU.mult),
                       reads=[f"tab{p}"], writes=[f"sA{p}"], pr=[(g * 1024, (g + 1) * 1024)], tag=f"t1_{I}_{g}", cost=1300)
                sch.op("vector", lambda rv=rv, sinb=sinb, t2v=t2v:
                       nc.vector.tensor_tensor(out=t2v, in0=rv, in1=sinb, op=ALU.mult),
                       reads=[f"tab{p}"], writes=[f"sB{p}"], pr=[(2048 + g * 1024, 3072 + g * 1024)], tag=f"t2_{I}_{g}", cost=1300)
                sch.op("vector", lambda g=g, p=p: nc.vector.tensor_tensor(
                           out=qkbf[:, 4 * g:4 * g + 4, :, :],
                           in0=scr[:, p, 0:1024], in1=scr[:, p, 1024:2048], op=ALU.add),
                       reads=[f"sA{p}", f"sB{p}"], writes=["qkbf"], tag=f"qkadd_{I}_{g}", cost=1300)
            tap("qkbf", I, lambda: qkbf[:], "qkbf")

            sch.op("gpsimd", lambda p=p: nc.gpsimd.tensor_tensor(out=qtb[p][:], in0=qkbf[:, :, 0, :],
                                                                 in1=gq[:], op=ALU.mult),
                   reads=["qkbf", "gq"], writes=[f"qtb{p}"], tag=f"qt_{I}", cost=2300)
            sch.op("vector", lambda: nc.vector.tensor_tensor(out=ktb[:], in0=qkbf[:, :, 1, :],
                                                             in1=gk[:], op=ALU.mult),
                   reads=["qkbf", "gk"], writes=["ktb"], tag=f"kt_{I}", cost=600)

            sch.op("tensor", lambda p=p: [nc.tensor.matmul(PS[:, h * 128:(h + 1) * 128], xnfm[p][:, h, :],
                                                           wv[:, h, :], start=True, stop=True)
                                          for h in range(NH)][-1],
                   reads=[f"xnfm{p}", "wv"], pw=[(0, 1024)], tag=f"v_{I}", cost=1000)
            sch.op("vector", lambda p=p: nc.vector.tensor_copy(out=vbf[p][:], in_=PS[:, 0:1024]),
                   writes=[f"vbf{p}"], pr=[(0, 1024)], tag=f"evv_{I}", cost=700)
            tap("vbf", I, lambda p=p: vbf[p][:], f"vbf{p}")

            sch.op("tensor", lambda: [nc.tensor.transpose(PSB[:, 6144 + h * 128:6272 + h * 128],
                                                          ktb[:, h, :], idn[:]) for h in range(NH)][-1],
                   reads=["ktb", "idn"], pw=[(3072 + h * 64, 3136 + h * 64) for h in range(NH)], tag=f"ktT_{I}", cost=620)
            sch.op("vector", lambda p=p: nc.vector.tensor_copy(out=kttm[p][:], in_=PSB[:, 6144:7168]),
                   writes=[f"kttm{p}"], pr=[(3072, 3584)], tag=f"evkt_{I}", cost=700)

            sch.op("tensor", lambda: [nc.tensor.matmul(PS[:, sacol(h):sacol(h) + 128],
                                                       qkbf[:, h, 1, :], qkbf[:, h, 0, :],
                                                       start=True, stop=True) for h in range(NH)][-1],
                   reads=["qkbf"], pw=[(sacol(h), sacol(h) + 128) for h in range(NH)], tag=f"s_{I}", cost=1000)
            for h in range(NH):
                sch.op("vector", lambda h=h, p=p: nc.vector.tensor_tensor(
                           out=stb[p][:, h, :], in0=PS[:, sacol(h):sacol(h) + 128], in1=dloc[:, h, :], op=ALU.mult),
                       reads=["dloc"], writes=[f"stb{p}h{h}"], pr=[(sacol(h), sacol(h) + 128)], tag=f"st_{I}_{h}", cost=200)
            tap("stb", I, lambda p=p: stb[p][:], [f"stb{p}h{h}" for h in range(NH)])

        def emit_midtail(I):
            p = I % 2

            for q in range(4):
                h0, h1 = 2 * q, 2 * q + 1
                for h in (h0, h1):
                    sch.op("tensor", lambda h=h, p=p, I=I: nc.tensor.matmul(
                               PS[:, ycol(h):ycol(h) + 128], stb[p][:, h, :], vbf[p][:, h, :],
                               start=True, stop=(I == 0)),
                           reads=[f"stb{p}h{h}", f"vbf{p}"], pw=[(ycol(h), ycol(h) + 128)], tag=f"yi_{I}_{h}", cost=150, grp=("oc" if I == 0 else "o"))
                for h in (h0, h1):
                    sch.op("tensor", lambda h=h, p=p: nc.tensor.matmul(
                               PS[:, sacol(h):sacol(h) + 128], kttm[p][:, h, :], vbf[p][:, h, :],
                               start=True, stop=True),
                           reads=[f"kttm{p}", f"vbf{p}"], pw=[(sacol(h), sacol(h) + 128)], tag=f"a_{I}_{h}", cost=150)
                sch.op("tensor", lambda q=q, p=p: (
                           nc.tensor.matmul(PS[:, 3072:3584], xnfm[p][:, 2 * q, :],
                                            wg[:, 2 * q, 0:512], start=(q == 0), stop=False),
                           nc.tensor.matmul(PS[:, 3584:4096], xnfm[p][:, 2 * q, :],
                                            wg[:, 2 * q, 512:1024], start=(q == 0), stop=False))[-1],
                       reads=[f"xnfm{p}", "wg"], pw=[(3072, 4096)], tag=f"g_{I}_{2 * q}", cost=950, grp=("o" if q == 0 else "-"))
                if I > 0:
                    for h in (h0, h1):
                        sch.op("tensor", lambda h=h, p=p: nc.tensor.matmul(
                                   PS[:, ycol(h):ycol(h) + 128], qtb[p][:, h, :], Rbf[:, h, :],
                                   start=False, stop=True),
                               reads=[f"qtb{p}", "Rbf"], pw=[(ycol(h), ycol(h) + 128)], tag=f"yc_{I}_{h}", cost=150, grp="c")
                sch.op("tensor", lambda q=q, p=p: (
                           nc.tensor.matmul(PS[:, 3072:3584], xnfm[p][:, 2 * q + 1, :],
                                            wg[:, 2 * q + 1, 0:512], start=False, stop=(q == 3)),
                           nc.tensor.matmul(PS[:, 3584:4096], xnfm[p][:, 2 * q + 1, :],
                                            wg[:, 2 * q + 1, 512:1024], start=False, stop=(q == 3)))[-1],
                       reads=[f"xnfm{p}", "wg"], pw=[(3072, 4096)], tag=f"g_{I}_{2 * q + 1}", cost=950, grp=("c" if q == 3 else "-"))

            sch.op("scalar", lambda p=p: nc.scalar.activation(out=scr[:, p, 1024:2048], in_=PS[:, 3072:4096],
                                                              func=AF.Silu),
                   writes=[f"sB{p}"], pr=[(3072, 4096)], tag=f"silu_{I}", cost=1100)
            sch.op("vector", lambda p=p: nc.vector.tensor_tensor(out=scr[:, p, 2048:3072], in0=scr[:, p, 1024:2048],
                                                                 in1=gnwbc[:], op=ALU.mult),
                   reads=[f"sB{p}", "gnwbc"], writes=[f"sC{p}"], tag=f"ga1_{I}", cost=600)
            sch.op("gpsimd", lambda p=p: nc.gpsimd.tensor_tensor(out=outb[:], in0=scr[:, p, 1024:2048],
                                                                 in1=gnbbc[:], op=ALU.mult),
                   reads=[f"sB{p}", "gnbbc"], writes=["out"], tag=f"ga2_{I}", cost=2300)

            for h in range(NH):
                sch.op("vector", lambda h=h: nc.vector.scalar_tensor_tensor(
                           out=Rst[:, h, :], in0=Rst[:, h, :], scalar=gc_list[h],
                           in1=PS[:, sacol(h):sacol(h) + 128], op0=ALU.mult, op1=ALU.add),
                       reads=[f"Rst{h}"], writes=[f"Rst{h}"], pr=[(sacol(h), sacol(h) + 128)], tag=f"rup_{I}_{h}", cost=180)
            sch.op("vector", lambda: nc.vector.tensor_copy(out=Rbf[:], in_=Rst[:]),
                   reads=[f"Rst{h}" for h in range(NH)], writes=["Rbf"], tag=f"rbf_{I}", cost=700)

            for h in range(NH):
                sch.op("vector", lambda h=h: nc.vector.bn_stats(out=st6g[:, h, :], in_=PS[:, ycol(h):ycol(h) + 128]),
                       writes=[f"st6g{h}", "st6g7"] if h == NH - 1 else [f"st6g{h}"],
                       pr=[(ycol(h), ycol(h) + 128)], tag=f"bng_{I}_{h}", cost=220)
            allst = [f"st6g{h}" for h in range(NH)]
            sch.op("vector", lambda: nc.vector.tensor_tensor(out=gtmp[:, 0, :], in0=st6g[:, :, 1],
                                                             in1=st6g[:, :, 4], op=ALU.add),
                   reads=allst, writes=["gtA"], tag=f"gnc1_{I}")
            sch.op("vector", lambda: nc.vector.tensor_tensor(out=gtmp[:, 1, :], in0=st6g[:, :, 1],
                                                             in1=st6g[:, :, 4], op=ALU.subtract),
                   reads=allst, writes=["gtD"], tag=f"gnc2_{I}")
            sch.op("vector", lambda: nc.vector.tensor_tensor(out=gtmp[:, 2, :], in0=gtmp[:, 1, :],
                                                             in1=gtmp[:, 1, :], op=ALU.mult),
                   reads=["gtD"], writes=["gtDD"], tag=f"gnc3_{I}")
            sch.op("vector", lambda: nc.vector.tensor_tensor(out=gtmp[:, 3, :], in0=st6g[:, :, 2],
                                                             in1=st6g[:, :, 5], op=ALU.add),
                   reads=allst, writes=["gtS"], tag=f"gnc4_{I}")
            sch.op("vector", lambda: nc.vector.scalar_tensor_tensor(out=gtmp[:, 4, :], in0=gtmp[:, 3, :],
                                                                    scalar=4.0 / HD, in1=gtmp[:, 2, :],
                                                                    op0=ALU.mult, op1=ALU.add),
                   reads=["gtS", "gtDD"], writes=["gtV"], tag=f"gnc5_{I}")
            sch.op("scalar", lambda: nc.scalar.activation(out=rstdg[:], in_=gtmp[:, 4, :], func=AF.Ln,
                                                          bias=cst[:, 0:1], scale=cst[:, 2:3]),
                   reads=["gtV", "cst"], writes=["rstdg"], tag=f"rstdgA_{I}")
            sch.op("scalar", lambda: nc.scalar.activation(out=rstdg[:], in_=rstdg[:], func=AF.Exp, scale=cst[:, 1:2]),
                   reads=["cst", "rstdg"], writes=["rstdg"], tag=f"rstdgB_{I}")
            sch.op("vector", lambda: nc.vector.scalar_tensor_tensor(out=negmrg[:], in0=gtmp[:, 0, :], scalar=-0.5,
                                                                    in1=rstdg[:], op0=ALU.mult, op1=ALU.mult),
                   reads=["gtA", "rstdg"], writes=["negmrg"], tag=f"negmrg_{I}")

            for h in range(NH):
                sch.op("vector", lambda h=h, p=p: nc.vector.tensor_scalar(
                           out=scr[:, p, h * 128:(h + 1) * 128], in0=PS[:, ycol(h):ycol(h) + 128],
                           scalar1=rstdg[:, h:h + 1], scalar2=negmrg[:, h:h + 1], op0=ALU.mult, op1=ALU.add),
                       reads=["rstdg", "negmrg"] + ([f"sA{p}"] if h == 0 else []),
                       writes=[f"sA{p}" if h == 0 else f"sA{p}q{h}"],
                       pr=[(ycol(h), ycol(h) + 128)], tag=f"gnap_{I}_{h}", cost=180)
            sch.op("vector", lambda p=p: nc.vector.tensor_tensor(out=scr[:, p, 1024:2048], in0=scr[:, p, 0:1024],
                                                                 in1=scr[:, p, 2048:3072], op=ALU.mult),
                   reads=[f"sA{p}", f"sC{p}"] + [f"sA{p}q{h}" for h in range(1, NH)],
                   writes=[f"sB{p}"], tag=f"gt1_{I}", cost=600)
            sch.op("vector", lambda p=p: nc.vector.tensor_tensor(out=hbf[:], in0=scr[:, p, 1024:2048],
                                                                 in1=outb[:], op=ALU.add),
                   reads=[f"sB{p}", "out"], writes=["hbf"], tag=f"gate_{I}", cost=600)
            tap("hbf", I, lambda: hbf[:], "hbf")

            sch.op("tensor", lambda: [nc.tensor.transpose(PSB[:, 2048 + h * 128:2176 + h * 128],
                                                          hbf[:, h, :], idn[:]) for h in range(NH)][-1],
                   reads=["hbf", "idn"], pw=[(1024 + h * 64, 1088 + h * 64) for h in range(NH)], tag=f"hT_{I}", cost=620)
            sch.op("vector", lambda p=p: nc.vector.tensor_copy(out=kttm[p][:], in_=PSB[:, 2048:3072]),
                   writes=[f"kttm{p}", "hfm"], pr=[(1024, 1536)], tag=f"evh_{I}", cost=700)
            for hb in range(NH):
                sch.op("tensor", lambda hb=hb, p=p: (nc.tensor.matmul(PS[:, 0:512], kttm[p][:, hb, :], wo[:, hb, 0:512],
                                                                 start=(hb == 0), stop=(hb == NH - 1)),
                                                nc.tensor.matmul(PS[:, 512:1024], kttm[p][:, hb, :], wo[:, hb, 512:1024],
                                                                 start=(hb == 0), stop=(hb == NH - 1)))[-1],
                       reads=["hfm", "wo"], pw=[(0, 1024)], tag=f"wo_{I}_{hb}", cost=950, grp=("o" if hb == 0 else "c" if hb == NH - 1 else "-"))

            s2 = 1 + 4 * p
            emit_ln_stats(I, p, lambda k: PS[:, 512 * k:512 * k + 512], [],
                          [[(512 * k, 512 * k + 512)] for k in range(2)], 1, "2")
            sch.op("vector", lambda p=p, s2=s2: nc.vector.tensor_scalar(out=scr[:, p, 2048:3072], in0=PS[:, 0:1024],
                                                                 scalar1=rstd[:, s2:s2 + 1], scalar2=negmr[:, s2:s2 + 1],
                                                                 op0=ALU.mult, op1=ALU.add),
                   reads=[f"rstd{s2}", f"negmr{s2}", f"sC{p}"], writes=[f"sC{p}"], pr=[(0, 1024)], tag=f"ln2ap_{I}", cost=1300)
            sch.op("vector", lambda p=p: nc.vector.tensor_tensor(out=x1[p][:], in0=scr[:, p, 2048:3072],
                                                                 in1=x0[p][:], op=ALU.add),
                   reads=[f"sC{p}", f"x0{p}"], writes=[f"x1{p}"], tag=f"resid1_{I}", cost=600)
            tap("x1", I, lambda p=p: x1[p][:], f"x1{p}")

            s3 = 2 + 4 * p
            emit_ln_stats(I, p, lambda k, p=p: x1[p][:, 512 * k:512 * k + 512], [f"x1{p}"], None, 2, "3")
            sch.op("vector", lambda p=p, s3=s3: nc.vector.tensor_scalar(out=xn2tm[:], in0=x1[p][:],
                                                                 scalar1=rstd[:, s3:s3 + 1], scalar2=negmr[:, s3:s3 + 1],
                                                                 op0=ALU.mult, op1=ALU.add),
                   reads=[f"x1{p}", f"rstd{s3}", f"negmr{s3}"], writes=["xn2tm"], tag=f"ln3ap_{I}", cost=700)
            sch.op("tensor", lambda: [nc.tensor.transpose(PSB[:, 3072 + h * 128:3200 + h * 128],
                                                          xn2tm[:, h * 128:(h + 1) * 128], idn[:])
                                      for h in range(NH)][-1],
                   reads=["xn2tm", "idn"], pw=[(1536 + h * 64, 1600 + h * 64) for h in range(NH)], tag=f"xn2T_{I}", cost=620)
            sch.op("scalar", lambda: nc.scalar.activation(out=xn2fm[:], in_=PSB[:, 3072:4096],
                                                          func=AF.Copy),
                   writes=["xn2fm"], pr=[(1536, 2048)], tag=f"evxn2_{I}", cost=1100)
            tap("xn2fm", I, lambda: xn2fm[:], "xn2fm")

            for k in range(8):
                base = 1024 + (k % 2) * 512
                sch.op("tensor", lambda k=k, base=base: (
                           nc.tensor.matmul(PS[:, base:base + 128], u1[:, k, 0:128], xn2fm[:, k, :],
                                            start=(k < 2), stop=False),
                           nc.tensor.matmul(PS[:, base + 128:base + 256], u1[:, k, 128:256], xn2fm[:, k, :],
                                            start=False, stop=(k >= 6)))[-1],
                       reads=["xn2fm", "u1"], pw=[(base, base + 256)], tag=f"p_{I}_{k}", cost=350, grp=("o" if k < 2 else "c" if k >= 6 else "-"))
            sch.op("vector", lambda p=p: nc.vector.tensor_copy(out=scr[:, p, 2048:2304], in_=PS[:, 1024:1280]),
                   writes=[f"sC{p}"], pr=[(1024, 1280)], tag=f"pev1_{I}", cost=450)
            sch.op("vector", lambda p=p: nc.vector.tensor_tensor(out=pbf[:], in0=scr[:, p, 2048:2304],
                                                             in1=PS[:, 1536:1792], op=ALU.add),
                   reads=[f"sC{p}"], writes=["pbf"], pr=[(1536, 1792)], tag=f"pev2_{I}", cost=450)
            tap("pbf", I, lambda: pbf[:], "pbf")

            qf_cols = (0, 512, 1024, 1536)
            for half in range(2):
                fbs = list(range(half * 16, half * 16 + 16))
                for rc in range(2):
                    sch.op("tensor", lambda rc=rc, fbs=fbs: [
                               nc.tensor.matmul(PS[:, 2048 + (fb % 16) * 128:2176 + (fb % 16) * 128],
                                                v1[:, rc, fb * 128:(fb + 1) * 128], pbf[:, rc, :],
                                                start=(rc == 0 and fb % 4 == 0),
                                                stop=False) for fb in fbs][-1],
                           reads=["pbf", "v1"], pw=[(2048, 4096)], tag=f"h_{I}_{half}_{rc}", cost=2900, grp=("o" if rc == 0 else "-"))
                sch.op("tensor", lambda fbs=fbs: [
                           nc.tensor.matmul(PS[:, 2048 + (fb % 16) * 128:2176 + (fb % 16) * 128],
                                            b1r[0:1, fb * 128:(fb + 1) * 128], onesb[:],
                                            start=False, stop=(fb % 4 == 3)) for fb in fbs][-1],
                       reads=["b1r", "ones"], pw=[(2048, 4096)], tag=f"hb1_{I}_{half}", cost=900, grp="c")
                sch.op("scalar", lambda: nc.scalar.activation(out=Hbf[:], in_=PS[:, 2048:4096],
                                                              func=AF.Gelu_apprx_tanh),
                       writes=["Hbf"], pr=[(2048, 4096)], tag=f"gelu_{I}_{half}", cost=1900)
                for fb in fbs:
                    c0 = qf_cols[fb % 4]
                    sch.op("tensor", lambda fb=fb, c0=c0: (
                               nc.tensor.matmul(PS[:, c0:c0 + 128], u2[:, fb, 0:128], Hbf[:, fb % 16, :],
                                                start=(fb < 4), stop=False),
                               nc.tensor.matmul(PS[:, c0 + 128:c0 + 256], u2[:, fb, 128:256], Hbf[:, fb % 16, :],
                                                start=False, stop=(fb >= 28)))[-1],
                           reads=["Hbf", "u2"], pw=[(c0, c0 + 256)], tag=f"u2_{I}_{fb}", cost=400, grp=("o" if fb < 4 else "c" if fb >= 28 else "-"))

            sch.op("vector", lambda p=p: nc.vector.tensor_copy(out=scr[:, p, 2048:2304], in_=PS[:, 0:256]),
                   writes=[f"sC{p}"], pr=[(0, 256)], tag=f"qfc1_{I}", cost=450)
            sch.op("vector", lambda p=p: nc.vector.tensor_tensor(out=scr[:, p, 2048:2304], in0=scr[:, p, 2048:2304], in1=PS[:, 512:768], op=ALU.add),
                   reads=[f"sC{p}"], writes=[f"sC{p}"], pr=[(512, 768)], tag=f"qfc2_{I}", cost=450)
            sch.op("vector", lambda p=p: nc.vector.tensor_tensor(out=scr[:, p, 2048:2304], in0=scr[:, p, 2048:2304], in1=PS[:, 1024:1280], op=ALU.add),
                   reads=[f"sC{p}"], writes=[f"sC{p}"], pr=[(1024, 1280)], tag=f"qfc3_{I}", cost=450)
            sch.op("vector", lambda p=p: nc.vector.tensor_tensor(out=pbf[:], in0=scr[:, p, 2048:2304], in1=PS[:, 1536:1792], op=ALU.add),
                   reads=[f"sC{p}"], writes=["pbf"], pr=[(1536, 1792)], tag=f"qfc4_{I}", cost=450)
            tap("qfbf", I, lambda: pbf[:], "pbf")

            sch.op("tensor", lambda: (nc.tensor.matmul(PS[:, 3072:3584], pbf[:, 0, :], v2[:, 0, 0:512],
                                                       start=True, stop=False),
                                      nc.tensor.matmul(PS[:, 3584:4096], pbf[:, 0, :], v2[:, 0, 512:1024],
                                                       start=True, stop=False))[-1],
                   reads=["pbf", "v2"], pw=[(3072, 4096)], tag=f"v2a_{I}", cost=950, grp="o")
            sch.op("tensor", lambda: (nc.tensor.matmul(PS[:, 3072:3584], pbf[:, 1, :], v2[:, 1, 0:512],
                                                       start=False, stop=False),
                                      nc.tensor.matmul(PS[:, 3584:4096], pbf[:, 1, :], v2[:, 1, 512:1024],
                                                       start=False, stop=False))[-1],
                   reads=["pbf", "v2"], pw=[(3072, 4096)], tag=f"v2b_{I}", cost=950, grp="-")
            sch.op("tensor", lambda: (nc.tensor.matmul(PS[:, 3072:3584], onesb[:], b2r[0:1, 0:512],
                                                       start=False, stop=True),
                                      nc.tensor.matmul(PS[:, 3584:4096], onesb[:], b2r[0:1, 512:1024],
                                                       start=False, stop=True))[-1],
                   reads=["ones", "b2r"], pw=[(3072, 4096)], tag=f"b2_{I}", cost=500, grp="c")
            s4 = 3 + 4 * p
            emit_ln_stats(I, p, lambda k: PS[:, 3072 + 512 * k:3072 + 512 * k + 512], [],
                          [[(3072 + 512 * k, 3072 + 512 * k + 512)] for k in range(2)], 3, "4")
            sch.op("vector", lambda p=p, s4=s4: nc.vector.tensor_scalar(out=scr[:, p, 2048:3072], in0=PS[:, 3072:4096],
                                                                 scalar1=rstd[:, s4:s4 + 1], scalar2=negmr[:, s4:s4 + 1],
                                                                 op0=ALU.mult, op1=ALU.add),
                   reads=[f"rstd{s4}", f"negmr{s4}", f"sC{p}"], writes=[f"sC{p}"], pr=[(3072, 4096)], tag=f"ln4ap_{I}", cost=1300)
            sch.op("gpsimd", lambda p=p: nc.gpsimd.tensor_tensor(out=outb[:], in0=scr[:, p, 2048:3072],
                                                                 in1=x1[p][:], op=ALU.add),
                   reads=[f"sC{p}", f"x1{p}"], writes=["out"], tag=f"resid2_{I}", cost=2300)
            sch.op("scalar", lambda I=I: nc.scalar.dma_start(out=y_h[I * C:(I + 1) * C, :], in_=outb[:]),
                   reads=["out"], dma=True, tag=f"store_{I}", cost=3500)

        for I in range(NCH):
            emit_front(I)
            if I > 0:
                emit_midtail(I - 1)
            if I + 1 < NCH:
                emit_loads(I + 1)
        emit_midtail(NCH - 1)

        block = E(nc.Block())
        sch.run(block)

    return nc, dbg_names


def kernel(L0_attn_X, L0_attn_W_Q, L0_attn_W_K, L0_attn_W_V, L0_attn_W_G, L0_attn_W_O,
           L0_attn_gn_weight, L0_attn_gn_bias, L0_ffn_U1, L0_ffn_V1, L0_ffn_b1,
           L0_ffn_U2, L0_ffn_V2, L0_ffn_b2):
    consts, gc_list = host_constants()
    nc, _ = build_nc(gc_list)
    common = dict(
        wq=np.ascontiguousarray(L0_attn_W_Q, np.float32),
        wk=np.ascontiguousarray(L0_attn_W_K, np.float32),
        wv=np.ascontiguousarray(L0_attn_W_V, np.float32),
        wg=np.ascontiguousarray(L0_attn_W_G, np.float32),
        wo=np.ascontiguousarray(L0_attn_W_O, np.float32),
        gnw=np.ascontiguousarray(L0_attn_gn_weight, np.float32),
        gnb=np.ascontiguousarray(L0_attn_gn_bias, np.float32),
        u1=np.ascontiguousarray(L0_ffn_U1, np.float32),
        v1=np.ascontiguousarray(L0_ffn_V1, np.float32),
        b1=np.ascontiguousarray(L0_ffn_b1, np.float32),
        u2=np.ascontiguousarray(L0_ffn_U2, np.float32),
        v2=np.ascontiguousarray(L0_ffn_V2, np.float32),
        b2=np.ascontiguousarray(L0_ffn_b2, np.float32),
        **consts,
    )
    X = np.ascontiguousarray(L0_attn_X, np.float32)
    in_maps = [dict(common, x=np.ascontiguousarray(X[c])) for c in range(8)]
    res = run_bass_kernel_spmd(nc, in_maps, core_ids=list(range(8)))
    return np.stack([np.asarray(r["y"], np.float32) for r in res.results], axis=0)


# version 15
# speedup vs baseline: 1.3622x; 1.0016x over previous
import math
from contextlib import ExitStack

import numpy as np
import ml_dtypes

import concourse.bass as bass
import concourse.mybir as mybir
from concourse.bass_utils import run_bass_kernel_spmd

F32 = mybir.dt.float32
BF16 = mybir.dt.bfloat16
AF = mybir.ActivationFunctionType
ALU = mybir.AluOpType

S, D, NH, HD, C = 4096, 1024, 8, 128, 128
NCH = S // C
EPS = 1e-5


def host_constants():
    gam = 1.0 - np.exp(np.linspace(math.log(1.0 / 32.0), math.log(1.0 / 512.0), NH))
    ii = np.arange(C)
    diff = ii[None, :] - ii[:, None]
    dloc = np.where(diff[:, :, None] >= 0,
                    gam[None, None, :] ** np.maximum(diff, 0)[:, :, None], 0.0)
    DLOC = np.ascontiguousarray(dloc.transpose(0, 2, 1)).astype(np.float32)
    GQ = np.ascontiguousarray(np.broadcast_to((gam[:, None] ** (ii[None, :] + 1))[None],
                                              (128, NH, C))).astype(np.float32)
    GK = np.ascontiguousarray(np.broadcast_to((gam[:, None] ** (C - 1 - ii[None, :]))[None],
                                              (128, NH, C))).astype(np.float32)
    GC = [float(np.float32(gam[h] ** C)) for h in range(NH)]

    half = HD // 2
    scale = (np.arange(0, HD, 2, dtype=np.float64) + 0.4 * HD) / (1.4 * HD)
    pos = np.arange(S, dtype=np.float64)
    sc = scale[None, :] ** (pos[:, None] / 512.0)
    inv_freq = 1.0 / 10000.0 ** (np.arange(half, dtype=np.float64) / half)
    sinus = pos[:, None] * inv_freq[None, :]
    sin, cos = np.sin(sinus), np.cos(sinus)

    def rep2(a):
        return np.repeat(a, 2, axis=-1)

    cosQ, sinQ = rep2(cos * sc), rep2(sin * sc)
    cosK, sinK = rep2(cos / sc), rep2(sin / sc)
    TAB = np.empty((NCH, 128, 2, 2, C), np.float32)
    for ci in range(NCH):
        sl = slice(ci * C, (ci + 1) * C)
        TAB[ci, :, 0, 0, :] = cosQ[sl].T
        TAB[ci, :, 1, 0, :] = sinQ[sl].T
        TAB[ci, :, 0, 1, :] = cosK[sl].T
        TAB[ci, :, 1, 1, :] = sinK[sl].T

    P = np.zeros((128, 128), np.float64)
    for m in range(64):
        P[2 * m, 2 * m + 1] = -1.0
        P[2 * m + 1, 2 * m] = 1.0
    PT = np.ascontiguousarray(P.T).astype(ml_dtypes.bfloat16)
    IDN = np.eye(128, dtype=ml_dtypes.bfloat16)
    IDNF = np.eye(128, dtype=np.float32)
    ONES = np.ones((1, 128), ml_dtypes.bfloat16)
    return dict(TAB=TAB, DLOC=DLOC, GQ=GQ, GK=GK, PT=PT, IDN=IDN, IDNF=IDNF, ONES=ONES), GC


class Sched:

    ENG_COST = {"tensor": 800, "vector": 1100, "scalar": 900, "gpsimd": 2200, "sync": 3000}
    SEM_LAT = 250.0

    def __init__(self, nc, sems):
        self.nc = nc
        self.sems = sems
        self.ops = []
        self.sb_w = {}
        self.sb_r = {}
        self.ps_w_bytes = []
        self.bank_reads = {}
        self.g_open = {}
        self.g_close = {}
        self.g_members = {}
        self.g_state = {}

    @staticmethod
    def banks(lo, hi):
        return range(lo // 512, (hi + 511) // 512)

    def op(self, eng, fn, reads=(), writes=(), pr=(), pw=(), dma=False, tag="", cost=None, grp="oc"):
        idx = len(self.ops)
        deps = set()

        def need(j):
            if j is not None and j != idx:
                deps.add(j)

        for b in reads:
            need(self.sb_w.get(b))
        for b in writes:
            need(self.sb_w.get(b))
            for j in self.sb_r.get(b, ()):
                need(j)
        for (lo, hi) in pr:
            for bk in self.banks(lo, hi):
                need(self.g_close.get(bk))
                if self.g_state.get(bk) == "open":
                    for m in self.g_members.get(bk, ()):
                        need(m)
                for (j, e2, l2, h2) in self.bank_reads.get(bk, ()):
                    if e2 != eng and l2 < hi and lo < h2:
                        need(j)
        opbanks = set()
        for (lo, hi) in pw:
            assert eng == "tensor", f"only PE writes PSUM (op {tag})"
            for (l2, h2, j) in self.ps_w_bytes:
                if l2 < hi and lo < h2:
                    need(j)
            for bk in self.banks(lo, hi):
                opbanks.add(bk)
        for (lo, hi) in pw:
            for bk in self.banks(lo, hi):
                if bk not in opbanks:
                    continue
                kept = []
                for ent in self.bank_reads.get(bk, ()):
                    j, e2, l2, h2 = ent
                    if l2 < hi and lo < h2:
                        need(j)
                        if lo <= l2 and h2 <= hi:
                            continue
                    kept.append(ent)
                self.bank_reads[bk] = kept
        for bk in opbanks:
            if grp in ("o", "oc"):
                need(self.g_close.get(bk))
                self.g_open[bk] = idx
                self.g_members[bk] = [idx]
                self.g_state[bk] = "open"
            else:
                need(self.g_open.get(bk))
                self.g_members.setdefault(bk, []).append(idx)
            if grp in ("c", "oc"):
                for m in self.g_members.get(bk, ()):
                    need(m)
                self.g_close[bk] = idx
                self.g_state[bk] = "closed"

        for b in reads:
            self.sb_r.setdefault(b, []).append(idx)
        for b in writes:
            self.sb_w[b] = idx
            self.sb_r[b] = []
        for (lo, hi) in pr:
            for bk in self.banks(lo, hi):
                self.bank_reads.setdefault(bk, []).append((idx, eng, lo, hi))
        for (lo, hi) in pw:
            self.ps_w_bytes = [e for e in self.ps_w_bytes if not (lo <= e[0] and e[1] <= hi)]
            self.ps_w_bytes.append((lo, hi, idx))

        self.ops.append(dict(idx=idx, eng=eng, fn=fn, dma=dma, tag=tag,
                             cost=float(cost if cost is not None else self.ENG_COST[eng]),
                             deps=deps))
        return idx

    def _order(self):
        import heapq
        n = len(self.ops)
        children = [[] for _ in range(n)]
        indeg = [0] * n
        for o in self.ops:
            for d in o["deps"]:
                children[d].append(o["idx"])
                indeg[o["idx"]] += 1
        uprank = [0.0] * n
        for i in range(n - 1, -1, -1):
            c = self.ops[i]["cost"]
            m = 0.0
            for ch in children[i]:
                if uprank[ch] > m:
                    m = uprank[ch]
            uprank[i] = c + m
        rdy = [0.0] * n
        eng_avail = {}
        heap = []
        for i in range(n):
            if indeg[i] == 0:
                heapq.heappush(heap, (0.0, -uprank[i], i))
        order = []
        while heap:
            cand = []
            best = None
            while heap:
                t, nr, i = heapq.heappop(heap)
                eng = self.ops[i]["eng"]
                est = max(t, eng_avail.get(eng, 0.0))
                cand.append((est, nr, i, t))
                if best is None or (est, nr) < best[:2]:
                    best = (est, nr, i, t)
                if len(cand) >= 24:
                    break
            for c in cand:
                if c[2] != best[2]:
                    heapq.heappush(heap, (c[3], c[1], c[2]))
            est, _, i, _ = best
            o = self.ops[i]
            f = est + o["cost"]
            eng_avail[o["eng"]] = f
            o["finish"] = f
            order.append(i)
            for ch in children[i]:
                lat = self.SEM_LAT if self.ops[ch]["eng"] != o["eng"] else 100.0
                r = f + lat
                if r > rdy[ch]:
                    rdy[ch] = r
                indeg[ch] -= 1
                if indeg[ch] == 0:
                    heapq.heappush(heap, (rdy[ch], -uprank[ch], ch))
        assert len(order) == n, f"cycle: scheduled {len(order)} of {n}"
        return order

    def run(self, block):
        order = self._order()
        cnt = {k: 0 for k in self.sems}
        tok = {}
        sem_of = {}
        last_dma = {}
        waits_per = {i: [] for i in range(len(self.ops))}
        waited = {}
        for i in order:
            o = self.ops[i]
            sem = ("d_" if o["dma"] else "m_") + o["eng"]
            deps = {}
            for d in o["deps"]:
                s, v = sem_of[d], tok[d]
                if v > deps.get(s, 0):
                    deps[s] = v
            if o["dma"] and cnt[sem] > 0:
                deps[sem] = max(deps.get(sem, 0), cnt[sem])
            for s, v in sorted(deps.items()):
                if waited.get((o["eng"], s), 0) < v:
                    waits_per[i].append((s, v))
                    waited[(o["eng"], s)] = v
            cnt[sem] += 16 if o["dma"] else 1
            tok[i] = cnt[sem]
            sem_of[i] = sem
        self.cnt = cnt
        per_eng = {e: [i for i in order if self.ops[i]["eng"] == e]
                   for e in ("sync", "scalar", "tensor", "vector", "gpsimd")}

        for eng in ("sync", "scalar", "tensor", "vector", "gpsimd"):
            idxs = per_eng[eng]

            def make(idxs=idxs, eng=eng):
                def go(e):
                    for i in idxs:
                        o = self.ops[i]
                        for (s, v) in waits_per[i]:
                            e.wait_ge(self.sems[s], v)
                        inst = o["fn"]()
                        inst.then_inc(self.sems[("d_" if o["dma"] else "m_") + eng], 16 if o["dma"] else 1)
                    if eng == "scalar":
                        e.wait_ge(self.sems["d_scalar"], self.cnt["d_scalar"])
                return go

            getattr(block, eng)(make())


def build_nc(gc_list, dbg=None):
    dbg = dbg or set()
    nc = bass.Bass("TRN2", target_bir_lowering=False)

    def din(name, shape, dtype=F32):
        return nc.dram_tensor(name, shape, dtype, kind="ExternalInput").ap()

    x_h = din("x", [S, D])
    wq_h = din("wq", [NH, HD, HD]); wk_h = din("wk", [NH, HD, HD]); wv_h = din("wv", [NH, HD, HD])
    wg_h = din("wg", [D, D]); wo_h = din("wo", [D, D])
    gnw_h = din("gnw", [D]); gnb_h = din("gnb", [D])
    u1_h = din("u1", [D, 256]); v1_h = din("v1", [256, 4096]); b1_h = din("b1", [4096])
    u2_h = din("u2", [4096, 256]); v2_h = din("v2", [256, D]); b2_h = din("b2", [D])
    tab_h = din("TAB", [NCH, 128, 2, 2, C])
    dloc_h = din("DLOC", [128, NH, C]); gq_h = din("GQ", [128, NH, C]); gk_h = din("GK", [128, NH, C])
    pt_h = din("PT", [128, 128], BF16); idn_h = din("IDN", [128, 128], BF16)
    ones_h = din("ONES", [1, 128], BF16)
    y_h = nc.dram_tensor("y", [S, D], F32, kind="ExternalOutput").ap()

    dbg_names = []

    with ExitStack() as st:
        E = st.enter_context
        wq = E(nc.sbuf_tensor("wq_s", [128, NH, HD], BF16))
        wk = E(nc.sbuf_tensor("wk_s", [128, NH, HD], BF16))
        wv = E(nc.sbuf_tensor("wv_s", [128, NH, HD], BF16))
        wqr = E(nc.sbuf_tensor("wqr_s", [128, NH, HD], BF16))
        wkr = E(nc.sbuf_tensor("wkr_s", [128, NH, HD], BF16))
        wg = E(nc.sbuf_tensor("wg_s", [128, 8, D], BF16))
        wo = E(nc.sbuf_tensor("wo_s", [128, 8, D], BF16))
        u1 = E(nc.sbuf_tensor("u1_s", [128, 8, 256], BF16))
        v1 = E(nc.sbuf_tensor("v1_s", [128, 2, 4096], BF16))
        u2 = E(nc.sbuf_tensor("u2_s", [128, 32, 256], BF16))
        v2 = E(nc.sbuf_tensor("v2_s", [128, 2, D], BF16))
        b2r = E(nc.sbuf_tensor("b2r_s", [1, D], BF16))
        onesb = E(nc.sbuf_tensor("ones_s", [1, 128], BF16))
        gnwbc = E(nc.sbuf_tensor("gnwbc", [128, D], F32))
        gnbbc = E(nc.sbuf_tensor("gnbbc", [128, D], F32))
        b1r = E(nc.sbuf_tensor("b1r_s", [1, 4096], BF16))
        dloc = E(nc.sbuf_tensor("dloc", [128, NH, C], F32))
        gq = E(nc.sbuf_tensor("gq", [128, NH, C], F32))
        gk = E(nc.sbuf_tensor("gk", [128, NH, C], F32))
        idn = E(nc.sbuf_tensor("idn", [128, 128], BF16))
        pt = E(nc.sbuf_tensor("pt", [128, 128], BF16))

        x0 = [E(nc.sbuf_tensor(f"x0_{p}", [128, D], F32)) for p in range(2)]
        tabs = [E(nc.sbuf_tensor(f"tab_{p}", [128, 2, 2, C], F32)) for p in range(2)]
        xntm = E(nc.sbuf_tensor("xntm", [128, D], BF16))
        xn2tm = E(nc.sbuf_tensor("xn2tm", [128, D], BF16))
        lnst = E(nc.sbuf_tensor("lnst", [128, 8, 16], F32))
        xnfm = [E(nc.sbuf_tensor(f"xnfm_{pp}", [128, NH, C], BF16)) for pp in range(2)]
        qkbf = E(nc.sbuf_tensor("qkbf", [128, NH, 2, C], BF16))
        qtb = [E(nc.sbuf_tensor(f"qtb_{pp}", [128, NH, C], BF16)) for pp in range(2)]
        ktb = E(nc.sbuf_tensor("ktb", [128, NH, C], BF16))
        kttm = [E(nc.sbuf_tensor(f"kttm_{pp}", [128, NH, C], BF16)) for pp in range(2)]
        vbf = [E(nc.sbuf_tensor(f"vbf_{pp}", [128, NH, C], BF16)) for pp in range(2)]
        stb = [E(nc.sbuf_tensor(f"stb_{pp}", [128, NH, C], BF16)) for pp in range(2)]
        Rst = E(nc.sbuf_tensor("Rst", [128, NH, HD], F32))
        Rbf = E(nc.sbuf_tensor("Rbf", [128, NH, HD], BF16))
        scr = E(nc.sbuf_tensor("scr", [128, 2, 3072], F32))
        hbf = E(nc.sbuf_tensor("hbf", [128, NH, C], BF16))
        x1 = [E(nc.sbuf_tensor(f"x1_{p}", [128, D], F32)) for p in range(2)]
        xn2fm = E(nc.sbuf_tensor("xn2fm", [128, NH, C], BF16))
        pbf = E(nc.sbuf_tensor("pbf", [128, 2, 128], BF16))
        Hbf = E(nc.sbuf_tensor("Hbf", [128, 16, 128], BF16))
        outb = E(nc.sbuf_tensor("out_s", [128, D], F32))
        st6g = E(nc.sbuf_tensor("st6g", [128, NH, 6], F32))
        gtmp = E(nc.sbuf_tensor("gtmp", [128, 5, NH], F32))
        rstd = E(nc.sbuf_tensor("rstd", [128, 8], F32))
        negmr = E(nc.sbuf_tensor("negmr", [128, 8], F32))
        rstdg = E(nc.sbuf_tensor("rstdg", [128, NH], F32))
        cst = E(nc.sbuf_tensor("cst", [128, 4], F32))
        negmrg = E(nc.sbuf_tensor("negmrg", [128, NH], F32))

        wtt = scr[:, 0, 0:64].bitcast(BF16)
        PS = E(nc.psum_tensor("PS", [128, 4096], F32))
        PSB = PS[:].bitcast(BF16)

        sems = {}
        for name in ("m_tensor", "m_vector", "m_scalar", "m_gpsimd", "d_sync", "d_scalar", "d_gpsimd"):
            sems[name] = E(nc.semaphore(name))
        sch = Sched(nc, sems)

        def tap(stage, I, ap_fn, bufname):
            if (stage, I) in dbg:
                ap = ap_fn()
                nm = f"dbg_{stage}_{I}"
                t = nc.dram_tensor(nm, list(ap.shape), ap.dtype, kind="ExternalOutput").ap()
                dbg_names.append(nm)
                names = [bufname] if isinstance(bufname, str) else list(bufname)
                sch.op("sync", lambda ap=ap, t=t: nc.sync.dma_start(out=t, in_=ap),
                       reads=names, dma=True, tag=f"tap_{stage}_{I}")

        for fn, wr, tg in (
            (lambda: nc.gpsimd.dma_start(out=wq[:], in_=wq_h.rearrange("h d e -> d h e")), ["wq"], "ld_wq"),
            (lambda: nc.gpsimd.dma_start(out=wk[:], in_=wk_h.rearrange("h d e -> d h e")), ["wk"], "ld_wk"),
            (lambda: nc.gpsimd.dma_start(out=wv[:], in_=wv_h.rearrange("h d e -> d h e")), ["wv"], "ld_wv"),
            (lambda: nc.gpsimd.dma_start(out=wg[:], in_=wg_h.rearrange("(b p) o -> p b o", p=128)), ["wg"], "ld_wg"),
            (lambda: nc.gpsimd.dma_start(out=wo[:], in_=wo_h.rearrange("(b p) o -> p b o", p=128)), ["wo"], "ld_wo"),
            (lambda: nc.gpsimd.dma_start(out=u1[:], in_=u1_h.rearrange("(b p) r -> p b r", p=128)), ["u1"], "ld_u1"),
            (lambda: nc.gpsimd.dma_start(out=v1[:], in_=v1_h.rearrange("(b p) f -> p b f", p=128)), ["v1"], "ld_v1"),
            (lambda: nc.gpsimd.dma_start(out=u2[:], in_=u2_h.rearrange("(b p) r -> p b r", p=128)), ["u2"], "ld_u2"),
            (lambda: nc.gpsimd.dma_start(out=v2[:], in_=v2_h.rearrange("(b p) o -> p b o", p=128)), ["v2"], "ld_v2"),
            (lambda: nc.gpsimd.dma_start(out=b1r[:], in_=b1_h.unsqueeze(0)), ["b1r"], "ld_b1"),
            (lambda: nc.gpsimd.dma_start(out=b2r[:], in_=b2_h.unsqueeze(0)), ["b2r"], "ld_b2"),
            (lambda: nc.gpsimd.dma_start(out=onesb[:], in_=ones_h), ["ones"], "ld_ones"),
        ):
            sch.op("gpsimd", fn, writes=wr, dma=True, tag=tg)

        for fn, wr, tg in (
            (lambda: nc.sync.dma_start(out=gnwbc[:], in_=gnw_h.unsqueeze(0).broadcast_to([128, D])), ["gnwbc"], "ld_gnw"),
            (lambda: nc.sync.dma_start(out=gnbbc[:], in_=gnb_h.unsqueeze(0).broadcast_to([128, D])), ["gnbbc"], "ld_gnb"),
            (lambda: nc.sync.dma_start(out=dloc[:], in_=dloc_h), ["dloc"], "ld_dloc"),
            (lambda: nc.sync.dma_start(out=gq[:], in_=gq_h), ["gq"], "ld_gq"),
            (lambda: nc.sync.dma_start(out=gk[:], in_=gk_h), ["gk"], "ld_gk"),
            (lambda: nc.sync.dma_start(out=idn[:], in_=idn_h), ["idn"], "ld_idn"),
            (lambda: nc.sync.dma_start(out=pt[:], in_=pt_h), ["pt"], "ld_pt"),
        ):
            sch.op("sync", fn, writes=wr, dma=True, tag=tg)

        for (src, dst, sname, dname) in ((wq, wqr, "wq", "wqr"), (wk, wkr, "wk", "wkr")):
            for h in range(NH):
                sch.op("tensor", lambda src=src, h=h: nc.tensor.transpose(PSB[:, 0:128], src[:, h, :], idn[:]),
                       reads=[sname, "idn"], pw=[(0, 64)], tag=f"T_{dname}{h}")
                sch.op("vector", lambda: nc.vector.tensor_copy(out=wtt, in_=PSB[:, 0:128]),
                       writes=["sA0"], pr=[(0, 64)], tag=f"ev_{dname}{h}")
                sch.op("tensor", lambda: nc.tensor.matmul(PS[:, 512:640], wtt, pt[:], start=True, stop=True),
                       reads=["sA0", "pt"], pw=[(512, 640)], tag=f"mm_{dname}{h}")
                sch.op("vector", lambda dst=dst, h=h: nc.vector.tensor_copy(out=dst[:, h, :], in_=PS[:, 512:640]),
                       writes=[dname], pr=[(512, 640)], tag=f"ev2_{dname}{h}")

        tap("wqr", 0, lambda: wqr[:], "wqr")
        tap("wkr", 0, lambda: wkr[:], "wkr")
        sch.op("vector", lambda: nc.vector.memset(Rst[:], 0.0),
               writes=[f"Rst{h}" for h in range(NH)], tag="memset_R")
        sch.op("vector", lambda: (nc.vector.memset(cst[:, 0:1], EPS),
                                  nc.vector.memset(cst[:, 1:2], -0.5),
                                  nc.vector.memset(cst[:, 2:3], 0.25),
                                  nc.vector.memset(cst[:, 3:4], 1.0))[-1],
               writes=["cst"], tag="memset_cst")

        def emit_loads(I):
            p = I % 2
            sch.op("sync", lambda I=I, p=p: nc.sync.dma_start(out=x0[p][:], in_=x_h[I * C:(I + 1) * C, :]),
                   writes=[f"x0{p}"], dma=True, tag=f"ld_x0_{I}", cost=3500)
            sch.op("sync", lambda I=I, p=p: nc.sync.dma_start(out=tabs[p][:], in_=tab_h[I]),
                   writes=[f"tab{p}"], dma=True, tag=f"ld_tab_{I}", cost=2800)

        emit_loads(0)

        def sacol(h):
            return 1024 + (h % 2) * 512 + (h // 2) * 128

        def ycol(h):
            return 2048 + (h % 2) * 512 + (h // 2) * 128

        def emit_ln_stats(I, p, srcfn, src_reads, src_prs, idx, tagp, extra_rd=()):
            s = idx + 4 * p
            for k in range(2):
                sch.op("vector", lambda k=k: nc.vector.bn_stats(out=lnst[:, s, 6 * k:6 * k + 6], in_=srcfn(k)),
                       reads=src_reads, writes=[f"lnst{s}{k}"], pr=src_prs[k] if src_prs else [],
                       tag=f"bn{k}{tagp}_{I}", cost=420)
            sch.op("vector", lambda: nc.vector.bn_aggr(out=lnst[:, s, 12:14], in_=lnst[:, s, 0:12]),
                   reads=[f"lnst{s}0", f"lnst{s}1"], writes=[f"lnagg{s}"], tag=f"agg{tagp}_{I}", cost=180)
            sch.op("scalar", lambda: nc.scalar.activation(out=lnst[:, s, 14:15], in_=lnst[:, s, 13:14],
                                                          func=AF.Ln, bias=cst[:, 0:1], scale=cst[:, 3:4]),
                   reads=[f"lnagg{s}", "cst"] + list(extra_rd), writes=[f"rstdL{s}"], tag=f"rstdA{tagp}_{I}", cost=380)
            sch.op("scalar", lambda: nc.scalar.activation(out=rstd[:, s:s + 1], in_=lnst[:, s, 14:15],
                                                          func=AF.Exp, scale=cst[:, 1:2]),
                   reads=[f"rstdL{s}", "cst"], writes=[f"rstd{s}"], tag=f"rstdB{tagp}_{I}", cost=380)
            sch.op("vector", lambda: nc.vector.tensor_scalar(out=negmr[:, s:s + 1], in0=lnst[:, s, 12:13],
                                                             scalar1=rstd[:, s:s + 1], scalar2=-1.0,
                                                             op0=ALU.mult, op1=ALU.mult),
                   reads=[f"lnagg{s}", f"rstd{s}"], writes=[f"negmr{s}"], tag=f"negmr{tagp}_{I}", cost=150)

        def emit_front(I):
            p = I % 2

            s1 = 0 + 4 * p
            emit_ln_stats(I, p, lambda k, p=p: x0[p][:, 512 * k:512 * k + 512], [f"x0{p}"], None, 0, "1")
            sch.op("vector", lambda p=p, s1=s1: nc.vector.tensor_scalar(out=xntm[:], in0=x0[p][:],
                                                                 scalar1=rstd[:, s1:s1 + 1], scalar2=negmr[:, s1:s1 + 1],
                                                                 op0=ALU.mult, op1=ALU.add),
                   reads=[f"x0{p}", f"rstd{s1}", f"negmr{s1}"], writes=["xntm"], tag=f"ln1ap_{I}", cost=700)
            tap("xntm", I, lambda: xntm[:], "xntm")

            sch.op("tensor", lambda: [nc.tensor.transpose(PSB[:, 6144 + h * 128:6272 + h * 128],
                                                          xntm[:, h * 128:(h + 1) * 128], idn[:])
                                      for h in range(NH)][-1],
                   reads=["xntm", "idn"], pw=[(3072 + h * 64, 3136 + h * 64) for h in range(NH)], tag=f"xnT_{I}", cost=620)
            sch.op("scalar", lambda p=p: nc.scalar.activation(out=xnfm[p][:], in_=PSB[:, 6144:7168],
                                                              func=AF.Copy),
                   writes=[f"xnfm{p}"], pr=[(3072, 3584)], tag=f"evxn_{I}", cost=1100)
            tap("xnfm", I, lambda p=p: xnfm[p][:], f"xnfm{p}")

            def qkmm(base, wa, wb, h, pp):
                c0 = base + h * 256
                nc.tensor.matmul(PS[:, c0:c0 + 128], wa[:, h, :], xnfm[pp][:, h, :],
                                 start=(h % 2 == 0), stop=False)
                return nc.tensor.matmul(PS[:, c0 + 128:c0 + 256], wb[:, h, :], xnfm[pp][:, h, :],
                                        start=False, stop=(h % 2 == 1))
            sch.op("tensor", lambda p=p, qkmm=qkmm: [qkmm(0, wq, wk, h, p) for h in range(NH)][-1],
                   reads=[f"xnfm{p}", "wq", "wk"], pw=[(0, 2048)], tag=f"qk_{I}", cost=1950)
            sch.op("tensor", lambda p=p, qkmm=qkmm: [qkmm(2048, wqr, wkr, h, p) for h in range(NH)][-1],
                   reads=[f"xnfm{p}", "wqr", "wkr"], pw=[(2048, 4096)], tag=f"qkr_{I}", cost=1950)

            for g in range(2):
                qv = PS[:, g * 1024:(g + 1) * 1024].rearrange("p (h s t) -> p h s t", h=4, s=2)
                rv = PS[:, 2048 + g * 1024:3072 + g * 1024].rearrange("p (h s t) -> p h s t", h=4, s=2)
                cosb = tabs[p][:, 0, :, :].unsqueeze(1).broadcast_to([128, 4, 2, C])
                sinb = tabs[p][:, 1, :, :].unsqueeze(1).broadcast_to([128, 4, 2, C])
                t1v = scr[:, p, 0:1024].rearrange("p (h s t) -> p h s t", h=4, s=2)
                t2v = scr[:, p, 1024:2048].rearrange("p (h s t) -> p h s t", h=4, s=2)
                sch.op("vector", lambda qv=qv, cosb=cosb, t1v=t1v:
                       nc.vector.tensor_tensor(out=t1v, in0=qv, in1=cosb, op=ALU.mult),
                       reads=[f"tab{p}"], writes=[f"sA{p}"], pr=[(g * 1024, (g + 1) * 1024)], tag=f"t1_{I}_{g}", cost=1300)
                sch.op("vector", lambda rv=rv, sinb=sinb, t2v=t2v:
                       nc.vector.tensor_tensor(out=t2v, in0=rv, in1=sinb, op=ALU.mult),
                       reads=[f"tab{p}"], writes=[f"sB{p}"], pr=[(2048 + g * 1024, 3072 + g * 1024)], tag=f"t2_{I}_{g}", cost=1300)
                sch.op("vector", lambda g=g, p=p: nc.vector.tensor_tensor(
                           out=qkbf[:, 4 * g:4 * g + 4, :, :],
                           in0=scr[:, p, 0:1024], in1=scr[:, p, 1024:2048], op=ALU.add),
                       reads=[f"sA{p}", f"sB{p}"], writes=["qkbf"], tag=f"qkadd_{I}_{g}", cost=1300)
            tap("qkbf", I, lambda: qkbf[:], "qkbf")

            sch.op("gpsimd", lambda p=p: nc.gpsimd.tensor_tensor(out=qtb[p][:], in0=qkbf[:, :, 0, :],
                                                                 in1=gq[:], op=ALU.mult),
                   reads=["qkbf", "gq"], writes=[f"qtb{p}"], tag=f"qt_{I}", cost=2300)
            sch.op("vector", lambda: nc.vector.tensor_tensor(out=ktb[:], in0=qkbf[:, :, 1, :],
                                                             in1=gk[:], op=ALU.mult),
                   reads=["qkbf", "gk"], writes=["ktb"], tag=f"kt_{I}", cost=600)

            sch.op("tensor", lambda p=p: [nc.tensor.matmul(PS[:, h * 128:(h + 1) * 128], xnfm[p][:, h, :],
                                                           wv[:, h, :], start=True, stop=True)
                                          for h in range(NH)][-1],
                   reads=[f"xnfm{p}", "wv"], pw=[(0, 1024)], tag=f"v_{I}", cost=1000)
            sch.op("vector", lambda p=p: nc.vector.tensor_copy(out=vbf[p][:], in_=PS[:, 0:1024]),
                   writes=[f"vbf{p}"], pr=[(0, 1024)], tag=f"evv_{I}", cost=700)
            tap("vbf", I, lambda p=p: vbf[p][:], f"vbf{p}")

            sch.op("tensor", lambda: [nc.tensor.transpose(PSB[:, 6144 + h * 128:6272 + h * 128],
                                                          ktb[:, h, :], idn[:]) for h in range(NH)][-1],
                   reads=["ktb", "idn"], pw=[(3072 + h * 64, 3136 + h * 64) for h in range(NH)], tag=f"ktT_{I}", cost=620)
            sch.op("vector", lambda p=p: nc.vector.tensor_copy(out=kttm[p][:], in_=PSB[:, 6144:7168]),
                   writes=[f"kttm{p}"], pr=[(3072, 3584)], tag=f"evkt_{I}", cost=700)

            sch.op("tensor", lambda: [nc.tensor.matmul(PS[:, sacol(h):sacol(h) + 128],
                                                       qkbf[:, h, 1, :], qkbf[:, h, 0, :],
                                                       start=True, stop=True) for h in range(NH)][-1],
                   reads=["qkbf"], pw=[(sacol(h), sacol(h) + 128) for h in range(NH)], tag=f"s_{I}", cost=1000)
            for h in range(NH):
                sch.op("vector", lambda h=h, p=p: nc.vector.tensor_tensor(
                           out=stb[p][:, h, :], in0=PS[:, sacol(h):sacol(h) + 128], in1=dloc[:, h, :], op=ALU.mult),
                       reads=["dloc"], writes=[f"stb{p}h{h}"], pr=[(sacol(h), sacol(h) + 128)], tag=f"st_{I}_{h}", cost=200)
            tap("stb", I, lambda p=p: stb[p][:], [f"stb{p}h{h}" for h in range(NH)])

        def emit_midtail(I):
            p = I % 2

            for q in range(4):
                h0, h1 = 2 * q, 2 * q + 1
                for h in (h0, h1):
                    sch.op("tensor", lambda h=h, p=p, I=I: nc.tensor.matmul(
                               PS[:, ycol(h):ycol(h) + 128], stb[p][:, h, :], vbf[p][:, h, :],
                               start=True, stop=(I == 0)),
                           reads=[f"stb{p}h{h}", f"vbf{p}"], pw=[(ycol(h), ycol(h) + 128)], tag=f"yi_{I}_{h}", cost=150, grp=("oc" if I == 0 else "o"))
                for h in (h0, h1):
                    sch.op("tensor", lambda h=h, p=p: nc.tensor.matmul(
                               PS[:, sacol(h):sacol(h) + 128], kttm[p][:, h, :], vbf[p][:, h, :],
                               start=True, stop=True),
                           reads=[f"kttm{p}", f"vbf{p}"], pw=[(sacol(h), sacol(h) + 128)], tag=f"a_{I}_{h}", cost=150)
                sch.op("tensor", lambda q=q, p=p: (
                           nc.tensor.matmul(PS[:, 3072:3584], xnfm[p][:, 2 * q, :],
                                            wg[:, 2 * q, 0:512], start=(q == 0), stop=False),
                           nc.tensor.matmul(PS[:, 3584:4096], xnfm[p][:, 2 * q, :],
                                            wg[:, 2 * q, 512:1024], start=(q == 0), stop=False))[-1],
                       reads=[f"xnfm{p}", "wg"], pw=[(3072, 4096)], tag=f"g_{I}_{2 * q}", cost=950, grp=("o" if q == 0 else "-"))
                if I > 0:
                    for h in (h0, h1):
                        sch.op("tensor", lambda h=h, p=p: nc.tensor.matmul(
                                   PS[:, ycol(h):ycol(h) + 128], qtb[p][:, h, :], Rbf[:, h, :],
                                   start=False, stop=True),
                               reads=[f"qtb{p}", "Rbf"], pw=[(ycol(h), ycol(h) + 128)], tag=f"yc_{I}_{h}", cost=150, grp="c")
                sch.op("tensor", lambda q=q, p=p: (
                           nc.tensor.matmul(PS[:, 3072:3584], xnfm[p][:, 2 * q + 1, :],
                                            wg[:, 2 * q + 1, 0:512], start=False, stop=(q == 3)),
                           nc.tensor.matmul(PS[:, 3584:4096], xnfm[p][:, 2 * q + 1, :],
                                            wg[:, 2 * q + 1, 512:1024], start=False, stop=(q == 3)))[-1],
                       reads=[f"xnfm{p}", "wg"], pw=[(3072, 4096)], tag=f"g_{I}_{2 * q + 1}", cost=950, grp=("c" if q == 3 else "-"))

            sch.op("scalar", lambda p=p: nc.scalar.activation(out=scr[:, p, 1024:2048], in_=PS[:, 3072:4096],
                                                              func=AF.Silu),
                   writes=[f"sB{p}"], pr=[(3072, 4096)], tag=f"silu_{I}", cost=1100)
            sch.op("vector", lambda p=p: nc.vector.tensor_tensor(out=scr[:, p, 2048:3072], in0=scr[:, p, 1024:2048],
                                                                 in1=gnwbc[:], op=ALU.mult),
                   reads=[f"sB{p}", "gnwbc"], writes=[f"sC{p}"], tag=f"ga1_{I}", cost=600)
            sch.op("gpsimd", lambda p=p: nc.gpsimd.tensor_tensor(out=outb[:], in0=scr[:, p, 1024:2048],
                                                                 in1=gnbbc[:], op=ALU.mult),
                   reads=[f"sB{p}", "gnbbc"], writes=["out"], tag=f"ga2_{I}", cost=2300)

            for h in range(NH):
                sch.op("vector", lambda h=h: nc.vector.scalar_tensor_tensor(
                           out=Rst[:, h, :], in0=Rst[:, h, :], scalar=gc_list[h],
                           in1=PS[:, sacol(h):sacol(h) + 128], op0=ALU.mult, op1=ALU.add),
                       reads=[f"Rst{h}"], writes=[f"Rst{h}"], pr=[(sacol(h), sacol(h) + 128)], tag=f"rup_{I}_{h}", cost=180)
            sch.op("vector", lambda: nc.vector.tensor_copy(out=Rbf[:], in_=Rst[:]),
                   reads=[f"Rst{h}" for h in range(NH)], writes=["Rbf"], tag=f"rbf_{I}", cost=700)

            for h in range(NH):
                sch.op("vector", lambda h=h: nc.vector.bn_stats(out=st6g[:, h, :], in_=PS[:, ycol(h):ycol(h) + 128]),
                       writes=[f"st6g{h}", "st6g7"] if h == NH - 1 else [f"st6g{h}"],
                       pr=[(ycol(h), ycol(h) + 128)], tag=f"bng_{I}_{h}", cost=220)
            allst = [f"st6g{h}" for h in range(NH)]
            sch.op("vector", lambda: nc.vector.tensor_tensor(out=gtmp[:, 0, :], in0=st6g[:, :, 1],
                                                             in1=st6g[:, :, 4], op=ALU.add),
                   reads=allst, writes=["gtA"], tag=f"gnc1_{I}")
            sch.op("vector", lambda: nc.vector.tensor_tensor(out=gtmp[:, 1, :], in0=st6g[:, :, 1],
                                                             in1=st6g[:, :, 4], op=ALU.subtract),
                   reads=allst, writes=["gtD"], tag=f"gnc2_{I}")
            sch.op("vector", lambda: nc.vector.tensor_tensor(out=gtmp[:, 2, :], in0=gtmp[:, 1, :],
                                                             in1=gtmp[:, 1, :], op=ALU.mult),
                   reads=["gtD"], writes=["gtDD"], tag=f"gnc3_{I}")
            sch.op("vector", lambda: nc.vector.tensor_tensor(out=gtmp[:, 3, :], in0=st6g[:, :, 2],
                                                             in1=st6g[:, :, 5], op=ALU.add),
                   reads=allst, writes=["gtS"], tag=f"gnc4_{I}")
            sch.op("vector", lambda: nc.vector.scalar_tensor_tensor(out=gtmp[:, 4, :], in0=gtmp[:, 3, :],
                                                                    scalar=4.0 / HD, in1=gtmp[:, 2, :],
                                                                    op0=ALU.mult, op1=ALU.add),
                   reads=["gtS", "gtDD"], writes=["gtV"], tag=f"gnc5_{I}")
            sch.op("scalar", lambda: nc.scalar.activation(out=rstdg[:], in_=gtmp[:, 4, :], func=AF.Ln,
                                                          bias=cst[:, 0:1], scale=cst[:, 2:3]),
                   reads=["gtV", "cst"], writes=["rstdg"], tag=f"rstdgA_{I}")
            sch.op("scalar", lambda: nc.scalar.activation(out=rstdg[:], in_=rstdg[:], func=AF.Exp, scale=cst[:, 1:2]),
                   reads=["cst", "rstdg"], writes=["rstdg"], tag=f"rstdgB_{I}")
            sch.op("vector", lambda: nc.vector.scalar_tensor_tensor(out=negmrg[:], in0=gtmp[:, 0, :], scalar=-0.5,
                                                                    in1=rstdg[:], op0=ALU.mult, op1=ALU.mult),
                   reads=["gtA", "rstdg"], writes=["negmrg"], tag=f"negmrg_{I}")

            for h in range(NH):
                sch.op("vector", lambda h=h, p=p: nc.vector.tensor_scalar(
                           out=scr[:, p, h * 128:(h + 1) * 128], in0=PS[:, ycol(h):ycol(h) + 128],
                           scalar1=rstdg[:, h:h + 1], scalar2=negmrg[:, h:h + 1], op0=ALU.mult, op1=ALU.add),
                       reads=["rstdg", "negmrg"] + ([f"sA{p}"] if h == 0 else []),
                       writes=[f"sA{p}" if h == 0 else f"sA{p}q{h}"],
                       pr=[(ycol(h), ycol(h) + 128)], tag=f"gnap_{I}_{h}", cost=180)
            sch.op("vector", lambda p=p: nc.vector.tensor_tensor(out=scr[:, p, 1024:2048], in0=scr[:, p, 0:1024],
                                                                 in1=scr[:, p, 2048:3072], op=ALU.mult),
                   reads=[f"sA{p}", f"sC{p}"] + [f"sA{p}q{h}" for h in range(1, NH)],
                   writes=[f"sB{p}"], tag=f"gt1_{I}", cost=600)
            sch.op("vector", lambda p=p: nc.vector.tensor_tensor(out=hbf[:], in0=scr[:, p, 1024:2048],
                                                                 in1=outb[:], op=ALU.add),
                   reads=[f"sB{p}", "out"], writes=["hbf"], tag=f"gate_{I}", cost=600)
            tap("hbf", I, lambda: hbf[:], "hbf")

            sch.op("tensor", lambda: [nc.tensor.transpose(PSB[:, 2048 + h * 128:2176 + h * 128],
                                                          hbf[:, h, :], idn[:]) for h in range(NH)][-1],
                   reads=["hbf", "idn"], pw=[(1024 + h * 64, 1088 + h * 64) for h in range(NH)], tag=f"hT_{I}", cost=620)
            sch.op("vector", lambda p=p: nc.vector.tensor_copy(out=kttm[p][:], in_=PSB[:, 2048:3072]),
                   writes=[f"kttm{p}", "hfm"], pr=[(1024, 1536)], tag=f"evh_{I}", cost=700)
            for hb in range(NH):
                sch.op("tensor", lambda hb=hb, p=p: (nc.tensor.matmul(PS[:, 0:512], kttm[p][:, hb, :], wo[:, hb, 0:512],
                                                                 start=(hb == 0), stop=(hb == NH - 1)),
                                                nc.tensor.matmul(PS[:, 512:1024], kttm[p][:, hb, :], wo[:, hb, 512:1024],
                                                                 start=(hb == 0), stop=(hb == NH - 1)))[-1],
                       reads=["hfm", "wo"], pw=[(0, 1024)], tag=f"wo_{I}_{hb}", cost=950, grp=("o" if hb == 0 else "c" if hb == NH - 1 else "-"))

            s2 = 1 + 4 * p
            emit_ln_stats(I, p, lambda k: PS[:, 512 * k:512 * k + 512], [],
                          [[(512 * k, 512 * k + 512)] for k in range(2)], 1, "2")
            sch.op("vector", lambda p=p, s2=s2: nc.vector.tensor_scalar(out=scr[:, p, 2048:3072], in0=PS[:, 0:1024],
                                                                 scalar1=rstd[:, s2:s2 + 1], scalar2=negmr[:, s2:s2 + 1],
                                                                 op0=ALU.mult, op1=ALU.add),
                   reads=[f"rstd{s2}", f"negmr{s2}", f"sC{p}"], writes=[f"sC{p}"], pr=[(0, 1024)], tag=f"ln2ap_{I}", cost=1300)
            sch.op("vector", lambda p=p: nc.vector.tensor_tensor(out=x1[p][:], in0=scr[:, p, 2048:3072],
                                                                 in1=x0[p][:], op=ALU.add),
                   reads=[f"sC{p}", f"x0{p}"], writes=[f"x1{p}"], tag=f"resid1_{I}", cost=600)
            tap("x1", I, lambda p=p: x1[p][:], f"x1{p}")

            s3 = 2 + 4 * p
            emit_ln_stats(I, p, lambda k, p=p: x1[p][:, 512 * k:512 * k + 512], [f"x1{p}"], None, 2, "3")
            sch.op("vector", lambda p=p, s3=s3: nc.vector.tensor_scalar(out=xn2tm[:], in0=x1[p][:],
                                                                 scalar1=rstd[:, s3:s3 + 1], scalar2=negmr[:, s3:s3 + 1],
                                                                 op0=ALU.mult, op1=ALU.add),
                   reads=[f"x1{p}", f"rstd{s3}", f"negmr{s3}"], writes=["xn2tm"], tag=f"ln3ap_{I}", cost=700)
            sch.op("tensor", lambda: [nc.tensor.transpose(PSB[:, 3072 + h * 128:3200 + h * 128],
                                                          xn2tm[:, h * 128:(h + 1) * 128], idn[:])
                                      for h in range(NH)][-1],
                   reads=["xn2tm", "idn"], pw=[(1536 + h * 64, 1600 + h * 64) for h in range(NH)], tag=f"xn2T_{I}", cost=620)
            sch.op("scalar", lambda: nc.scalar.activation(out=xn2fm[:], in_=PSB[:, 3072:4096],
                                                          func=AF.Copy),
                   writes=["xn2fm"], pr=[(1536, 2048)], tag=f"evxn2_{I}", cost=1100)
            tap("xn2fm", I, lambda: xn2fm[:], "xn2fm")

            for k in range(8):
                base = 1024 + (k % 2) * 512
                sch.op("tensor", lambda k=k, base=base: (
                           nc.tensor.matmul(PS[:, base:base + 128], u1[:, k, 0:128], xn2fm[:, k, :],
                                            start=(k < 2), stop=False),
                           nc.tensor.matmul(PS[:, base + 128:base + 256], u1[:, k, 128:256], xn2fm[:, k, :],
                                            start=False, stop=(k >= 6)))[-1],
                       reads=["xn2fm", "u1"], pw=[(base, base + 256)], tag=f"p_{I}_{k}", cost=350, grp=("o" if k < 2 else "c" if k >= 6 else "-"))
            sch.op("vector", lambda p=p: nc.vector.tensor_copy(out=scr[:, p, 2048:2304], in_=PS[:, 1024:1280]),
                   writes=[f"sC{p}"], pr=[(1024, 1280)], tag=f"pev1_{I}", cost=450)
            sch.op("vector", lambda p=p: nc.vector.tensor_tensor(out=pbf[:], in0=scr[:, p, 2048:2304],
                                                             in1=PS[:, 1536:1792], op=ALU.add),
                   reads=[f"sC{p}"], writes=["pbf"], pr=[(1536, 1792)], tag=f"pev2_{I}", cost=450)
            tap("pbf", I, lambda: pbf[:], "pbf")

            qf_cols = (0, 512, 1024, 1536)
            for half in range(2):
                fbs = list(range(half * 16, half * 16 + 16))
                for rc in range(2):
                    sch.op("tensor", lambda rc=rc, fbs=fbs: [
                               nc.tensor.matmul(PS[:, 2048 + (fb % 16) * 128:2176 + (fb % 16) * 128],
                                                v1[:, rc, fb * 128:(fb + 1) * 128], pbf[:, rc, :],
                                                start=(rc == 0 and fb % 4 == 0),
                                                stop=False) for fb in fbs][-1],
                           reads=["pbf", "v1"], pw=[(2048, 4096)], tag=f"h_{I}_{half}_{rc}", cost=2900, grp=("o" if rc == 0 else "-"))
                sch.op("tensor", lambda fbs=fbs: [
                           nc.tensor.matmul(PS[:, 2048 + (fb % 16) * 128:2176 + (fb % 16) * 128],
                                            b1r[0:1, fb * 128:(fb + 1) * 128], onesb[:],
                                            start=False, stop=(fb % 4 == 3)) for fb in fbs][-1],
                       reads=["b1r", "ones"], pw=[(2048, 4096)], tag=f"hb1_{I}_{half}", cost=900, grp="c")
                sch.op("scalar", lambda: nc.scalar.activation(out=Hbf[:], in_=PS[:, 2048:4096],
                                                              func=AF.Gelu_apprx_tanh),
                       writes=["Hbf"], pr=[(2048, 4096)], tag=f"gelu_{I}_{half}", cost=1900)
                for fb in fbs:
                    c0 = qf_cols[fb % 4]
                    sch.op("tensor", lambda fb=fb, c0=c0: (
                               nc.tensor.matmul(PS[:, c0:c0 + 128], u2[:, fb, 0:128], Hbf[:, fb % 16, :],
                                                start=(fb < 4), stop=False),
                               nc.tensor.matmul(PS[:, c0 + 128:c0 + 256], u2[:, fb, 128:256], Hbf[:, fb % 16, :],
                                                start=False, stop=(fb >= 28)))[-1],
                           reads=["Hbf", "u2"], pw=[(c0, c0 + 256)], tag=f"u2_{I}_{fb}", cost=400, grp=("o" if fb < 4 else "c" if fb >= 28 else "-"))

            sch.op("vector", lambda p=p: nc.vector.tensor_copy(out=scr[:, p, 2048:2304], in_=PS[:, 0:256]),
                   writes=[f"sC{p}"], pr=[(0, 256)], tag=f"qfc1_{I}", cost=450)
            sch.op("vector", lambda p=p: nc.vector.tensor_tensor(out=scr[:, p, 2048:2304], in0=scr[:, p, 2048:2304], in1=PS[:, 512:768], op=ALU.add),
                   reads=[f"sC{p}"], writes=[f"sC{p}"], pr=[(512, 768)], tag=f"qfc2_{I}", cost=450)
            sch.op("vector", lambda p=p: nc.vector.tensor_tensor(out=scr[:, p, 2048:2304], in0=scr[:, p, 2048:2304], in1=PS[:, 1024:1280], op=ALU.add),
                   reads=[f"sC{p}"], writes=[f"sC{p}"], pr=[(1024, 1280)], tag=f"qfc3_{I}", cost=450)
            sch.op("vector", lambda p=p: nc.vector.tensor_tensor(out=pbf[:], in0=scr[:, p, 2048:2304], in1=PS[:, 1536:1792], op=ALU.add),
                   reads=[f"sC{p}"], writes=["pbf"], pr=[(1536, 1792)], tag=f"qfc4_{I}", cost=450)
            tap("qfbf", I, lambda: pbf[:], "pbf")

            sch.op("tensor", lambda: (nc.tensor.matmul(PS[:, 3072:3584], pbf[:, 0, :], v2[:, 0, 0:512],
                                                       start=True, stop=False),
                                      nc.tensor.matmul(PS[:, 3584:4096], pbf[:, 0, :], v2[:, 0, 512:1024],
                                                       start=True, stop=False))[-1],
                   reads=["pbf", "v2"], pw=[(3072, 4096)], tag=f"v2a_{I}", cost=950, grp="o")
            sch.op("tensor", lambda: (nc.tensor.matmul(PS[:, 3072:3584], pbf[:, 1, :], v2[:, 1, 0:512],
                                                       start=False, stop=False),
                                      nc.tensor.matmul(PS[:, 3584:4096], pbf[:, 1, :], v2[:, 1, 512:1024],
                                                       start=False, stop=False))[-1],
                   reads=["pbf", "v2"], pw=[(3072, 4096)], tag=f"v2b_{I}", cost=950, grp="-")
            sch.op("tensor", lambda: (nc.tensor.matmul(PS[:, 3072:3584], onesb[:], b2r[0:1, 0:512],
                                                       start=False, stop=True),
                                      nc.tensor.matmul(PS[:, 3584:4096], onesb[:], b2r[0:1, 512:1024],
                                                       start=False, stop=True))[-1],
                   reads=["ones", "b2r"], pw=[(3072, 4096)], tag=f"b2_{I}", cost=500, grp="c")
            s4 = 3 + 4 * p
            emit_ln_stats(I, p, lambda k: PS[:, 3072 + 512 * k:3072 + 512 * k + 512], [],
                          [[(3072 + 512 * k, 3072 + 512 * k + 512)] for k in range(2)], 3, "4")
            sch.op("vector", lambda p=p, s4=s4: nc.vector.tensor_scalar(out=scr[:, p, 2048:3072], in0=PS[:, 3072:4096],
                                                                 scalar1=rstd[:, s4:s4 + 1], scalar2=negmr[:, s4:s4 + 1],
                                                                 op0=ALU.mult, op1=ALU.add),
                   reads=[f"rstd{s4}", f"negmr{s4}", f"sC{p}"], writes=[f"sC{p}"], pr=[(3072, 4096)], tag=f"ln4ap_{I}", cost=1300)
            sch.op("gpsimd", lambda p=p: nc.gpsimd.tensor_tensor(out=outb[:], in0=scr[:, p, 2048:3072],
                                                                 in1=x1[p][:], op=ALU.add),
                   reads=[f"sC{p}", f"x1{p}"], writes=["out"], tag=f"resid2_{I}", cost=2300)
            sch.op("scalar", lambda I=I: nc.scalar.dma_start(out=y_h[I * C:(I + 1) * C, :], in_=outb[:]),
                   reads=["out"], dma=True, tag=f"store_{I}", cost=3500)

        for I in range(NCH):
            emit_front(I)
            if I > 0:
                emit_midtail(I - 1)
            if I + 1 < NCH:
                emit_loads(I + 1)
        emit_midtail(NCH - 1)

        block = E(nc.Block())
        sch.run(block)

    return nc, dbg_names


def kernel(L0_attn_X, L0_attn_W_Q, L0_attn_W_K, L0_attn_W_V, L0_attn_W_G, L0_attn_W_O,
           L0_attn_gn_weight, L0_attn_gn_bias, L0_ffn_U1, L0_ffn_V1, L0_ffn_b1,
           L0_ffn_U2, L0_ffn_V2, L0_ffn_b2):
    consts, gc_list = host_constants()
    nc, _ = build_nc(gc_list)
    common = dict(
        wq=np.ascontiguousarray(L0_attn_W_Q, np.float32),
        wk=np.ascontiguousarray(L0_attn_W_K, np.float32),
        wv=np.ascontiguousarray(L0_attn_W_V, np.float32),
        wg=np.ascontiguousarray(L0_attn_W_G, np.float32),
        wo=np.ascontiguousarray(L0_attn_W_O, np.float32),
        gnw=np.ascontiguousarray(L0_attn_gn_weight, np.float32),
        gnb=np.ascontiguousarray(L0_attn_gn_bias, np.float32),
        u1=np.ascontiguousarray(L0_ffn_U1, np.float32),
        v1=np.ascontiguousarray(L0_ffn_V1, np.float32),
        b1=np.ascontiguousarray(L0_ffn_b1, np.float32),
        u2=np.ascontiguousarray(L0_ffn_U2, np.float32),
        v2=np.ascontiguousarray(L0_ffn_V2, np.float32),
        b2=np.ascontiguousarray(L0_ffn_b2, np.float32),
        **consts,
    )
    X = np.ascontiguousarray(L0_attn_X, np.float32)
    in_maps = [dict(common, x=np.ascontiguousarray(X[c])) for c in range(8)]
    res = run_bass_kernel_spmd(nc, in_maps, core_ids=list(range(8)))
    return np.stack([np.asarray(r["y"], np.float32) for r in res.results], axis=0)
